# version 18
# baseline (speedup 1.0000x reference)
import os
import numpy as np
import ml_dtypes
import concourse.bass as bass
import concourse.mybir as mybir
from concourse.bass_utils import run_bass_kernel_spmd

F32 = mybir.dt.float32
BF16 = mybir.dt.bfloat16
AF = mybir.ActivationFunctionType
ALU = mybir.AluOpType
AX = mybir.AxisListType.X

NCORES = 8
NT = 1024
D = 2048
H = 16
HD = 128
HID = 5632
NHS = 44
NDC = 16
NCH = 8
GCNT = float(HID * D)
EPS_LN = 1e-5
EPS_RMS = 1e-5
EPS_Q = 1e-8


class Buf:
    __slots__ = ("name", "last_writer", "readers")

    def __init__(self, name):
        self.name = name
        self.last_writer = None
        self.readers = []


class Prog:
    def __init__(self):
        self.ops = []

    def add(self, eng, fn, reads=(), writes=(), dma=False, cc=False, tag=""):
        ch = "cc" if cc else (eng + "_dma" if dma else eng)
        raw = set()
        war = set()
        for b in reads:
            if b.last_writer is not None:
                raw.add(b.last_writer)
        for b in writes:
            if b.last_writer is not None:
                war.add(b.last_writer)
            war.update(b.readers)
        if not hasattr(self, "chpos"):
            self.chpos = {}
        pos = self.chpos.get(ch, 0)
        self.chpos[ch] = pos + 1
        op = {"id": len(self.ops), "eng": eng, "ch": ch, "fn": fn, "dma": dma,
              "cc": cc, "raw": raw, "war": war, "tag": tag, "inc": False,
              "val": 0, "pos": pos}
        self.ops.append(op)
        oid = op["id"]
        for b in reads:
            b.readers.append(oid)
        for b in writes:
            b.last_writer = oid
            b.readers = []
        return oid

    def _needs_edge(self, op, dop, is_raw):
        if dop["ch"] != op["ch"] or dop["dma"] or dop["cc"]:
            return True
        if dop["ch"] == "tensor":
            return False
        return is_raw and (op["pos"] - dop["pos"] < int(os.environ.get("RAWD", "6")))

    def build(self, nc):
        ops = self.ops
        if os.environ.get("DMASER", "0") == "1":
            last_dma = {}
            for op in ops:
                if op["dma"]:
                    if op["ch"] in last_dma:
                        op["raw"].add(last_dma[op["ch"]])
                    last_dma[op["ch"]] = op["id"]
        NSLOT = 8
        for op in ops:
            if op["dma"]:
                op["inc"] = True
            for d in op["raw"]:
                if self._needs_edge(op, ops[d], True):
                    ops[d]["inc"] = True
            for d in op["war"]:
                if self._needs_edge(op, ops[d], False):
                    ops[d]["inc"] = True
        counters = {}
        slot_use = {}
        for op in ops:
            if op["dma"]:
                ch = op["ch"]
                idx = slot_use.setdefault(ch, 0)
                slot_use[ch] = idx + 1
                slot = idx % NSLOT
                op["sem"] = f"{ch}{slot}"
                op["val"] = 16 * (idx // NSLOT + 1)
                op["prewait"] = (op["sem"], op["val"] - 16) if op["val"] > 16 else None
            elif op["inc"]:
                ch = op["ch"]
                counters[ch] = counters.get(ch, 0) + 1
                op["sem"] = ch
                op["val"] = counters[ch]
        for op in ops:
            waits = {}
            if op.get("prewait"):
                sm, v = op["prewait"]
                if v > 0:
                    waits[sm] = max(waits.get(sm, 0), v)
            for d, isr in [(d, True) for d in op["raw"]] + [(d, False) for d in op["war"]]:
                dop = ops[d]
                if not self._needs_edge(op, dop, isr):
                    continue
                waits[dop["sem"]] = max(waits.get(dop["sem"], 0), dop["val"])
            op["waits"] = waits
        sem_names = set()
        for op in ops:
            if op["inc"]:
                sem_names.add(op["sem"])
            sem_names.update(op["waits"].keys())
        self.sems = {n: nc.alloc_semaphore("sem_" + n) for n in sorted(sem_names)}

    def emit(self, block):
        sems = self.sems

        def body_for(engine_name):
            def body(eng):
                hw = {}
                for op in self.ops:
                    if op["eng"] != engine_name:
                        continue
                    for ch, v in sorted(op["waits"].items()):
                        if hw.get(ch, 0) < v:
                            eng.wait_ge(sems[ch], v)
                            hw[ch] = v
                    inst = op["fn"](eng)
                    if op["inc"]:
                        assert inst is not None, op["tag"]
                        inst.then_inc(sems[op["sem"]], 16 if op["dma"] else 1)
            return body

        block.tensor(body_for("tensor"))
        block.vector(body_for("vector"))
        block.scalar(body_for("scalar"))
        block.gpsimd(body_for("gpsimd"))
        block.sync(body_for("sync"))


def build_nc():
    nc = bass.Bass("TRN2", target_bir_lowering=False,
                   detect_race_conditions=False)
    P = Prog()

    x_in = nc.dram_tensor("x_in", [NDC, 128, NT], F32, kind="ExternalInput").ap()
    ld_in = nc.dram_tensor("ld_in", [1, H], F32, kind="ExternalInput").ap()
    wq_in = nc.dram_tensor("wq_in", [D, D], F32, kind="ExternalInput").ap()
    wk_in = nc.dram_tensor("wk_in", [D, D], F32, kind="ExternalInput").ap()
    wv_in = nc.dram_tensor("wv_in", [D, D], F32, kind="ExternalInput").ap()
    wo_in = nc.dram_tensor("wo_in", [D, D], F32, kind="ExternalInput").ap()
    wg1_in = nc.dram_tensor("wg1_in", [D, HD], F32, kind="ExternalInput").ap()
    wg2_in = nc.dram_tensor("wg2_in", [HD, D], F32, kind="ExternalInput").ap()
    nw_in = nc.dram_tensor("nw_in", [128, NDC], F32, kind="ExternalInput").ap()
    w1_in = nc.dram_tensor("w1_in", [HID, D], F32, kind="ExternalInput").ap()
    w2_in = nc.dram_tensor("w2_in", [HID, D], F32, kind="ExternalInput").ap()
    w3_in = nc.dram_tensor("w3_in", [D, HID], F32, kind="ExternalInput").ap()
    w1g_in = nc.dram_tensor("w1g_in", [HID // 8, D], F32, kind="ExternalInput").ap()
    w2g_in = nc.dram_tensor("w2g_in", [HID // 8, D], F32, kind="ExternalInput").ap()
    w3g_in = nc.dram_tensor("w3g_in", [D, HID // 8], F32, kind="ExternalInput").ap()
    cid_in = nc.dram_tensor("cid_in", [1, 1], F32, kind="ExternalInput").ap()
    x_out = nc.dram_tensor("x_out", [NDC, 128, NT], F32, kind="ExternalOutput").ap()
    KDBG = os.environ.get("KDBG", "") == "1"
    if KDBG:
        dbg_xa = nc.dram_tensor("dbg_xa", [NDC, 128, NT], F32, kind="ExternalOutput").ap()
        dbg_xb = nc.dram_tensor("dbg_xb", [NDC, 128, NT], F32, kind="ExternalOutput").ap()
        dbg_osp = nc.dram_tensor("dbg_osp", [NCH, 128, D], F32, kind="ExternalOutput").ap()
        dbg_qsp = nc.dram_tensor("dbg_qsp", [NCH, H, 128, 128], F32, kind="ExternalOutput").ap()
        dbg_B = nc.dram_tensor("dbg_B", [NCORES, 128, D], F32, kind="ExternalOutput").ap()
        dbg_thr = nc.dram_tensor("dbg_thr", [128, 8], F32, kind="ExternalOutput").ap()
        dbg_xs = nc.dram_tensor("dbg_xs", [2, 128, NDC * NT], BF16, kind="ExternalOutput").ap()
        dbg_gacc = nc.dram_tensor("dbg_gacc", [128, 48], F32, kind="ExternalOutput").ap()
        dbg_Sin = nc.dram_tensor("dbg_Sin", [128, D], F32, kind="ExternalOutput").ap()
        dbg_f = nc.dram_tensor("dbg_f", [4, 128, D], F32, kind="ExternalOutput").ap()
        dbg_och = nc.dram_tensor("dbg_och", [NCH, 128, D], F32, kind="ExternalOutput").ap()
        dbg_grow = nc.dram_tensor("dbg_grow", [1, 8], F32, kind="ExternalOutput").ap()

    _i = np.arange(128, dtype=np.float32)
    diff = _i[None, :] - _i[:, None]
    tri = (diff >= 0).astype(np.float32)
    diffm = diff * tri
    c_ident = nc.inline_tensor(np.eye(128, dtype=np.float32), "c_ident").ap()
    c_identb = nc.inline_tensor(np.eye(128, dtype=np.float32).astype(ml_dtypes.bfloat16),
                                "c_identb").ap()
    c_onec = nc.inline_tensor(np.ones((128, 1), np.float32), "c_onec").ap()
    c_oner = nc.inline_tensor(np.ones((1, 128), np.float32), "c_oner").ap()
    c_diffm = nc.inline_tensor(diffm, "c_diffm").ap()
    c_tri = nc.inline_tensor(tri, "c_tri").ap()
    c_iop1 = nc.inline_tensor((_i + 1.0)[:, None], "c_iop1").ap()
    c_iorev = nc.inline_tensor((127.0 - _i)[:, None], "c_iorev").ap()
    c_ioci = nc.inline_tensor(
        _i[:, None] + 1.0 + 128.0 * np.arange(8, dtype=np.float32)[None, :],
        "c_ioci").ap()
    c_128 = nc.inline_tensor(np.full((128, 1), 128.0, np.float32), "c_128").ap()
    sel = np.zeros((128, 16), np.float32)
    for j in range(4):
        sel[:, 4 * j + j] = 1.0
    c_sel4 = nc.inline_tensor(sel, "c_sel4").ap()
    FCONST_VALS = [1.0 / D, -1.0, EPS_LN, 0.5 / GCNT, 0.5 * EPS_Q, 1.0 / HD,
                   1.0, 0.0]
    c_fc = nc.inline_tensor(
        np.tile(np.asarray(FCONST_VALS, np.float32)[None, :], (128, 1)),
        "c_fc").ap()

    q1t_sp = nc.dram_tensor("q1t_sp", [NHS, NDC, 128, 128], BF16).ap()
    q2t_sp = nc.dram_tensor("q2t_sp", [NHS, NDC, 128, 128], BF16).ap()
    q3t_sp = nc.dram_tensor("q3t_sp", [NHS, NDC, 128, 128], BF16).ap()
    x_dram = nc.dram_tensor("x_dram", [NDC, 128, NT], F32).ap()
    o_sp = nc.dram_tensor("o_sp", [NCH, 128, D], F32).ap()
    q_sp = nc.dram_tensor("q_sp", [NCH, H, 128, 128], F32).ap()
    cc_gin = nc.dram_tensor("cc_gin", [1, 4], F32).ap()
    cc_gout = nc.dram_tensor("cc_gout", [1, 4], F32).ap()
    cc_bin = nc.dram_tensor("cc_bin", [128, D], F32).ap()
    cc_bout = nc.dram_tensor("cc_bout", [NCORES, 128, D], F32).ap()

    sb = lambda n, s, d=F32: nc.alloc_sbuf_tensor(n, s, d)
    xs_h = sb("xs_h", [128, NDC * NT], BF16)
    xs_l = sb("xs_l", [128, NDC * NT], BF16)
    hidp = sb("hidp", [128, 22528])
    strm = sb("strm", [128, 12288], BF16)
    S_all = sb("S_all", [128, D])
    g1h = sb("g1h", [128, NT], BF16)
    g1l = sb("g1l", [128, NT], BF16)
    Mb = sb("Mb", [128, NT])
    Rb = sb("Rb", [128, NT])
    xt_sb = sb("xt_sb", [128, 512])
    xev = sb("xev", [128, 512])
    xnew = sb("xnew", [128, 512])
    sq_sb = sb("sq_sb", [128, 512])
    gacc = sb("gacc", [128, 48])
    identt = sb("identt", [128, 128])
    identb = sb("identb", [128, 128], BF16)
    onec = sb("onec", [128, 1])
    oner = sb("oner", [1, 128])
    iop1 = sb("iop1", [128, 1])
    iorev = sb("iorev", [128, 1])
    ioci = sb("ioci", [128, 8])
    c128t = sb("c128t", [128, 1])
    fconst = sb("fconst", [128, 8])
    ld_sb = sb("ld_sb", [1, 16])
    ldb = sb("ldb", [128, 16])
    qdcd = sb("qdcd", [128, H * 8])
    kdc = sb("kdc", [128, H])
    cdc = sb("cdc", [128, H])
    nwc = sb("nwc", [128, NDC])
    thr = sb("thr", [128, 8])
    cidc = sb("cidc", [128, 1])
    tjc = sb("tjc", [128, 4])
    grow = sb("grow", [1, 8])

    banks = [nc.alloc_psum_tensor(f"bank{i}", [128, 512], F32) for i in range(8)]
    for _j, _v in enumerate(FCONST_VALS):
        nc.const_aps.register(F32, float(_v), fconst[:, _j:_j + 1])

    b_xs = Buf("xs")
    b_h = [Buf(f"h{i}") for i in range(22)]
    b_s = [Buf(f"s{i}") for i in range(6)]
    b_S = Buf("S")
    b_g1 = Buf("g1")
    b_const = Buf("const")
    b_decay = Buf("decay")
    b_MR = Buf("MR")
    b_grow = Buf("grow")
    b_thr = Buf("thr")
    b_bank = [Buf(f"bank{i}") for i in range(8)]
    b_xin = Buf("xin")
    b_xdram = Buf("xdram")
    b_xout = Buf("xout")
    b_qsp = [[Buf(f"q{w}sp{h}") for h in range(NHS)] for w in range(3)]
    b_osp = Buf("osp")
    b_qspill = Buf("qspill")
    b_ccg = Buf("ccg")
    b_ccb = Buf("ccb")
    b_xt = Buf("xt")
    b_xev = Buf("xev")
    b_xnew = Buf("xnew")
    b_sq = Buf("sq")
    b_gacc = Buf("gacc")
    b_nw = Buf("nw")
    b_cid = Buf("cid")

    def hreg(lo_f32, n_f32):
        i0 = lo_f32 // 1024
        i1 = (lo_f32 + n_f32 + 1023) // 1024
        return [b_h[i] for i in range(i0, i1)]

    def sreg(lo_b, n_b):
        i0 = lo_b // 2048
        i1 = (lo_b + n_b + 2047) // 2048
        return [b_s[i] for i in range(i0, i1)]

    def dma(q, out, in_, reads, writes, tag=""):
        return P.add(q, lambda e, o=out, i=in_: e.dma_start(out=o, in_=i),
                     reads=reads, writes=writes, dma=True, tag=tag)

    def dma_tr(out, in_, reads, writes, tag=""):
        return P.add("scalar", lambda e, o=out, i=in_: e.dma_start(out=o, in_=i, transpose=True),
                     reads=reads, writes=writes, dma=True, tag=tag)

    def mm(ps, lhsT, rhs, st, sp, reads, wbank, tag=""):
        return P.add("tensor", lambda e, p=ps, l=lhsT, r=rhs, a=st, b=sp:
                     e.matmul(p, l, r, start=a, stop=b),
                     reads=reads, writes=[wbank], tag=tag)

    for dst, src, t in [(identt[:], c_ident, "id"), (onec[:], c_onec, "oc"),
                        (oner[:], c_oner, "or"), (iop1[:], c_iop1, "i1"),
                        (iorev[:], c_iorev, "ir"), (ioci[:], c_ioci, "ic"),
                        (c128t[:], c_128, "c128"),
                        (fconst[:], c_fc, "fc"), (identb[:], c_identb, "idb"),
                        (ld_sb[:], ld_in, "ld")]:
        dma("sync", dst, src, [], [b_const], t)
    dma("sync", nwc[:], nw_in, [], [b_nw], "nw")
    dma("sync", grow[0:1, 0:1], cid_in, [], [b_grow], "cid0")
    P.add("tensor", lambda e: e.matmul(banks[6][:, 0:1], oner[:], grow[0:1, 0:1],
                                       start=True, stop=True),
          reads=[b_grow, b_const], writes=[b_bank[6]], tag="cidbc")
    P.add("scalar", lambda e: e.copy(cidc[:], banks[6][:, 0:1]),
          reads=[b_bank[6]], writes=[b_cid], tag="cidcp")

    gstage = hidp[:, 0:2048]
    r_gst = hreg(0, 2048)

    def emit_gamma(w_ap, idx):
        cbase = idx * 12
        t = w_ap.rearrange("(a p) d -> a p d", p=64)
        for a in range(11):
            dma("sync", gstage[0:64, 0:D], t[a], r_gst, r_gst, f"gld{idx}_{a}")
            P.add("vector", lambda e, a=a: e.tensor_reduce(
                gacc[0:64, cbase + a:cbase + a + 1], gstage[0:64, 0:D], AX,
                ALU.add, apply_absolute_value=True),
                reads=r_gst, writes=[b_gacc], tag=f"gr{idx}_{a}")
        P.add("vector", lambda e: e.tensor_reduce(
            gacc[0:64, cbase + 11:cbase + 12], gacc[0:64, cbase:cbase + 11],
            AX, ALU.add), reads=[b_gacc], writes=[b_gacc], tag=f"gs{idx}")
        mm(banks[4 + idx][0:1, 0:1], onec[0:64, 0:1],
           gacc[0:64, cbase + 11:cbase + 12], True, True,
           [b_gacc, b_const], b_bank[4 + idx], f"gm{idx}")

    emit_gamma(w1g_in, 0)
    emit_gamma(w2g_in, 1)
    t3 = w3g_in.rearrange("(a p) h -> a p h", p=128)
    for a in range(16):
        dma("sync", gstage[:, 0:704], t3[a], r_gst, r_gst, f"gld2_{a}")
        P.add("vector", lambda e, a=a: e.tensor_reduce(
            gacc[:, 32 + a:33 + a], gstage[:, 0:704], AX, ALU.add,
            apply_absolute_value=True),
            reads=r_gst, writes=[b_gacc], tag=f"gr2_{a}")
    P.add("vector", lambda e: e.tensor_reduce(
        gacc[:, 30:31], gacc[:, 32:48], AX, ALU.add),
        reads=[b_gacc], writes=[b_gacc], tag="gs2a")
    mm(banks[6][0:1, 0:1], onec[:], gacc[:, 30:31], True, True,
       [b_gacc, b_const], b_bank[6], "gm2")
    for j in range(3):
        P.add("scalar", lambda e, j=j: e.copy(grow[0:1, j:j + 1], banks[4 + j][0:1, 0:1]),
              reads=[b_bank[4 + j]], writes=[b_grow], tag=f"gc{j}")
    P.add("scalar", lambda e: e.memzero(grow[0:1, 3:4]),
          reads=[], writes=[b_grow], tag="gcz")
    if KDBG:
        dma("gpsimd", dbg_gacc, gacc[:], [b_gacc], [b_xout], "dgacc")
        dma("gpsimd", dbg_grow, grow[:], [b_grow], [b_xout], "dgrow")
    dma("gpsimd", cc_gin, grow[0:1, 0:4], [b_grow], [b_ccg], "gw")
    P.add("gpsimd", lambda e: e.collective_compute(
        "AllReduce", ALU.add, replica_groups=[list(range(NCORES))],
        ins=[cc_gin], outs=[cc_gout]),
        reads=[b_ccg], writes=[b_ccg], cc=True, tag="gcc")
    dma("gpsimd", grow[0:1, 4:8], cc_gout, [b_ccg], [b_grow], "gr")
    P.add("scalar", lambda e: e.activation(grow[0:1, 0:3], grow[0:1, 4:7], AF.Copy,
                                           scale=0.5 / GCNT, bias=0.5 * EPS_Q),
          reads=[b_grow, b_const], writes=[b_grow], tag="gt")
    P.add("scalar", lambda e: e.mul(grow[0:1, 4:7], grow[0:1, 0:3], -1.0),
          reads=[b_grow, b_const], writes=[b_grow], tag="gtn")
    P.add("tensor", lambda e: e.matmul(banks[6][:, 0:8], oner[:], grow[0:1, 0:8],
                                       start=True, stop=True),
          reads=[b_grow, b_const], writes=[b_bank[6]], tag="gbc")
    P.add("scalar", lambda e: e.copy(thr[:], banks[6][:, 0:8]),
          reads=[b_bank[6]], writes=[b_thr], tag="gthr")

    def emit_quant_hs(hs):
        qstg = [(Mb[:, :], [b_MR]), (Rb[:, :], [b_MR])]
        qnat_s = [(S_all[:, 0:1024], [b_S]),
                  (S_all[:, 1024:2048], [b_S])]
        qtr_s = [(g1h[:, :], [b_g1]), (g1l[:, :], [b_g1])]
        w3v = w3_in.rearrange("(t p) h -> p t h", p=128)
        if True:
            for widx, (w_ap, sp_ap) in enumerate(
                    ((w1_in, q1t_sp), (w2_in, q2t_sp), (w3_in, q3t_sp))):
                for half in range(2):
                    sl = (hs * 6 + widx * 2 + half) % 2
                    stg, r_sl = qstg[sl]
                    qn, r_qn = qnat_s[sl]
                    qt, r_qt = qtr_s[sl]
                    if widx < 2:
                        dma("sync", stg, w_ap[hs * 128:(hs + 1) * 128,
                                              half * 1024:(half + 1) * 1024],
                            r_sl, r_sl, f"qld{widx}_{hs}_{half}")
                    else:
                        dma("sync", stg.rearrange("p (t h) -> p t h", t=8),
                            w3v[:, half * 8:(half + 1) * 8, hs * 128:(hs + 1) * 128],
                            r_sl, r_sl, f"qld{widx}_{hs}_{half}")
                    P.add("vector", lambda e, qn=qn, stg=stg, widx=widx:
                          e.tensor_single_scalar(qn, stg, thr[:, 4 + widx:5 + widx],
                                                 ALU.is_le),
                          reads=r_sl + [b_thr] + r_qn, writes=r_qn, tag=f"qa{widx}_{hs}{half}")
                    P.add("vector", lambda e, qn=qn, stg=stg, widx=widx:
                          e.scalar_tensor_tensor(qn, stg, thr[:, widx:widx + 1],
                                                 qn, ALU.is_ge, ALU.subtract),
                          reads=r_sl + [b_thr] + r_qn, writes=r_qn, tag=f"qb{widx}_{hs}{half}")
                    for t in range(8):
                        bki = 5 if t % 2 == 0 else 7
                        P.add("tensor", lambda e, qn=qn, t=t, bki=bki: e.transpose(
                            banks[bki][:, 0:128], qn[:, t * 128:(t + 1) * 128], identt[:]),
                            reads=r_qn + [b_const], writes=[b_bank[bki]],
                            tag=f"qT{widx}_{hs}_{half}_{t}")
                        P.add("scalar", lambda e, qt=qt, t=t, bki=bki: e.copy(
                            qt[:, t * 128:(t + 1) * 128], banks[bki][:, 0:128]),
                            reads=[b_bank[bki]], writes=r_qt,
                            tag=f"qE{widx}_{hs}_{half}_{t}")
                    dma("gpsimd", sp_ap[hs, half * 8:(half + 1) * 8].rearrange("t p h -> p t h"),
                        qt.rearrange("p (t h) -> p t h", t=8),
                        r_qt, [b_qsp[widx][hs]], f"qs{widx}_{hs}_{half}")

    def emit_stats_standalone(src):
        for dc in range(NDC):
            st, sp = dc == 0, dc == NDC - 1
            for hf in range(2):
                dma("sync", xt_sb[:], src[dc][:, hf * 512:(hf + 1) * 512],
                    [b_xt], [b_xt], f"sld{dc}_{hf}")
                P.add("scalar", lambda e: e.square(sq_sb[:], xt_sb[:]),
                      reads=[b_xt], writes=[b_sq], tag=f"sq{dc}_{hf}")
                mm(banks[4 + hf][0:1, :], onec[:], xt_sb[:], st, sp,
                   [b_xt, b_const], b_bank[4 + hf], f"sm{dc}_{hf}")
                mm(banks[6 + hf][0:1, :], onec[:], sq_sb[:], st, sp,
                   [b_sq, b_const], b_bank[6 + hf], f"sz{dc}_{hf}")

    rowM = Mb[0:1, :]
    rowR = Rb[0:1, :]

    def emit_mr(tag):
        for hf in range(2):
            P.add("scalar", lambda e, hf=hf: e.activation(
                Mb[0:1, hf * 512:(hf + 1) * 512], banks[4 + hf][0:1, :],
                AF.Copy, scale=1.0 / D),
                reads=[b_bank[4 + hf], b_const], writes=[b_MR], tag=f"m{tag}{hf}")
            P.add("scalar", lambda e, hf=hf: e.activation(
                Rb[0:1, hf * 512:(hf + 1) * 512], banks[6 + hf][0:1, :],
                AF.Copy, scale=1.0 / D),
                reads=[b_bank[6 + hf], b_const], writes=[b_MR], tag=f"e{tag}{hf}")
        for hf in range(2):
            P.add("tensor", lambda e, hf=hf: e.matmul(
                banks[0][:, :], oner[:], Mb[0:1, hf * 512:(hf + 1) * 512],
                start=True, stop=True),
                reads=[b_MR, b_const], writes=[b_bank[0]], tag=f"mb{tag}{hf}")
            P.add("scalar", lambda e, hf=hf: e.copy(
                Mb[:, hf * 512:(hf + 1) * 512], banks[0][:, :]),
                reads=[b_bank[0]], writes=[b_MR], tag=f"mc{tag}{hf}")
            P.add("tensor", lambda e, hf=hf: e.matmul(
                banks[1][:, :], oner[:], Rb[0:1, hf * 512:(hf + 1) * 512],
                start=True, stop=True),
                reads=[b_MR, b_const], writes=[b_bank[1]], tag=f"rb{tag}{hf}")
            P.add("scalar", lambda e, hf=hf: e.copy(
                Rb[:, hf * 512:(hf + 1) * 512], banks[1][:, :]),
                reads=[b_bank[1]], writes=[b_MR], tag=f"rc{tag}{hf}")
        for hf in range(2):
            P.add("vector", lambda e, hf=hf: e.tensor_mul(
                sq_sb[:], Mb[:, hf * 512:(hf + 1) * 512], Mb[:, hf * 512:(hf + 1) * 512]),
                reads=[b_MR], writes=[b_sq], tag=f"v1{tag}{hf}")
            P.add("vector", lambda e, hf=hf: e.tensor_sub(
                Rb[:, hf * 512:(hf + 1) * 512], Rb[:, hf * 512:(hf + 1) * 512], sq_sb[:]),
                reads=[b_MR, b_sq], writes=[b_MR], tag=f"v2{tag}{hf}")
            P.add("scalar", lambda e, hf=hf: e.activation(
                Rb[:, hf * 512:(hf + 1) * 512], Rb[:, hf * 512:(hf + 1) * 512],
                AF.Sqrt, bias=EPS_LN),
                reads=[b_MR, b_const], writes=[b_MR], tag=f"v3{tag}{hf}")
            P.add("vector", lambda e, hf=hf: e.reciprocal(
                Rb[:, hf * 512:(hf + 1) * 512], Rb[:, hf * 512:(hf + 1) * 512]),
                reads=[b_MR], writes=[b_MR], tag=f"v4{tag}{hf}")

    def emit_xsplit(src, tag, src_buf):
        for dc in range(NDC):
            for hf in range(2):
                o = dc * NT + hf * 512
                par = (dc * 2 + hf) % 2
                buf = xt_sb if par == 0 else sq_sb
                bb = b_xt if par == 0 else b_sq
                dma("sync", buf[:], src[dc][:, hf * 512:(hf + 1) * 512],
                    [bb, src_buf], [bb], f"x{tag}l{dc}_{hf}")
                P.add("vector", lambda e, hf=hf, buf=buf: e.tensor_sub(
                    buf[:], buf[:], Mb[:, hf * 512:(hf + 1) * 512]),
                    reads=[b_MR, bb], writes=[bb], tag=f"x{tag}s{dc}_{hf}")
                P.add("vector", lambda e, hf=hf, buf=buf: e.tensor_mul(
                    buf[:], buf[:], Rb[:, hf * 512:(hf + 1) * 512]),
                    reads=[b_MR, bb], writes=[bb], tag=f"x{tag}m{dc}_{hf}")
                P.add("scalar", lambda e, o=o, buf=buf: e.copy(xs_h[:, o:o + 512], buf[:]),
                      reads=[bb], writes=[b_xs], tag=f"x{tag}h{dc}_{hf}")
                P.add("vector", lambda e, o=o, buf=buf: e.tensor_sub(
                    xs_l[:, o:o + 512], buf[:], xs_h[:, o:o + 512]),
                    reads=[bb, b_xs], writes=[b_xs], tag=f"x{tag}L{dc}_{hf}")

    hid_h = hidp[:, 0:11264].bitcast(BF16)
    hid_l = hidp[:, 11264:22528].bitcast(BF16)
    r_hidh = hreg(0, 11264)
    r_hidl = hreg(11264, 11264)

    b_q3s = [Buf(f"q3s{i}") for i in range(4)]

    def emit_ffn(add_src, dst, fuse_stats, tag, src_buf, dst_buf, zip_cb=None):
        for tile in range(2):
            c0 = tile * 512
            for hs in range(NHS):
                if zip_cb is not None and tile == 0:
                    zip_cb(hs)
                slot = hs % 2
                so = slot * 4096
                q1b = strm[:, so:so + 2048]
                q2b = strm[:, so + 2048:so + 4096]
                rs = sreg(so, 4096)
                dma("sync", q1b.rearrange("p (t h) -> p t h", t=NDC),
                    q1t_sp[hs].rearrange("t p h -> p t h"),
                    [b_qsp[0][hs]] + rs, rs, f"f{tag}q1_{tile}_{hs}")
                dma("sync", q2b.rearrange("p (t h) -> p t h", t=NDC),
                    q2t_sp[hs].rearrange("t p h -> p t h"),
                    [b_qsp[1][hs]] + rs, rs, f"f{tag}q2_{tile}_{hs}")
                pA1, bA1 = banks[slot], b_bank[slot]
                pA2, bA2 = banks[2 + slot], b_bank[2 + slot]
                for dc in range(NDC):
                    xh = xs_h[:, dc * NT + c0:dc * NT + c0 + 512]
                    xl = xs_l[:, dc * NT + c0:dc * NT + c0 + 512]
                    q1 = q1b[:, dc * 128:(dc + 1) * 128]
                    q2 = q2b[:, dc * 128:(dc + 1) * 128]
                    mm(pA1[:], q1, xh, dc == 0, False, rs + [b_xs], bA1, "fA")
                    mm(pA1[:], q1, xl, False, dc == NDC - 1, rs + [b_xs], bA1, "fA")
                    mm(pA2[:], q2, xh, dc == 0, False, rs + [b_xs], bA2, "fA")
                    mm(pA2[:], q2, xl, False, dc == NDC - 1, rs + [b_xs], bA2, "fA")
                hh = hid_h[:, hs * 512:(hs + 1) * 512]
                hl = hid_l[:, hs * 512:(hs + 1) * 512]
                rh = hreg(hs * 256, 256) + hreg(5632 + hs * 256, 256)
                P.add("scalar", lambda e, pA1=pA1: e.activation(sq_sb[:], pA1[:], AF.Sigmoid),
                      reads=[bA1], writes=[b_sq], tag=f"fsg{hs}")
                P.add("vector", lambda e, pA2=pA2: e.tensor_mul(xev[:], sq_sb[:], pA2[:]),
                      reads=[b_sq, bA2], writes=[b_xev], tag=f"fmu{hs}")
                P.add("scalar", lambda e, hh=hh: e.copy(hh, xev[:]),
                      reads=[b_xev], writes=rh, tag=f"fhh{hs}")
                P.add("vector", lambda e, hh=hh, hl=hl: e.tensor_sub(hl, xev[:], hh),
                      reads=[b_xev] + rh, writes=rh, tag=f"fhl{hs}")
            for g0, g1_ in ((0, 4), (4, 8), (8, 12), (12, 16)):
                nb = g1_ - g0
                for hs in range(NHS):
                    slot = hs % 4
                    so = 8192 + slot * 512
                    q3b = strm[:, so:so + 512]
                    rs3 = [b_q3s[slot]]
                    dma("sync", q3b[:, 0:nb * 128].rearrange("p (t h) -> p t h", t=nb),
                        q3t_sp[hs, g0:g1_].rearrange("t p h -> p t h"),
                        [b_qsp[2][hs]] + rs3, rs3, f"f{tag}q3_{tile}_{g0}_{hs}")
                    hh = hid_h[:, hs * 512:(hs + 1) * 512]
                    hl = hid_l[:, hs * 512:(hs + 1) * 512]
                    rh = hreg(hs * 256, 256) + hreg(5632 + hs * 256, 256)
                    for d2 in range(g0, g1_):
                        q3 = q3b[:, (d2 - g0) * 128:(d2 - g0 + 1) * 128]
                        pB, bB = banks[d2 - g0], b_bank[d2 - g0]
                        mm(pB[:], q3, hh, hs == 0, False, rs3 + rh, bB, "fB")
                        mm(pB[:], q3, hl, False, hs == NHS - 1, rs3 + rh, bB, "fB")
                for d2 in range(g0, g1_):
                    pB, bB = banks[d2 - g0], b_bank[d2 - g0]
                    dma("sync", xev[:], add_src[d2][:, c0:c0 + 512],
                        [b_xev, src_buf], [b_xev], f"f{tag}xl{tile}_{d2}")
                    P.add("vector", lambda e, pB=pB: e.scalar_tensor_tensor(
                        xnew[:], pB[:], 0.5, xev[:], ALU.mult, ALU.add),
                        reads=[bB, b_xev], writes=[b_xnew], tag=f"fev{d2}")
                    dma("gpsimd", dst[d2][:, c0:c0 + 512], xnew[:],
                        [b_xnew], [dst_buf], f"f{tag}xs{tile}_{d2}")
                    if fuse_stats:
                        first = (d2 == g0 == 0)
                        last = (d2 == NDC - 1)
                        P.add("scalar", lambda e: e.square(sq_sb[:], xnew[:]),
                              reads=[b_xnew], writes=[b_sq], tag=f"fq{d2}")
                        mm(banks[4 + tile][0:1, :], onec[:], xnew[:], first, last,
                           [b_xnew, b_const], b_bank[4 + tile], f"fs{d2}")
                        mm(banks[6 + tile][0:1, :], onec[:], sq_sb[:], first, last,
                           [b_sq, b_const], b_bank[6 + tile], f"fz{d2}")

    emit_stats_standalone(x_in)
    emit_mr("p1")
    emit_xsplit(x_in, "p1", b_xin)
    emit_ffn(x_in, x_dram, True, "1", b_xin, b_xdram, zip_cb=emit_quant_hs)

    if KDBG:
        for dc in range(NDC):
            dma("gpsimd", dbg_xa[dc], x_dram[dc], [b_xdram], [b_xout], f"dxa{dc}")
        dma("gpsimd", dbg_thr, thr[:], [b_thr], [b_xout], "dthr")
    emit_mr("p2")
    emit_xsplit(x_dram, "p2", b_xdram)
    if KDBG:
        dma("gpsimd", dbg_xs[0], xs_h[:], [b_xs], [b_xout], "dxs0")
        dma("gpsimd", dbg_xs[1], xs_l[:], [b_xs], [b_xout], "dxs1")

    P.add("tensor", lambda e: e.matmul(banks[6][:, 0:16], oner[:], ld_sb[0:1, :],
                                       start=True, stop=True),
          reads=[b_const], writes=[b_bank[6]], tag="ldb")
    P.add("scalar", lambda e: e.copy(ldb[:], banks[6][:, 0:16]),
          reads=[b_bank[6]], writes=[b_decay], tag="ldbc")
    DTm = hidp[:, 0:2048]
    r_DT = hreg(0, 2048)
    difft = hidp[:, 11520:11648]
    trit = hidp[:, 11648:11776]
    r_dtc = hreg(11520, 256)
    dma("sync", difft[:], c_diffm, r_dtc, r_dtc, "df")
    dma("sync", trit[:], c_tri, r_dtc, r_dtc, "tr")
    for h in range(H):
        P.add("scalar", lambda e, h=h: e.activation(
            qdcd[:, h * 8:(h + 1) * 8], ioci[:], AF.Exp, scale=ldb[:, h:h + 1]),
            reads=[b_const, b_decay], writes=[b_decay], tag=f"qd{h}")
        P.add("scalar", lambda e, h=h: e.activation(
            kdc[:, h:h + 1], iorev[:], AF.Exp, scale=ldb[:, h:h + 1]),
            reads=[b_const, b_decay], writes=[b_decay], tag=f"kd{h}")
        P.add("scalar", lambda e, h=h: e.activation(
            cdc[:, h:h + 1], c128t[:], AF.Exp, scale=ldb[:, h:h + 1]),
            reads=[b_const, b_decay], writes=[b_decay], tag=f"cd{h}")
        P.add("scalar", lambda e, h=h: e.activation(
            DTm[:, h * 128:(h + 1) * 128], difft[:], AF.Exp, scale=ldb[:, h:h + 1]),
            reads=r_dtc + [b_decay], writes=r_DT, tag=f"dt{h}")
        P.add("vector", lambda e, h=h: e.tensor_mul(
            DTm[:, h * 128:(h + 1) * 128], DTm[:, h * 128:(h + 1) * 128], trit[:]),
            reads=r_dtc + r_DT, writes=r_DT, tag=f"dm{h}")

    wg1s = hidp[:, 12288:14336]
    wg1h = hidp[:, 14336:15360].bitcast(BF16)
    wg1l = hidp[:, 15360:16384].bitcast(BF16)
    r_wg1s = hreg(12288, 2048)
    r_wg1 = hreg(14336, 2048)
    g1f = hidp[:, 16384:17408]
    r_g1f = hreg(16384, 1024)
    dma("sync", wg1s[:].rearrange("p (a g) -> p a g", a=NDC),
        wg1_in.rearrange("(a p) g -> p a g", p=128),
        r_wg1s, r_wg1s, "wg1l")
    P.add("scalar", lambda e: e.copy(wg1h[:], wg1s[:]), reads=r_wg1s, writes=r_wg1, tag="wg1h")
    P.add("vector", lambda e: e.tensor_sub(wg1l[:], wg1s[:], wg1h[:]),
          reads=r_wg1s + r_wg1, writes=r_wg1, tag="wg1lo")
    for h2 in range(2):
        pG, bG = banks[6], b_bank[6]
        for dc in range(NDC):
            xh = xs_h[:, dc * NT + h2 * 512:dc * NT + h2 * 512 + 512]
            xl = xs_l[:, dc * NT + h2 * 512:dc * NT + h2 * 512 + 512]
            w_h = wg1h[:, dc * 128:(dc + 1) * 128]
            w_l = wg1l[:, dc * 128:(dc + 1) * 128]
            mm(pG[:], w_h, xh, dc == 0, False, r_wg1 + [b_xs], bG, "g1")
            mm(pG[:], w_l, xh, False, False, r_wg1 + [b_xs], bG, "g1")
            mm(pG[:], w_h, xl, False, dc == NDC - 1, r_wg1 + [b_xs], bG, "g1")
        P.add("scalar", lambda e, pG=pG: e.copy(g1f[:, 0:512], pG[:]),
              reads=[bG], writes=r_g1f, tag=f"g1e{h2}")
        P.add("scalar", lambda e, h2=h2: e.copy(g1h[:, h2 * 512:(h2 + 1) * 512], g1f[:, 0:512]),
              reads=r_g1f, writes=[b_g1], tag=f"g1h{h2}")
        P.add("vector", lambda e, h2=h2: e.tensor_sub(
            g1l[:, h2 * 512:(h2 + 1) * 512], g1f[:, 0:512], g1h[:, h2 * 512:(h2 + 1) * 512]),
            reads=r_g1f + [b_g1], writes=[b_g1], tag=f"g1l{h2}")

    qkv_sets = []
    for par, base in ((0, 5120), (1, 16896)):
        qkv_sets.append((
            hidp[:, base:base + 1024], hreg(base, 1024),
            hidp[:, base + 1024:base + 2048], hreg(base + 1024, 1024),
            hidp[:, base + 2048:base + 3072], hreg(base + 2048, 1024),
        ))
    khat = hidp[:, 8192:8320]
    sTt = hidp[:, 8320:8448]
    otmp = hidp[:, 8448:8576]
    ogac = hidp[:, 8576:9600]
    ptmp = hidp[:, 10880:11392]
    r_sc = hreg(8192, 384)
    r_og = hreg(8576, 1024)
    r_pt = hreg(10880, 512)

    for h in range(H):
        qf, r_qf, kf, r_kf, vf, r_vf = qkv_sets[h % 2]
        wbufs = []
        for wi, w_ap in enumerate((wq_in, wk_in)):
            wh_b = strm[:, wi * 4096:wi * 4096 + 2048]
            wl_b = strm[:, wi * 4096 + 2048:wi * 4096 + 4096]
            rw = sreg(wi * 4096, 4096)
            sl = (h * 3 + wi) % 2
            wstg = hidp[:, 12288 + sl * 2048:12288 + sl * 2048 + 2048]
            r_ws = hreg(12288 + sl * 2048, 2048)
            dma("sync", wstg[:].rearrange("p (a e) -> p a e", a=NDC),
                w_ap.rearrange("(a p) e -> p a e", p=128)[:, :, h * 128:(h + 1) * 128],
                r_ws, r_ws, f"wl{h}_{wi}")
            P.add("scalar", lambda e, wh_b=wh_b, wstg=wstg: e.copy(wh_b[:], wstg[:]),
                  reads=r_ws, writes=rw, tag=f"wh{h}_{wi}")
            P.add("vector", lambda e, wh_b=wh_b, wl_b=wl_b, wstg=wstg: e.tensor_sub(
                wl_b[:], wstg[:], wh_b[:]),
                reads=r_ws + rw, writes=rw, tag=f"wlo{h}_{wi}")
            wbufs.append((wh_b, wl_b, rw))

        for wi, dstf, rdst in ((0, qf, r_qf), (1, kf, r_kf)):
            wh_b, wl_b, rw = wbufs[wi]
            for h2 in range(2):
                pP, bP = banks[h2], b_bank[h2]
                for dc in range(NDC):
                    xh = xs_h[:, dc * NT + h2 * 512:dc * NT + h2 * 512 + 512]
                    xl = xs_l[:, dc * NT + h2 * 512:dc * NT + h2 * 512 + 512]
                    w_h = wh_b[:, dc * 128:(dc + 1) * 128]
                    w_l = wl_b[:, dc * 128:(dc + 1) * 128]
                    mm(pP[:], w_h, xh, dc == 0, False, rw + [b_xs], bP, "qk")
                    mm(pP[:], w_l, xh, False, False, rw + [b_xs], bP, "qk")
                    mm(pP[:], w_h, xl, False, dc == NDC - 1, rw + [b_xs], bP, "qk")
                P.add("scalar", lambda e, pP=pP, dstf=dstf, h2=h2: e.activation(
                    dstf[:, h2 * 512:h2 * 512 + 512], pP[:], AF.Silu),
                    reads=[bP], writes=rdst, tag=f"pe{h}_{wi}_{h2}")
        dma("gpsimd", q_sp[:, h].rearrange("ch p c -> p ch c"),
            qf[:].rearrange("p (ch c) -> p ch c", ch=NCH),
            r_qf, [b_qspill], f"qsp{h}")
        wh_b, wl_b, rw = wbufs[1]
        sl = (h * 3 + 2) % 2
        wstg = hidp[:, 12288 + sl * 2048:12288 + sl * 2048 + 2048]
        r_ws = hreg(12288 + sl * 2048, 2048)
        dma("sync", wstg[:].rearrange("p (a e) -> p a e", a=NDC),
            wv_in.rearrange("(a p) e -> p a e", p=128)[:, :, h * 128:(h + 1) * 128],
            r_ws, r_ws, f"wl{h}_2")
        P.add("scalar", lambda e, wh_b=wh_b, wstg=wstg: e.copy(wh_b[:], wstg[:]),
              reads=r_ws, writes=rw, tag=f"wh{h}_2")
        P.add("vector", lambda e, wh_b=wh_b, wl_b=wl_b, wstg=wstg: e.tensor_sub(
            wl_b[:], wstg[:], wh_b[:]),
            reads=r_ws + rw, writes=rw, tag=f"wlo{h}_2")
        for h2 in range(2):
            pV, bV = banks[2 + h2], b_bank[2 + h2]
            for dc in range(NDC):
                xh = xs_h[:, dc * NT + h2 * 512:dc * NT + h2 * 512 + 512]
                xl = xs_l[:, dc * NT + h2 * 512:dc * NT + h2 * 512 + 512]
                w_h = wh_b[:, dc * 128:(dc + 1) * 128]
                w_l = wl_b[:, dc * 128:(dc + 1) * 128]
                mm(pV[:], w_h, xh, dc == 0, False, rw + [b_xs], bV, "v")
                mm(pV[:], w_l, xh, False, False, rw + [b_xs], bV, "v")
                mm(pV[:], w_h, xl, False, dc == NDC - 1, rw + [b_xs], bV, "v")
            P.add("scalar", lambda e, pV=pV, h2=h2, vf=vf: e.activation(
                vf[:, h2 * 512:(h2 + 1) * 512], pV[:], AF.Silu),
                reads=[bV], writes=r_vf, tag=f"ve{h}_{h2}")
        for ch in range(NCH):
            pT, bT = banks[2 + ch % 2], b_bank[2 + ch % 2]
            P.add("tensor", lambda e, ch=ch, pT=pT, vf=vf: e.transpose(
                pT[:, 0:128], vf[:, ch * 128:(ch + 1) * 128], identt[:]),
                reads=r_vf + [b_const], writes=[bT], tag=f"vt{h}_{ch}")
            P.add("scalar", lambda e, ch=ch, pT=pT, vf=vf: e.copy(
                vf[:, ch * 128:(ch + 1) * 128], pT[:, 0:128]),
                reads=[bT], writes=r_vf, tag=f"vc{h}_{ch}")

        P.add("vector", lambda e, h=h: e.memset(S_all[:, h * 128:(h + 1) * 128], 0.0),
              reads=[], writes=[b_S], tag=f"sz{h}")

        Ssl = S_all[:, h * 128:(h + 1) * 128]
        for ci in range(NCH):
            qsl = qf[:, ci * 128:(ci + 1) * 128]
            ksl = kf[:, ci * 128:(ci + 1) * 128]
            vsl = vf[:, ci * 128:(ci + 1) * 128]
            mm(banks[4][:, 0:128], ksl, qsl, True, True, r_kf + r_qf, b_bank[4], "sc")
            P.add("vector", lambda e, h=h: e.tensor_mul(
                sTt[:], banks[4][:, 0:128], DTm[:, h * 128:(h + 1) * 128]),
                reads=[b_bank[4]] + r_DT, writes=r_sc, tag=f"sm{h}_{ci}")
            mm(banks[5][:, 0:128], sTt[:], vsl, True, True, r_sc + r_vf,
               b_bank[5], "oi")
            if ci > 0:
                mm(banks[6][:, 0:128], qsl, Ssl, True, True, r_qf + [b_S],
                   b_bank[6], "ox")
                P.add("scalar", lambda e, h=h: e.activation(
                    otmp[:], banks[6][:, 0:128], AF.Copy,
                    scale=qdcd[:, h * 8:h * 8 + 1]),
                    reads=[b_bank[6], b_decay], writes=r_sc, tag=f"oxs{h}_{ci}")
                P.add("vector", lambda e, ci=ci: e.tensor_add(
                    ogac[:, ci * 128:(ci + 1) * 128], banks[5][:, 0:128], otmp[:]),
                    reads=[b_bank[5]] + r_sc, writes=r_og, tag=f"oev{h}_{ci}")
            else:
                P.add("vector", lambda e, ci=ci: e.tensor_copy(
                    ogac[:, ci * 128:(ci + 1) * 128], banks[5][:, 0:128]),
                    reads=[b_bank[5]], writes=r_og, tag=f"oev{h}_{ci}")
            P.add("tensor", lambda e, ksl=ksl: e.transpose(
                banks[7][:, 128:256], ksl, identt[:]),
                reads=r_kf + [b_const], writes=[b_bank[7]], tag=f"kt{h}_{ci}")
            P.add("scalar", lambda e, h=h: e.activation(
                khat[:], banks[7][:, 128:256], AF.Copy, scale=kdc[:, h:h + 1]),
                reads=[b_bank[7], b_decay], writes=r_sc, tag=f"ks{h}_{ci}")
            mm(banks[6][:, 128:256], khat[:], vsl, True, True, r_sc + r_vf,
               b_bank[6], "ds")
            P.add("vector", lambda e, h=h, Ssl=Ssl: e.scalar_tensor_tensor(
                Ssl, Ssl, cdc[:, h:h + 1], banks[6][:, 128:256],
                ALU.mult, ALU.add),
                reads=[b_S, b_decay, b_bank[6]], writes=[b_S], tag=f"su{h}_{ci}")
        dma("gpsimd", o_sp[:, :, h * 128:(h + 1) * 128].rearrange("ch p c -> p ch c"),
            ogac[:].rearrange("p (ch c) -> p ch c", ch=NCH),
            r_og, [b_osp], f"osp{h}")
        if KDBG and h == 0:
            dma("gpsimd", dbg_f[0][:, 0:1024], qf[:], r_qf, [b_xout], "dq0")
            dma("gpsimd", dbg_f[1][:, 0:2048].bitcast(BF16), strm[:, 8192:12288],
                b_q3s + [b_s[5]], [b_xout], "dwv")
            dma("gpsimd", dbg_f[2][:, 0:1024], vf[:], r_vf, [b_xout], "dv0")
            dma("gpsimd", dbg_f[3][:, 0:128], S_all[:, 0:128], [b_S], [b_xout], "dS0")

    dma("gpsimd", cc_bin, S_all[:], [b_S], [b_ccb], "bw")
    P.add("gpsimd", lambda e: e.collective_compute(
        "AllGather", ALU.bypass, replica_groups=[list(range(NCORES))],
        ins=[cc_bin], outs=[cc_bout]),
        reads=[b_ccb], writes=[b_ccb], cc=True, tag="bcc")
    if KDBG:
        for j in range(NCORES):
            dma("gpsimd", dbg_B[j], cc_bout[j], [b_ccb], [b_xout], f"dB{j}")
        for ci in range(NCH):
            dma("gpsimd", dbg_osp[ci], o_sp[ci], [b_osp], [b_xout], f"dO{ci}")
            dma("gpsimd", dbg_qsp[ci], q_sp[ci], [b_qspill], [b_xout], f"dQ{ci}")
    ldrep = hidp[0:1, 0:2048]
    LD2048 = hidp[:, 2048:4096]
    Bj = hidp[:, 4096:6144]
    FBj = hidp[:, 6144:8192]
    r_ldr = hreg(0, 2048)
    r_LD = hreg(2048, 2048)
    r_Bj = hreg(4096, 2048)
    r_FB = hreg(6144, 2048)
    P.add("vector", lambda e: e.tensor_copy(
        ldrep.rearrange("q (h r) -> q h r", h=16),
        ld_sb[0:1, :].rearrange("q h -> q h 1" if False else "q (h r) -> q h r", r=1
                                ).broadcast_to([1, 16, 128])),
        reads=[b_const], writes=r_ldr, tag="ldrep")
    for h4 in range(4):
        P.add("tensor", lambda e, h4=h4: e.matmul(
            banks[6][:, :], oner[:], ldrep[0:1, h4 * 512:(h4 + 1) * 512],
            start=True, stop=True),
            reads=r_ldr + [b_const], writes=[b_bank[6]], tag=f"ldm{h4}")
        P.add("scalar", lambda e, h4=h4: e.copy(LD2048[:, h4 * 512:(h4 + 1) * 512],
                                                banks[6][:, :]),
              reads=[b_bank[6]], writes=r_LD, tag=f"ldc{h4}")
    P.add("vector", lambda e: e.memset(S_all[:], 0.0), reads=[], writes=[b_S], tag="sinz")
    for j in range(NCORES):
        P.add("vector", lambda e, j=j: e.tensor_scalar(
            tjc[:, 0:1], cidc[:], 1024.0, (j + 1) * 1024.0, ALU.mult, ALU.subtract),
            reads=[b_cid], writes=[b_cid], tag=f"tj{j}")
        P.add("vector", lambda e: e.tensor_single_scalar(
            tjc[:, 1:2], tjc[:, 0:1], 0.0, ALU.is_ge),
            reads=[b_cid], writes=[b_cid], tag=f"tm{j}")
        P.add("vector", lambda e: e.tensor_scalar(
            tjc[:, 2:3], tjc[:, 1:2], -1e6, 1e6, ALU.mult, ALU.add),
            reads=[b_cid], writes=[b_cid], tag=f"ta{j}")
        P.add("vector", lambda e: e.tensor_add(tjc[:, 0:1], tjc[:, 0:1], tjc[:, 2:3]),
              reads=[b_cid], writes=[b_cid], tag=f"tb{j}")
        P.add("scalar", lambda e: e.activation(FBj[:], LD2048[:], AF.Exp,
                                               scale=tjc[:, 0:1]),
              reads=r_LD + [b_cid], writes=r_FB, tag=f"fb{j}")
        dma("sync", Bj[:], cc_bout[j], [b_ccb] + r_Bj, r_Bj, f"bl{j}")
        P.add("vector", lambda e: e.tensor_mul(Bj[:], Bj[:], FBj[:]),
              reads=r_FB + r_Bj, writes=r_Bj, tag=f"bm{j}")
        P.add("vector", lambda e: e.tensor_add(S_all[:], S_all[:], Bj[:]),
              reads=r_Bj + [b_S], writes=[b_S], tag=f"bs{j}")

    if KDBG:
        dma("gpsimd", dbg_Sin, S_all[:], [b_S], [b_xout], "dSin")
    ogf = hidp[:, 0:8192]
    och = hidp[:, 16384:18432]
    sg = hidp[:, 18432:20480]
    qc = hidp[:, 20480:22528]
    woF = hidp[:, 12288:16384]
    ogbh = hidp[:, 0:4096].bitcast(BF16)
    ogbl = hidp[:, 5120:9216].bitcast(BF16)
    wobh = strm[:, 4096:8192]
    wobl = strm[:, 8192:12288]
    r_ogf = hreg(0, 9216)
    r_wob = sreg(4096, 8192)
    r_och = hreg(16384, 2048)
    r_sg = hreg(18432, 2048)
    r_qc = hreg(20480, 2048)
    r_wo = hreg(12288, 4096)
    wg2h = strm[:, 0:2048]
    wg2l = strm[:, 2048:4096]
    r_wg2 = sreg(0, 4096)
    for seg in range(4):
        dma("sync", ptmp[:, 0:512], wg2_in[:, seg * 512:(seg + 1) * 512], r_pt, r_pt, f"wg2_{seg}")
        P.add("scalar", lambda e, seg=seg: e.copy(
            wg2h[:, seg * 512:(seg + 1) * 512], ptmp[:, 0:512]),
            reads=r_pt, writes=r_wg2, tag=f"wg2h{seg}")
        P.add("vector", lambda e, seg=seg: e.tensor_sub(
            wg2l[:, seg * 512:(seg + 1) * 512], ptmp[:, 0:512],
            wg2h[:, seg * 512:(seg + 1) * 512]),
            reads=r_pt + r_wg2, writes=r_wg2, tag=f"wg2l{seg}")

    for hf in range(2):
        for ci in range(hf * 4, hf * 4 + 4):
            cl = ci - hf * 4
            dma("sync", och[:], o_sp[ci], [b_osp] + r_och, r_och, f"ol{ci}")
            dma("sync", qc[:].rearrange("p (h c) -> p h c", h=H),
                q_sp[ci].rearrange("h p c -> p h c"),
                [b_qspill] + r_qc, r_qc, f"ql{ci}")
            for d4 in range(4):
                pG, bG = banks[2 + d4 % 2], b_bank[2 + d4 % 2]
                g1hs = g1h[:, ci * 128:(ci + 1) * 128]
                g1ls = g1l[:, ci * 128:(ci + 1) * 128]
                mm(pG[:], g1hs, wg2h[:, d4 * 512:(d4 + 1) * 512], True, False,
                   [b_g1] + r_wg2, bG, "gt")
                mm(pG[:], g1ls, wg2h[:, d4 * 512:(d4 + 1) * 512], False, False,
                   [b_g1] + r_wg2, bG, "gt")
                mm(pG[:], g1hs, wg2l[:, d4 * 512:(d4 + 1) * 512], False, True,
                   [b_g1] + r_wg2, bG, "gt")
                P.add("scalar", lambda e, pG=pG, d4=d4: e.activation(
                    sg[:, d4 * 512:(d4 + 1) * 512], pG[:], AF.Sigmoid),
                    reads=[bG], writes=r_sg, tag=f"sig{ci}_{d4}")
            for h in range(H):
                pC, bC = banks[h % 2], b_bank[h % 2]
                mm(pC[:, 0:128], qc[:, h * 128:(h + 1) * 128],
                   S_all[:, h * 128:(h + 1) * 128],
                   True, True, r_qc + [b_S], bC, "corr")
                P.add("vector", lambda e, pC=pC, h=h, ci=ci: e.scalar_tensor_tensor(
                    och[:, h * 128:(h + 1) * 128], pC[:, 0:128],
                    qdcd[:, h * 8 + ci:h * 8 + ci + 1], och[:, h * 128:(h + 1) * 128],
                    ALU.mult, ALU.add),
                    reads=[bC, b_decay] + r_och, writes=r_och, tag=f"cadd{ci}_{h}")
            if KDBG:
                dma("gpsimd", dbg_och[ci], och[:], r_och, [b_xout], f"doch{ci}")
            P.add("vector", lambda e: e.tensor_mul(och[:], och[:], sg[:]),
                  reads=r_sg + r_och, writes=r_och, tag=f"om{ci}")
            for q4 in range(4):
                P.add("scalar", lambda e, q4=q4: e.square(
                    sg[:, q4 * 512:(q4 + 1) * 512], och[:, q4 * 512:(q4 + 1) * 512]),
                    reads=r_och, writes=r_sg, tag=f"rs{q4}{ci}")
            P.add("vector", lambda e: e.tensor_reduce(
                gacc[:, 0:16], sg[:].rearrange("p (h c) -> p h c", h=16), AX, ALU.add),
                reads=r_sg, writes=[b_gacc], tag=f"rr{ci}")
            P.add("scalar", lambda e: e.activation(gacc[:, 16:32], gacc[:, 0:16],
                                                   AF.Sqrt, scale=1.0 / HD, bias=EPS_RMS),
                  reads=[b_gacc, b_const], writes=[b_gacc], tag=f"rq{ci}")
            P.add("vector", lambda e: e.reciprocal(gacc[:, 16:32], gacc[:, 16:32]),
                  reads=[b_gacc], writes=[b_gacc], tag=f"rc{ci}")
            for h in range(H):
                P.add("scalar", lambda e, h=h: e.activation(
                    sg[:, h * 128:(h + 1) * 128], och[:, h * 128:(h + 1) * 128],
                    AF.Copy, scale=gacc[:, 16 + h:17 + h]),
                    reads=r_och + [b_gacc], writes=r_sg, tag=f"on{ci}_{h}")
                bkt = 2 + (h % 2)
                P.add("tensor", lambda e, h=h, bkt=bkt: e.transpose(
                    banks[bkt][:, 0:128], sg[:, h * 128:(h + 1) * 128], identt[:]),
                    reads=r_sg + [b_const], writes=[b_bank[bkt]], tag=f"ot{ci}_{h}")
                P.add("scalar", lambda e, h=h, cl=cl, bkt=bkt: e.copy(
                    ogbh[:, h * 512 + cl * 128:h * 512 + cl * 128 + 128],
                    banks[bkt][:, 0:128]),
                    reads=[b_bank[bkt]], writes=r_ogf, tag=f"oh{ci}_{h}")
                P.add("vector", lambda e, h=h, cl=cl, bkt=bkt: e.tensor_sub(
                    ogbl[:, h * 512 + cl * 128:h * 512 + cl * 128 + 128],
                    banks[bkt][:, 0:128],
                    ogbh[:, h * 512 + cl * 128:h * 512 + cl * 128 + 128]),
                    reads=[b_bank[bkt]] + r_ogf, writes=r_ogf, tag=f"ol{ci}_{h}")

        for eg in range(8):
            for dc in range(NDC):
                dma("sync", ptmp[:, 0:256],
                    wo_in[dc * 128:(dc + 1) * 128, eg * 256:(eg + 1) * 256],
                    r_pt, r_pt, f"wol{hf}_{eg}_{dc}")
                P.add("scalar", lambda e, dc=dc: e.activation(
                    woF[:, dc * 256:(dc + 1) * 256], ptmp[:, 0:256], AF.Copy,
                    scale=nwc[:, dc:dc + 1]),
                    reads=r_pt + [b_nw], writes=r_wo, tag=f"woh{hf}_{eg}_{dc}")
            P.add("scalar", lambda e: e.copy(wobh[:], woF[:]),
                  reads=r_wo, writes=r_wob, tag=f"wbh{hf}_{eg}")
            P.add("vector", lambda e: e.tensor_sub(wobl[:], woF[:], wobh[:]),
                  reads=r_wo + r_wob, writes=r_wob, tag=f"wbl{hf}_{eg}")
            if KDBG and hf == 0 and eg == 0:
                dma("gpsimd", dbg_f[0][:, 0:2048], woF[:, 0:2048], r_wo, [b_xout], "dwoF0")
                dma("gpsimd", dbg_f[1][:, 0:2048], woF[:, 2048:4096], r_wo, [b_xout], "dwoF1")
            for es in range(2):
                e_abs = eg * 2 + es
                pW, bW = banks[es], b_bank[es]
                for dc in range(NDC):
                    ogh_s = ogbh[:, dc * 512:dc * 512 + 512]
                    ogl_s = ogbl[:, dc * 512:dc * 512 + 512]
                    wh_s = wobh[:, dc * 256 + es * 128:dc * 256 + es * 128 + 128]
                    wl_s = wobl[:, dc * 256 + es * 128:dc * 256 + es * 128 + 128]
                    mm(pW[:], wh_s, ogh_s, dc == 0, False, r_wob + r_ogf, bW, "wo")
                    mm(pW[:], wl_s, ogh_s, False, False, r_wob + r_ogf, bW, "wo")
                    mm(pW[:], wh_s, ogl_s, False, dc == NDC - 1, r_wob + r_ogf, bW, "wo")
                dma("sync", xev[:], x_dram[e_abs][:, hf * 512:hf * 512 + 512],
                    [b_xev, b_xdram], [b_xev], f"axl{e_abs}_{hf}")
                P.add("vector", lambda e, pW=pW: e.tensor_add(xnew[:], pW[:], xev[:]),
                      reads=[bW, b_xev], writes=[b_xnew], tag=f"aev{e_abs}_{hf}")
                dma("gpsimd", x_dram[e_abs][:, hf * 512:hf * 512 + 512], xnew[:],
                    [b_xnew], [b_xdram], f"axs{e_abs}_{hf}")
                first = (e_abs == 0)
                last = (e_abs == NDC - 1)
                P.add("scalar", lambda e: e.square(sq_sb[:], xnew[:]),
                      reads=[b_xnew], writes=[b_sq], tag=f"asq{e_abs}_{hf}")
                mm(banks[4 + hf][0:1, :], onec[:], xnew[:], first, last,
                   [b_xnew, b_const], b_bank[4 + hf], f"as{e_abs}_{hf}")
                mm(banks[6 + hf][0:1, :], onec[:], sq_sb[:], first, last,
                   [b_sq, b_const], b_bank[6 + hf], f"az{e_abs}_{hf}")

    if KDBG:
        for dc in range(NDC):
            dma("gpsimd", dbg_xb[dc], x_dram[dc], [b_xdram], [b_xout], f"dxb{dc}")
    emit_mr("p3")
    emit_xsplit(x_dram, "p3", b_xdram)
    emit_ffn(x_dram, x_out, False, "2", b_xdram, b_xout)

    P.add("sync", lambda e: e.nop(), reads=[b_xout], writes=[], tag="final")

    _LAST_P[0] = P
    P.build(nc)
    with nc.Block() as block:
        P.emit(block)
    return nc


_NC_CACHE = [None]
_LAST_RES = [None]
_LAST_P = [None]


def kernel(L0_attn_x, L0_attn_log_decay, L0_attn_Wq, L0_attn_Wk, L0_attn_Wv,
           L0_attn_Wo, L0_attn_Wg1, L0_attn_Wg2, L0_attn_norm_w,
           L0_ffn_w1, L0_ffn_w2, L0_ffn_w3):
    x = np.asarray(L0_attn_x, dtype=np.float32)
    w1 = np.ascontiguousarray(np.asarray(L0_ffn_w1, np.float32))
    w2 = np.ascontiguousarray(np.asarray(L0_ffn_w2, np.float32))
    w3 = np.ascontiguousarray(np.asarray(L0_ffn_w3, np.float32))
    shared = {
        "ld_in": np.ascontiguousarray(np.asarray(L0_attn_log_decay, np.float32)[None, :]),
        "wq_in": np.ascontiguousarray(np.asarray(L0_attn_Wq, np.float32)),
        "wk_in": np.ascontiguousarray(np.asarray(L0_attn_Wk, np.float32)),
        "wv_in": np.ascontiguousarray(np.asarray(L0_attn_Wv, np.float32)),
        "wo_in": np.ascontiguousarray(np.asarray(L0_attn_Wo, np.float32)),
        "wg1_in": np.ascontiguousarray(np.asarray(L0_attn_Wg1, np.float32)),
        "wg2_in": np.ascontiguousarray(np.asarray(L0_attn_Wg2, np.float32)),
        "nw_in": np.ascontiguousarray(
            np.asarray(L0_attn_norm_w, np.float32).reshape(NDC, 128).T),
        "w1_in": w1, "w2_in": w2, "w3_in": w3,
    }
    if _NC_CACHE[0] is None:
        _NC_CACHE[0] = build_nc()
    nc = _NC_CACHE[0]
    in_maps = []
    for c in range(NCORES):
        xs = np.ascontiguousarray(
            x[0, c * NT:(c + 1) * NT, :].T.reshape(NDC, 128, NT))
        m = dict(shared)
        m["x_in"] = xs
        m["w1g_in"] = np.ascontiguousarray(w1[c * 704:(c + 1) * 704])
        m["w2g_in"] = np.ascontiguousarray(w2[c * 704:(c + 1) * 704])
        m["w3g_in"] = np.ascontiguousarray(w3[:, c * 704:(c + 1) * 704])
        m["cid_in"] = np.array([[float(c)]], dtype=np.float32)
        in_maps.append(m)
    res = run_bass_kernel_spmd(nc, in_maps, core_ids=list(range(NCORES)))
    _LAST_RES[0] = res
    out = np.empty((1, NCORES * NT, D), dtype=np.float32)
    for c in range(NCORES):
        o = res.results[c]["x_out"].reshape(D, NT)
        out[0, c * NT:(c + 1) * NT, :] = o.T
    return out


# version 19
# speedup vs baseline: 1.0191x; 1.0191x over previous
import os
import numpy as np
import ml_dtypes
import concourse.bass as bass
import concourse.mybir as mybir
from concourse.bass_utils import run_bass_kernel_spmd

F32 = mybir.dt.float32
BF16 = mybir.dt.bfloat16
AF = mybir.ActivationFunctionType
ALU = mybir.AluOpType
AX = mybir.AxisListType.X

NCORES = 8
NT = 1024
D = 2048
H = 16
HD = 128
HID = 5632
NHS = 44
NDC = 16
NCH = 8
GCNT = float(HID * D)
EPS_LN = 1e-5
EPS_RMS = 1e-5
EPS_Q = 1e-8


class Buf:
    __slots__ = ("name", "last_writer", "readers")

    def __init__(self, name):
        self.name = name
        self.last_writer = None
        self.readers = []


class Prog:
    def __init__(self):
        self.ops = []

    def add(self, eng, fn, reads=(), writes=(), dma=False, cc=False, tag=""):
        ch = "cc" if cc else (eng + "_dma" if dma else eng)
        raw = set()
        war = set()
        for b in reads:
            if b.last_writer is not None:
                raw.add(b.last_writer)
        for b in writes:
            if b.last_writer is not None:
                war.add(b.last_writer)
            war.update(b.readers)
        if not hasattr(self, "chpos"):
            self.chpos = {}
        pos = self.chpos.get(ch, 0)
        self.chpos[ch] = pos + 1
        op = {"id": len(self.ops), "eng": eng, "ch": ch, "fn": fn, "dma": dma,
              "cc": cc, "raw": raw, "war": war, "tag": tag, "inc": False,
              "val": 0, "pos": pos}
        self.ops.append(op)
        oid = op["id"]
        for b in reads:
            b.readers.append(oid)
        for b in writes:
            b.last_writer = oid
            b.readers = []
        return oid

    def _needs_edge(self, op, dop, is_raw):
        if dop["ch"] != op["ch"] or dop["dma"] or dop["cc"]:
            return True
        if dop["ch"] == "tensor":
            return False
        return is_raw and (op["pos"] - dop["pos"] < int(os.environ.get("RAWD", "6")))

    def build(self, nc):
        ops = self.ops
        if os.environ.get("DMASER", "0") == "1":
            last_dma = {}
            for op in ops:
                if op["dma"]:
                    if op["ch"] in last_dma:
                        op["raw"].add(last_dma[op["ch"]])
                    last_dma[op["ch"]] = op["id"]
        NSLOT = 8
        for op in ops:
            if op["dma"]:
                op["inc"] = True
            for d in op["raw"]:
                if self._needs_edge(op, ops[d], True):
                    ops[d]["inc"] = True
            for d in op["war"]:
                if self._needs_edge(op, ops[d], False):
                    ops[d]["inc"] = True
        counters = {}
        slot_use = {}
        for op in ops:
            if op["dma"]:
                ch = op["ch"]
                idx = slot_use.setdefault(ch, 0)
                slot_use[ch] = idx + 1
                slot = idx % NSLOT
                op["sem"] = f"{ch}{slot}"
                op["val"] = 16 * (idx // NSLOT + 1)
                op["prewait"] = (op["sem"], op["val"] - 16) if op["val"] > 16 else None
            elif op["inc"]:
                ch = op["ch"]
                counters[ch] = counters.get(ch, 0) + 1
                op["sem"] = ch
                op["val"] = counters[ch]
        for op in ops:
            waits = {}
            if op.get("prewait"):
                sm, v = op["prewait"]
                if v > 0:
                    waits[sm] = max(waits.get(sm, 0), v)
            for d, isr in [(d, True) for d in op["raw"]] + [(d, False) for d in op["war"]]:
                dop = ops[d]
                if not self._needs_edge(op, dop, isr):
                    continue
                waits[dop["sem"]] = max(waits.get(dop["sem"], 0), dop["val"])
            op["waits"] = waits
        sem_names = set()
        for op in ops:
            if op["inc"]:
                sem_names.add(op["sem"])
            sem_names.update(op["waits"].keys())
        self.sems = {n: nc.alloc_semaphore("sem_" + n) for n in sorted(sem_names)}

    def emit(self, block):
        sems = self.sems

        def body_for(engine_name):
            def body(eng):
                hw = {}
                for op in self.ops:
                    if op["eng"] != engine_name:
                        continue
                    for ch, v in sorted(op["waits"].items()):
                        if hw.get(ch, 0) < v:
                            eng.wait_ge(sems[ch], v)
                            hw[ch] = v
                    inst = op["fn"](eng)
                    if op["inc"]:
                        assert inst is not None, op["tag"]
                        inst.then_inc(sems[op["sem"]], 16 if op["dma"] else 1)
            return body

        block.tensor(body_for("tensor"))
        block.vector(body_for("vector"))
        block.scalar(body_for("scalar"))
        block.gpsimd(body_for("gpsimd"))
        block.sync(body_for("sync"))


def build_nc():
    nc = bass.Bass("TRN2", target_bir_lowering=False,
                   detect_race_conditions=False)
    P = Prog()

    x_in = nc.dram_tensor("x_in", [NDC, 128, NT], F32, kind="ExternalInput").ap()
    ld_in = nc.dram_tensor("ld_in", [1, H], F32, kind="ExternalInput").ap()
    wq_in = nc.dram_tensor("wq_in", [D, D], F32, kind="ExternalInput").ap()
    wk_in = nc.dram_tensor("wk_in", [D, D], F32, kind="ExternalInput").ap()
    wv_in = nc.dram_tensor("wv_in", [D, D], F32, kind="ExternalInput").ap()
    wo_in = nc.dram_tensor("wo_in", [D, D], F32, kind="ExternalInput").ap()
    wg1_in = nc.dram_tensor("wg1_in", [D, HD], F32, kind="ExternalInput").ap()
    wg2_in = nc.dram_tensor("wg2_in", [HD, D], F32, kind="ExternalInput").ap()
    nw_in = nc.dram_tensor("nw_in", [128, NDC], F32, kind="ExternalInput").ap()
    w1_in = nc.dram_tensor("w1_in", [HID, D], F32, kind="ExternalInput").ap()
    w2_in = nc.dram_tensor("w2_in", [HID, D], F32, kind="ExternalInput").ap()
    w3_in = nc.dram_tensor("w3_in", [D, HID], F32, kind="ExternalInput").ap()
    w1g_in = nc.dram_tensor("w1g_in", [HID // 8, D], F32, kind="ExternalInput").ap()
    w2g_in = nc.dram_tensor("w2g_in", [HID // 8, D], F32, kind="ExternalInput").ap()
    w3g_in = nc.dram_tensor("w3g_in", [D, HID // 8], F32, kind="ExternalInput").ap()
    cid_in = nc.dram_tensor("cid_in", [1, 1], F32, kind="ExternalInput").ap()
    x_out = nc.dram_tensor("x_out", [NDC, 128, NT], F32, kind="ExternalOutput").ap()
    KDBG = os.environ.get("KDBG", "") == "1"
    if KDBG:
        dbg_xa = nc.dram_tensor("dbg_xa", [NDC, 128, NT], F32, kind="ExternalOutput").ap()
        dbg_xb = nc.dram_tensor("dbg_xb", [NDC, 128, NT], F32, kind="ExternalOutput").ap()
        dbg_osp = nc.dram_tensor("dbg_osp", [NCH, 128, D], F32, kind="ExternalOutput").ap()
        dbg_qsp = nc.dram_tensor("dbg_qsp", [NCH, H, 128, 128], F32, kind="ExternalOutput").ap()
        dbg_B = nc.dram_tensor("dbg_B", [NCORES, 128, D], F32, kind="ExternalOutput").ap()
        dbg_thr = nc.dram_tensor("dbg_thr", [128, 8], F32, kind="ExternalOutput").ap()
        dbg_xs = nc.dram_tensor("dbg_xs", [2, 128, NDC * NT], BF16, kind="ExternalOutput").ap()
        dbg_gacc = nc.dram_tensor("dbg_gacc", [128, 48], F32, kind="ExternalOutput").ap()
        dbg_Sin = nc.dram_tensor("dbg_Sin", [128, D], F32, kind="ExternalOutput").ap()
        dbg_f = nc.dram_tensor("dbg_f", [4, 128, D], F32, kind="ExternalOutput").ap()
        dbg_och = nc.dram_tensor("dbg_och", [NCH, 128, D], F32, kind="ExternalOutput").ap()
        dbg_grow = nc.dram_tensor("dbg_grow", [1, 8], F32, kind="ExternalOutput").ap()

    _i = np.arange(128, dtype=np.float32)
    diff = _i[None, :] - _i[:, None]
    tri = (diff >= 0).astype(np.float32)
    diffm = diff * tri
    c_ident = nc.inline_tensor(np.eye(128, dtype=np.float32), "c_ident").ap()
    c_identb = nc.inline_tensor(np.eye(128, dtype=np.float32).astype(ml_dtypes.bfloat16),
                                "c_identb").ap()
    c_onec = nc.inline_tensor(np.ones((128, 1), np.float32), "c_onec").ap()
    c_oner = nc.inline_tensor(np.ones((1, 128), np.float32), "c_oner").ap()
    c_diffm = nc.inline_tensor(diffm, "c_diffm").ap()
    c_tri = nc.inline_tensor(tri, "c_tri").ap()
    c_iop1 = nc.inline_tensor((_i + 1.0)[:, None], "c_iop1").ap()
    c_iorev = nc.inline_tensor((127.0 - _i)[:, None], "c_iorev").ap()
    c_ioci = nc.inline_tensor(
        _i[:, None] + 1.0 + 128.0 * np.arange(8, dtype=np.float32)[None, :],
        "c_ioci").ap()
    c_128 = nc.inline_tensor(np.full((128, 1), 128.0, np.float32), "c_128").ap()
    sel = np.zeros((128, 16), np.float32)
    for j in range(4):
        sel[:, 4 * j + j] = 1.0
    c_sel4 = nc.inline_tensor(sel, "c_sel4").ap()
    FCONST_VALS = [1.0 / D, -1.0, EPS_LN, 0.5 / GCNT, 0.5 * EPS_Q, 1.0 / HD,
                   1.0, 0.0]
    c_fc = nc.inline_tensor(
        np.tile(np.asarray(FCONST_VALS, np.float32)[None, :], (128, 1)),
        "c_fc").ap()

    q1t_sp = nc.dram_tensor("q1t_sp", [NHS, NDC, 128, 128], BF16).ap()
    q2t_sp = nc.dram_tensor("q2t_sp", [NHS, NDC, 128, 128], BF16).ap()
    q3t_sp = nc.dram_tensor("q3t_sp", [NHS, NDC, 128, 128], BF16).ap()
    x_dram = nc.dram_tensor("x_dram", [NDC, 128, NT], F32).ap()
    o_sp = nc.dram_tensor("o_sp", [NCH, 128, D], F32).ap()
    q_sp = nc.dram_tensor("q_sp", [NCH, H, 128, 128], F32).ap()
    cc_gin = nc.dram_tensor("cc_gin", [1, 4], F32).ap()
    cc_gout = nc.dram_tensor("cc_gout", [1, 4], F32).ap()
    cc_bin = nc.dram_tensor("cc_bin", [128, D], F32).ap()
    cc_bout = nc.dram_tensor("cc_bout", [NCORES, 128, D], F32).ap()

    sb = lambda n, s, d=F32: nc.alloc_sbuf_tensor(n, s, d)
    xs_h = sb("xs_h", [128, NDC * NT], BF16)
    xs_l = sb("xs_l", [128, NDC * NT], BF16)
    hidp = sb("hidp", [128, 22528])
    strm = sb("strm", [128, 12288], BF16)
    S_all = sb("S_all", [128, D])
    g1h = sb("g1h", [128, NT], BF16)
    g1l = sb("g1l", [128, NT], BF16)
    Mb = sb("Mb", [128, NT])
    Rb = sb("Rb", [128, NT])
    xt_sb = sb("xt_sb", [128, 512])
    xev = sb("xev", [128, 512])
    xnew = sb("xnew", [128, 512])
    sq_sb = sb("sq_sb", [128, 512])
    gacc = sb("gacc", [128, 48])
    identt = sb("identt", [128, 128])
    identb = sb("identb", [128, 128], BF16)
    onec = sb("onec", [128, 1])
    oner = sb("oner", [1, 128])
    iop1 = sb("iop1", [128, 1])
    iorev = sb("iorev", [128, 1])
    ioci = sb("ioci", [128, 8])
    c128t = sb("c128t", [128, 1])
    fconst = sb("fconst", [128, 8])
    ld_sb = sb("ld_sb", [1, 16])
    ldb = sb("ldb", [128, 16])
    qdcd = sb("qdcd", [128, H * 8])
    kdc = sb("kdc", [128, H])
    cdc = sb("cdc", [128, H])
    nwc = sb("nwc", [128, NDC])
    thr = sb("thr", [128, 8])
    cidc = sb("cidc", [128, 1])
    tjc = sb("tjc", [128, 4])
    grow = sb("grow", [1, 8])

    banks = [nc.alloc_psum_tensor(f"bank{i}", [128, 512], F32) for i in range(8)]
    for _j, _v in enumerate(FCONST_VALS):
        nc.const_aps.register(F32, float(_v), fconst[:, _j:_j + 1])

    b_xs = Buf("xs")
    b_h = [Buf(f"h{i}") for i in range(22)]
    b_s = [Buf(f"s{i}") for i in range(6)]
    b_S = Buf("S")
    b_g1 = Buf("g1")
    b_const = Buf("const")
    b_decay = Buf("decay")
    b_MR = Buf("MR")
    b_grow = Buf("grow")
    b_thr = Buf("thr")
    b_bank = [Buf(f"bank{i}") for i in range(8)]
    b_xin = Buf("xin")
    b_xdram = Buf("xdram")
    b_xout = Buf("xout")
    b_qsp = [[Buf(f"q{w}sp{h}") for h in range(NHS)] for w in range(3)]
    b_osp = Buf("osp")
    b_qspill = Buf("qspill")
    b_ccg = Buf("ccg")
    b_ccb = Buf("ccb")
    b_xt = Buf("xt")
    b_xev = Buf("xev")
    b_xnew = Buf("xnew")
    b_sq = Buf("sq")
    b_gacc = Buf("gacc")
    b_nw = Buf("nw")
    b_cid = Buf("cid")

    def hreg(lo_f32, n_f32):
        i0 = lo_f32 // 1024
        i1 = (lo_f32 + n_f32 + 1023) // 1024
        return [b_h[i] for i in range(i0, i1)]

    def sreg(lo_b, n_b):
        i0 = lo_b // 2048
        i1 = (lo_b + n_b + 2047) // 2048
        return [b_s[i] for i in range(i0, i1)]

    def dma(q, out, in_, reads, writes, tag=""):
        return P.add(q, lambda e, o=out, i=in_: e.dma_start(out=o, in_=i),
                     reads=reads, writes=writes, dma=True, tag=tag)

    def dma_tr(out, in_, reads, writes, tag=""):
        return P.add("scalar", lambda e, o=out, i=in_: e.dma_start(out=o, in_=i, transpose=True),
                     reads=reads, writes=writes, dma=True, tag=tag)

    def mm(ps, lhsT, rhs, st, sp, reads, wbank, tag=""):
        return P.add("tensor", lambda e, p=ps, l=lhsT, r=rhs, a=st, b=sp:
                     e.matmul(p, l, r, start=a, stop=b),
                     reads=reads, writes=[wbank], tag=tag)

    for dst, src, t in [(identt[:], c_ident, "id"), (onec[:], c_onec, "oc"),
                        (oner[:], c_oner, "or"), (iop1[:], c_iop1, "i1"),
                        (iorev[:], c_iorev, "ir"), (ioci[:], c_ioci, "ic"),
                        (c128t[:], c_128, "c128"),
                        (fconst[:], c_fc, "fc"), (identb[:], c_identb, "idb"),
                        (ld_sb[:], ld_in, "ld")]:
        dma("sync", dst, src, [], [b_const], t)
    dma("sync", nwc[:], nw_in, [], [b_nw], "nw")
    dma("sync", grow[0:1, 0:1], cid_in, [], [b_grow], "cid0")
    P.add("tensor", lambda e: e.matmul(banks[6][:, 0:1], oner[:], grow[0:1, 0:1],
                                       start=True, stop=True),
          reads=[b_grow, b_const], writes=[b_bank[6]], tag="cidbc")
    P.add("scalar", lambda e: e.copy(cidc[:], banks[6][:, 0:1]),
          reads=[b_bank[6]], writes=[b_cid], tag="cidcp")

    gstage = hidp[:, 0:2048]
    r_gst = hreg(0, 2048)

    def emit_gamma(w_ap, idx):
        cbase = idx * 12
        t = w_ap.rearrange("(a p) d -> a p d", p=64)
        for a in range(11):
            dma("sync", gstage[0:64, 0:D], t[a], r_gst, r_gst, f"gld{idx}_{a}")
            P.add("vector", lambda e, a=a: e.tensor_reduce(
                gacc[0:64, cbase + a:cbase + a + 1], gstage[0:64, 0:D], AX,
                ALU.add, apply_absolute_value=True),
                reads=r_gst, writes=[b_gacc], tag=f"gr{idx}_{a}")
        P.add("vector", lambda e: e.tensor_reduce(
            gacc[0:64, cbase + 11:cbase + 12], gacc[0:64, cbase:cbase + 11],
            AX, ALU.add), reads=[b_gacc], writes=[b_gacc], tag=f"gs{idx}")
        mm(banks[4 + idx][0:1, 0:1], onec[0:64, 0:1],
           gacc[0:64, cbase + 11:cbase + 12], True, True,
           [b_gacc, b_const], b_bank[4 + idx], f"gm{idx}")

    emit_gamma(w1g_in, 0)
    emit_gamma(w2g_in, 1)
    t3 = w3g_in.rearrange("(a p) h -> a p h", p=128)
    for a in range(16):
        dma("sync", gstage[:, 0:704], t3[a], r_gst, r_gst, f"gld2_{a}")
        P.add("vector", lambda e, a=a: e.tensor_reduce(
            gacc[:, 32 + a:33 + a], gstage[:, 0:704], AX, ALU.add,
            apply_absolute_value=True),
            reads=r_gst, writes=[b_gacc], tag=f"gr2_{a}")
    P.add("vector", lambda e: e.tensor_reduce(
        gacc[:, 30:31], gacc[:, 32:48], AX, ALU.add),
        reads=[b_gacc], writes=[b_gacc], tag="gs2a")
    mm(banks[6][0:1, 0:1], onec[:], gacc[:, 30:31], True, True,
       [b_gacc, b_const], b_bank[6], "gm2")
    for j in range(3):
        P.add("scalar", lambda e, j=j: e.copy(grow[0:1, j:j + 1], banks[4 + j][0:1, 0:1]),
              reads=[b_bank[4 + j]], writes=[b_grow], tag=f"gc{j}")
    P.add("scalar", lambda e: e.memzero(grow[0:1, 3:4]),
          reads=[], writes=[b_grow], tag="gcz")
    if KDBG:
        dma("gpsimd", dbg_gacc, gacc[:], [b_gacc], [b_xout], "dgacc")
        dma("gpsimd", dbg_grow, grow[:], [b_grow], [b_xout], "dgrow")
    dma("gpsimd", cc_gin, grow[0:1, 0:4], [b_grow], [b_ccg], "gw")
    P.add("gpsimd", lambda e: e.collective_compute(
        "AllReduce", ALU.add, replica_groups=[list(range(NCORES))],
        ins=[cc_gin], outs=[cc_gout]),
        reads=[b_ccg], writes=[b_ccg], cc=True, tag="gcc")
    dma("gpsimd", grow[0:1, 4:8], cc_gout, [b_ccg], [b_grow], "gr")
    P.add("scalar", lambda e: e.activation(grow[0:1, 0:3], grow[0:1, 4:7], AF.Copy,
                                           scale=0.5 / GCNT, bias=0.5 * EPS_Q),
          reads=[b_grow, b_const], writes=[b_grow], tag="gt")
    P.add("scalar", lambda e: e.mul(grow[0:1, 4:7], grow[0:1, 0:3], -1.0),
          reads=[b_grow, b_const], writes=[b_grow], tag="gtn")
    P.add("tensor", lambda e: e.matmul(banks[6][:, 0:8], oner[:], grow[0:1, 0:8],
                                       start=True, stop=True),
          reads=[b_grow, b_const], writes=[b_bank[6]], tag="gbc")
    P.add("scalar", lambda e: e.copy(thr[:], banks[6][:, 0:8]),
          reads=[b_bank[6]], writes=[b_thr], tag="gthr")

    def emit_quant_hs(hs):
        qstg = [(Mb[:, :], [b_MR]), (Rb[:, :], [b_MR])]
        qnat_s = [(S_all[:, 0:1024], [b_S]),
                  (S_all[:, 1024:2048], [b_S])]
        qtr_s = [(g1h[:, :], [b_g1]), (g1l[:, :], [b_g1])]
        w3v = w3_in.rearrange("(t p) h -> p t h", p=128)
        if True:
            for widx, (w_ap, sp_ap) in enumerate(
                    ((w1_in, q1t_sp), (w2_in, q2t_sp), (w3_in, q3t_sp))):
                for half in range(2):
                    sl = (hs * 6 + widx * 2 + half) % 2
                    stg, r_sl = qstg[sl]
                    qn, r_qn = qnat_s[sl]
                    qt, r_qt = qtr_s[sl]
                    if widx < 2:
                        dma("sync", stg, w_ap[hs * 128:(hs + 1) * 128,
                                              half * 1024:(half + 1) * 1024],
                            r_sl, r_sl, f"qld{widx}_{hs}_{half}")
                    else:
                        dma("sync", stg.rearrange("p (t h) -> p t h", t=8),
                            w3v[:, half * 8:(half + 1) * 8, hs * 128:(hs + 1) * 128],
                            r_sl, r_sl, f"qld{widx}_{hs}_{half}")
                    P.add("vector", lambda e, qn=qn, stg=stg, widx=widx:
                          e.tensor_single_scalar(qn, stg, thr[:, 4 + widx:5 + widx],
                                                 ALU.is_le),
                          reads=r_sl + [b_thr] + r_qn, writes=r_qn, tag=f"qa{widx}_{hs}{half}")
                    P.add("vector", lambda e, qn=qn, stg=stg, widx=widx:
                          e.scalar_tensor_tensor(qn, stg, thr[:, widx:widx + 1],
                                                 qn, ALU.is_ge, ALU.subtract),
                          reads=r_sl + [b_thr] + r_qn, writes=r_qn, tag=f"qb{widx}_{hs}{half}")
                    for t in range(8):
                        bki = 5 if t % 2 == 0 else 7
                        P.add("tensor", lambda e, qn=qn, t=t, bki=bki: e.transpose(
                            banks[bki][:, 0:128], qn[:, t * 128:(t + 1) * 128], identt[:]),
                            reads=r_qn + [b_const], writes=[b_bank[bki]],
                            tag=f"qT{widx}_{hs}_{half}_{t}")
                        P.add("scalar", lambda e, qt=qt, t=t, bki=bki: e.copy(
                            qt[:, t * 128:(t + 1) * 128], banks[bki][:, 0:128]),
                            reads=[b_bank[bki]], writes=r_qt,
                            tag=f"qE{widx}_{hs}_{half}_{t}")
                    dma("gpsimd", sp_ap[hs, half * 8:(half + 1) * 8].rearrange("t p h -> p t h"),
                        qt.rearrange("p (t h) -> p t h", t=8),
                        r_qt, [b_qsp[widx][hs]], f"qs{widx}_{hs}_{half}")

    def emit_stats_standalone(src):
        for dc in range(NDC):
            st, sp = dc == 0, dc == NDC - 1
            for hf in range(2):
                dma("sync", xt_sb[:], src[dc][:, hf * 512:(hf + 1) * 512],
                    [b_xt], [b_xt], f"sld{dc}_{hf}")
                P.add("scalar", lambda e: e.square(sq_sb[:], xt_sb[:]),
                      reads=[b_xt], writes=[b_sq], tag=f"sq{dc}_{hf}")
                mm(banks[4 + hf][0:1, :], onec[:], xt_sb[:], st, sp,
                   [b_xt, b_const], b_bank[4 + hf], f"sm{dc}_{hf}")
                mm(banks[6 + hf][0:1, :], onec[:], sq_sb[:], st, sp,
                   [b_sq, b_const], b_bank[6 + hf], f"sz{dc}_{hf}")

    rowM = Mb[0:1, :]
    rowR = Rb[0:1, :]

    def emit_mr(tag):
        for hf in range(2):
            P.add("scalar", lambda e, hf=hf: e.activation(
                Mb[0:1, hf * 512:(hf + 1) * 512], banks[4 + hf][0:1, :],
                AF.Copy, scale=1.0 / D),
                reads=[b_bank[4 + hf], b_const], writes=[b_MR], tag=f"m{tag}{hf}")
            P.add("scalar", lambda e, hf=hf: e.activation(
                Rb[0:1, hf * 512:(hf + 1) * 512], banks[6 + hf][0:1, :],
                AF.Copy, scale=1.0 / D),
                reads=[b_bank[6 + hf], b_const], writes=[b_MR], tag=f"e{tag}{hf}")
        for hf in range(2):
            P.add("tensor", lambda e, hf=hf: e.matmul(
                banks[0][:, :], oner[:], Mb[0:1, hf * 512:(hf + 1) * 512],
                start=True, stop=True),
                reads=[b_MR, b_const], writes=[b_bank[0]], tag=f"mb{tag}{hf}")
            P.add("scalar", lambda e, hf=hf: e.copy(
                Mb[:, hf * 512:(hf + 1) * 512], banks[0][:, :]),
                reads=[b_bank[0]], writes=[b_MR], tag=f"mc{tag}{hf}")
            P.add("tensor", lambda e, hf=hf: e.matmul(
                banks[1][:, :], oner[:], Rb[0:1, hf * 512:(hf + 1) * 512],
                start=True, stop=True),
                reads=[b_MR, b_const], writes=[b_bank[1]], tag=f"rb{tag}{hf}")
            P.add("scalar", lambda e, hf=hf: e.copy(
                Rb[:, hf * 512:(hf + 1) * 512], banks[1][:, :]),
                reads=[b_bank[1]], writes=[b_MR], tag=f"rc{tag}{hf}")
        for hf in range(2):
            P.add("vector", lambda e, hf=hf: e.tensor_mul(
                sq_sb[:], Mb[:, hf * 512:(hf + 1) * 512], Mb[:, hf * 512:(hf + 1) * 512]),
                reads=[b_MR], writes=[b_sq], tag=f"v1{tag}{hf}")
            P.add("vector", lambda e, hf=hf: e.tensor_sub(
                Rb[:, hf * 512:(hf + 1) * 512], Rb[:, hf * 512:(hf + 1) * 512], sq_sb[:]),
                reads=[b_MR, b_sq], writes=[b_MR], tag=f"v2{tag}{hf}")
            P.add("scalar", lambda e, hf=hf: e.activation(
                Rb[:, hf * 512:(hf + 1) * 512], Rb[:, hf * 512:(hf + 1) * 512],
                AF.Sqrt, bias=EPS_LN),
                reads=[b_MR, b_const], writes=[b_MR], tag=f"v3{tag}{hf}")
            P.add("vector", lambda e, hf=hf: e.reciprocal(
                Rb[:, hf * 512:(hf + 1) * 512], Rb[:, hf * 512:(hf + 1) * 512]),
                reads=[b_MR], writes=[b_MR], tag=f"v4{tag}{hf}")

    def emit_xsplit(src, tag, src_buf):
        for dc in range(NDC):
            for hf in range(2):
                o = dc * NT + hf * 512
                par = (dc * 2 + hf) % 2
                buf = xt_sb if par == 0 else sq_sb
                bb = b_xt if par == 0 else b_sq
                dma("sync", buf[:], src[dc][:, hf * 512:(hf + 1) * 512],
                    [bb, src_buf], [bb], f"x{tag}l{dc}_{hf}")
                P.add("vector", lambda e, hf=hf, buf=buf: e.tensor_sub(
                    buf[:], buf[:], Mb[:, hf * 512:(hf + 1) * 512]),
                    reads=[b_MR, bb], writes=[bb], tag=f"x{tag}s{dc}_{hf}")
                P.add("vector", lambda e, hf=hf, buf=buf: e.tensor_mul(
                    buf[:], buf[:], Rb[:, hf * 512:(hf + 1) * 512]),
                    reads=[b_MR, bb], writes=[bb], tag=f"x{tag}m{dc}_{hf}")
                P.add("scalar", lambda e, o=o, buf=buf: e.copy(xs_h[:, o:o + 512], buf[:]),
                      reads=[bb], writes=[b_xs], tag=f"x{tag}h{dc}_{hf}")
                P.add("vector", lambda e, o=o, buf=buf: e.tensor_sub(
                    xs_l[:, o:o + 512], buf[:], xs_h[:, o:o + 512]),
                    reads=[bb, b_xs], writes=[b_xs], tag=f"x{tag}L{dc}_{hf}")

    hid_h = hidp[:, 0:11264].bitcast(BF16)
    hid_l = hidp[:, 11264:22528].bitcast(BF16)
    r_hidh = hreg(0, 11264)
    r_hidl = hreg(11264, 11264)

    b_q3s = [Buf(f"q3s{i}") for i in range(4)]

    def emit_ffn(add_src, dst, fuse_stats, tag, src_buf, dst_buf, zip_cb=None):
        for tile in range(2):
            c0 = tile * 512
            for hs in range(NHS):
                if zip_cb is not None and tile == 0:
                    zip_cb(hs)
                slot = hs % 2
                so = slot * 4096
                q1b = strm[:, so:so + 2048]
                q2b = strm[:, so + 2048:so + 4096]
                rs = sreg(so, 4096)
                dma("sync", q1b.rearrange("p (t h) -> p t h", t=NDC),
                    q1t_sp[hs].rearrange("t p h -> p t h"),
                    [b_qsp[0][hs]] + rs, rs, f"f{tag}q1_{tile}_{hs}")
                dma("sync", q2b.rearrange("p (t h) -> p t h", t=NDC),
                    q2t_sp[hs].rearrange("t p h -> p t h"),
                    [b_qsp[1][hs]] + rs, rs, f"f{tag}q2_{tile}_{hs}")
                pA1, bA1 = banks[slot], b_bank[slot]
                pA2, bA2 = banks[2 + slot], b_bank[2 + slot]
                for dc in range(NDC):
                    xh = xs_h[:, dc * NT + c0:dc * NT + c0 + 512]
                    xl = xs_l[:, dc * NT + c0:dc * NT + c0 + 512]
                    q1 = q1b[:, dc * 128:(dc + 1) * 128]
                    q2 = q2b[:, dc * 128:(dc + 1) * 128]
                    mm(pA1[:], q1, xh, dc == 0, False, rs + [b_xs], bA1, "fA")
                    mm(pA1[:], q1, xl, False, dc == NDC - 1, rs + [b_xs], bA1, "fA")
                    mm(pA2[:], q2, xh, dc == 0, False, rs + [b_xs], bA2, "fA")
                    mm(pA2[:], q2, xl, False, dc == NDC - 1, rs + [b_xs], bA2, "fA")
                hh = hid_h[:, hs * 512:(hs + 1) * 512]
                hl = hid_l[:, hs * 512:(hs + 1) * 512]
                rh = hreg(hs * 256, 256) + hreg(5632 + hs * 256, 256)
                P.add("scalar", lambda e, pA1=pA1: e.activation(sq_sb[:], pA1[:], AF.Sigmoid),
                      reads=[bA1], writes=[b_sq], tag=f"fsg{hs}")
                P.add("vector", lambda e, pA2=pA2: e.tensor_mul(xev[:], sq_sb[:], pA2[:]),
                      reads=[b_sq, bA2], writes=[b_xev], tag=f"fmu{hs}")
                P.add("scalar", lambda e, hh=hh: e.copy(hh, xev[:]),
                      reads=[b_xev], writes=rh, tag=f"fhh{hs}")
                P.add("vector", lambda e, hh=hh, hl=hl: e.tensor_sub(hl, xev[:], hh),
                      reads=[b_xev] + rh, writes=rh, tag=f"fhl{hs}")
            for g0, g1_ in ((0, 4), (4, 8), (8, 12), (12, 16)):
                nb = g1_ - g0
                for hs in range(NHS):
                    slot = hs % 4
                    so = 8192 + slot * 512
                    q3b = strm[:, so:so + 512]
                    rs3 = [b_q3s[slot]]
                    dma("sync", q3b[:, 0:nb * 128].rearrange("p (t h) -> p t h", t=nb),
                        q3t_sp[hs, g0:g1_].rearrange("t p h -> p t h"),
                        [b_qsp[2][hs]] + rs3, rs3, f"f{tag}q3_{tile}_{g0}_{hs}")
                    hh = hid_h[:, hs * 512:(hs + 1) * 512]
                    hl = hid_l[:, hs * 512:(hs + 1) * 512]
                    rh = hreg(hs * 256, 256) + hreg(5632 + hs * 256, 256)
                    for d2 in range(g0, g1_):
                        q3 = q3b[:, (d2 - g0) * 128:(d2 - g0 + 1) * 128]
                        pB, bB = banks[d2 - g0], b_bank[d2 - g0]
                        mm(pB[:], q3, hh, hs == 0, False, rs3 + rh, bB, "fB")
                        mm(pB[:], q3, hl, False, hs == NHS - 1, rs3 + rh, bB, "fB")
                for d2 in range(g0, g1_):
                    pB, bB = banks[d2 - g0], b_bank[d2 - g0]
                    dma("sync", xev[:], add_src[d2][:, c0:c0 + 512],
                        [b_xev, src_buf], [b_xev], f"f{tag}xl{tile}_{d2}")
                    P.add("vector", lambda e, pB=pB: e.scalar_tensor_tensor(
                        xnew[:], pB[:], 0.5, xev[:], ALU.mult, ALU.add),
                        reads=[bB, b_xev], writes=[b_xnew], tag=f"fev{d2}")
                    dma("gpsimd", dst[d2][:, c0:c0 + 512], xnew[:],
                        [b_xnew], [dst_buf], f"f{tag}xs{tile}_{d2}")
                    if fuse_stats:
                        first = (d2 == g0 == 0)
                        last = (d2 == NDC - 1)
                        P.add("scalar", lambda e: e.square(sq_sb[:], xnew[:]),
                              reads=[b_xnew], writes=[b_sq], tag=f"fq{d2}")
                        mm(banks[4 + tile][0:1, :], onec[:], xnew[:], first, last,
                           [b_xnew, b_const], b_bank[4 + tile], f"fs{d2}")
                        mm(banks[6 + tile][0:1, :], onec[:], sq_sb[:], first, last,
                           [b_sq, b_const], b_bank[6 + tile], f"fz{d2}")

    emit_stats_standalone(x_in)
    emit_mr("p1")
    emit_xsplit(x_in, "p1", b_xin)
    emit_ffn(x_in, x_dram, True, "1", b_xin, b_xdram, zip_cb=emit_quant_hs)

    if KDBG:
        for dc in range(NDC):
            dma("gpsimd", dbg_xa[dc], x_dram[dc], [b_xdram], [b_xout], f"dxa{dc}")
        dma("gpsimd", dbg_thr, thr[:], [b_thr], [b_xout], "dthr")
    emit_mr("p2")
    emit_xsplit(x_dram, "p2", b_xdram)
    if KDBG:
        dma("gpsimd", dbg_xs[0], xs_h[:], [b_xs], [b_xout], "dxs0")
        dma("gpsimd", dbg_xs[1], xs_l[:], [b_xs], [b_xout], "dxs1")

    P.add("tensor", lambda e: e.matmul(banks[6][:, 0:16], oner[:], ld_sb[0:1, :],
                                       start=True, stop=True),
          reads=[b_const], writes=[b_bank[6]], tag="ldb")
    P.add("scalar", lambda e: e.copy(ldb[:], banks[6][:, 0:16]),
          reads=[b_bank[6]], writes=[b_decay], tag="ldbc")
    DTm = hidp[:, 0:2048]
    r_DT = hreg(0, 2048)
    difft = hidp[:, 11520:11648]
    trit = hidp[:, 11648:11776]
    r_dtc = hreg(11520, 256)
    dma("sync", difft[:], c_diffm, r_dtc, r_dtc, "df")
    dma("sync", trit[:], c_tri, r_dtc, r_dtc, "tr")
    for h in range(H):
        P.add("scalar", lambda e, h=h: e.activation(
            qdcd[:, h * 8:(h + 1) * 8], ioci[:], AF.Exp, scale=ldb[:, h:h + 1]),
            reads=[b_const, b_decay], writes=[b_decay], tag=f"qd{h}")
        P.add("scalar", lambda e, h=h: e.activation(
            kdc[:, h:h + 1], iorev[:], AF.Exp, scale=ldb[:, h:h + 1]),
            reads=[b_const, b_decay], writes=[b_decay], tag=f"kd{h}")
        P.add("scalar", lambda e, h=h: e.activation(
            cdc[:, h:h + 1], c128t[:], AF.Exp, scale=ldb[:, h:h + 1]),
            reads=[b_const, b_decay], writes=[b_decay], tag=f"cd{h}")
        P.add("scalar", lambda e, h=h: e.activation(
            DTm[:, h * 128:(h + 1) * 128], difft[:], AF.Exp, scale=ldb[:, h:h + 1]),
            reads=r_dtc + [b_decay], writes=r_DT, tag=f"dt{h}")
        P.add("vector", lambda e, h=h: e.tensor_mul(
            DTm[:, h * 128:(h + 1) * 128], DTm[:, h * 128:(h + 1) * 128], trit[:]),
            reads=r_dtc + r_DT, writes=r_DT, tag=f"dm{h}")

    wg1s = hidp[:, 12288:14336]
    wg1h = hidp[:, 14336:15360].bitcast(BF16)
    wg1l = hidp[:, 15360:16384].bitcast(BF16)
    r_wg1s = hreg(12288, 2048)
    r_wg1 = hreg(14336, 2048)
    g1f = hidp[:, 16384:17408]
    r_g1f = hreg(16384, 1024)
    dma("sync", wg1s[:].rearrange("p (a g) -> p a g", a=NDC),
        wg1_in.rearrange("(a p) g -> p a g", p=128),
        r_wg1s, r_wg1s, "wg1l")
    P.add("scalar", lambda e: e.copy(wg1h[:], wg1s[:]), reads=r_wg1s, writes=r_wg1, tag="wg1h")
    P.add("vector", lambda e: e.tensor_sub(wg1l[:], wg1s[:], wg1h[:]),
          reads=r_wg1s + r_wg1, writes=r_wg1, tag="wg1lo")
    for h2 in range(2):
        pG, bG = banks[6], b_bank[6]
        for dc in range(NDC):
            xh = xs_h[:, dc * NT + h2 * 512:dc * NT + h2 * 512 + 512]
            xl = xs_l[:, dc * NT + h2 * 512:dc * NT + h2 * 512 + 512]
            w_h = wg1h[:, dc * 128:(dc + 1) * 128]
            w_l = wg1l[:, dc * 128:(dc + 1) * 128]
            mm(pG[:], w_h, xh, dc == 0, False, r_wg1 + [b_xs], bG, "g1")
            mm(pG[:], w_l, xh, False, False, r_wg1 + [b_xs], bG, "g1")
            mm(pG[:], w_h, xl, False, dc == NDC - 1, r_wg1 + [b_xs], bG, "g1")
        P.add("scalar", lambda e, pG=pG: e.copy(g1f[:, 0:512], pG[:]),
              reads=[bG], writes=r_g1f, tag=f"g1e{h2}")
        P.add("scalar", lambda e, h2=h2: e.copy(g1h[:, h2 * 512:(h2 + 1) * 512], g1f[:, 0:512]),
              reads=r_g1f, writes=[b_g1], tag=f"g1h{h2}")
        P.add("vector", lambda e, h2=h2: e.tensor_sub(
            g1l[:, h2 * 512:(h2 + 1) * 512], g1f[:, 0:512], g1h[:, h2 * 512:(h2 + 1) * 512]),
            reads=r_g1f + [b_g1], writes=[b_g1], tag=f"g1l{h2}")

    qkv_sets = []
    for par, base in ((0, 5120), (1, 16896)):
        qkv_sets.append((
            hidp[:, base:base + 1024], hreg(base, 1024),
            hidp[:, base + 1024:base + 2048], hreg(base + 1024, 1024),
            hidp[:, base + 2048:base + 3072], hreg(base + 2048, 1024),
        ))
    khat = hidp[:, 8192:8320]
    sTt = hidp[:, 8320:8448]
    otmp = hidp[:, 8448:8576]
    ogac = hidp[:, 8576:9600]
    ptmp = hidp[:, 10880:11392]
    r_sc = hreg(8192, 384)
    r_og = hreg(8576, 1024)
    r_pt = hreg(10880, 512)

    for h in range(H):
        qf, r_qf, kf, r_kf, vf, r_vf = qkv_sets[h % 2]
        wbufs = []
        for wi, w_ap in enumerate((wq_in, wk_in)):
            wh_b = strm[:, wi * 4096:wi * 4096 + 2048]
            wl_b = strm[:, wi * 4096 + 2048:wi * 4096 + 4096]
            rw = sreg(wi * 4096, 4096)
            sl = (h * 3 + wi) % 2
            wstg = hidp[:, 12288 + sl * 2048:12288 + sl * 2048 + 2048]
            r_ws = hreg(12288 + sl * 2048, 2048)
            dma("sync", wstg[:].rearrange("p (a e) -> p a e", a=NDC),
                w_ap.rearrange("(a p) e -> p a e", p=128)[:, :, h * 128:(h + 1) * 128],
                r_ws, r_ws, f"wl{h}_{wi}")
            P.add("scalar", lambda e, wh_b=wh_b, wstg=wstg: e.copy(wh_b[:], wstg[:]),
                  reads=r_ws, writes=rw, tag=f"wh{h}_{wi}")
            P.add("vector", lambda e, wh_b=wh_b, wl_b=wl_b, wstg=wstg: e.tensor_sub(
                wl_b[:], wstg[:], wh_b[:]),
                reads=r_ws + rw, writes=rw, tag=f"wlo{h}_{wi}")
            wbufs.append((wh_b, wl_b, rw))

        for wi, dstf, rdst in ((0, qf, r_qf), (1, kf, r_kf)):
            wh_b, wl_b, rw = wbufs[wi]
            for h2 in range(2):
                pP, bP = banks[h2], b_bank[h2]
                for dc in range(NDC):
                    xh = xs_h[:, dc * NT + h2 * 512:dc * NT + h2 * 512 + 512]
                    xl = xs_l[:, dc * NT + h2 * 512:dc * NT + h2 * 512 + 512]
                    w_h = wh_b[:, dc * 128:(dc + 1) * 128]
                    w_l = wl_b[:, dc * 128:(dc + 1) * 128]
                    mm(pP[:], w_h, xh, dc == 0, False, rw + [b_xs], bP, "qk")
                    mm(pP[:], w_l, xh, False, False, rw + [b_xs], bP, "qk")
                    mm(pP[:], w_h, xl, False, dc == NDC - 1, rw + [b_xs], bP, "qk")
                P.add("scalar", lambda e, pP=pP, dstf=dstf, h2=h2: e.activation(
                    dstf[:, h2 * 512:h2 * 512 + 512], pP[:], AF.Silu),
                    reads=[bP], writes=rdst, tag=f"pe{h}_{wi}_{h2}")
        dma("gpsimd", q_sp[:, h].rearrange("ch p c -> p ch c"),
            qf[:].rearrange("p (ch c) -> p ch c", ch=NCH),
            r_qf, [b_qspill], f"qsp{h}")
        vh_b = strm[:, 8192:10240]
        vl_b = strm[:, 10240:12288]
        rwv = b_q3s + [b_s[5]]
        sl = (h * 3 + 2) % 2
        wstg = hidp[:, 12288 + sl * 2048:12288 + sl * 2048 + 2048]
        r_ws = hreg(12288 + sl * 2048, 2048)
        dma("sync", wstg[:].rearrange("p (a e) -> p a e", a=NDC),
            wv_in.rearrange("(a p) e -> p a e", p=128)[:, :, h * 128:(h + 1) * 128],
            r_ws, r_ws, f"wl{h}_2")
        P.add("scalar", lambda e, wstg=wstg, vh_b=vh_b: e.copy(vh_b[:], wstg[:]),
              reads=r_ws, writes=rwv, tag=f"wh{h}_2")
        P.add("vector", lambda e, wstg=wstg, vh_b=vh_b, vl_b=vl_b: e.tensor_sub(
            vl_b[:], wstg[:], vh_b[:]),
            reads=r_ws + rwv, writes=rwv, tag=f"wlo{h}_2")
        for ch in range(NCH):
            pV, bV = banks[2 + ch % 2], b_bank[2 + ch % 2]
            for dc in range(NDC):
                xh = xs_h[:, dc * NT + ch * 128:dc * NT + ch * 128 + 128]
                xl = xs_l[:, dc * NT + ch * 128:dc * NT + ch * 128 + 128]
                w_h = vh_b[:, dc * 128:(dc + 1) * 128]
                w_l = vl_b[:, dc * 128:(dc + 1) * 128]
                mm(pV[:, 0:128], xh, w_h, dc == 0, False, rwv + [b_xs], bV, "v")
                mm(pV[:, 0:128], xl, w_h, False, False, rwv + [b_xs], bV, "v")
                mm(pV[:, 0:128], xh, w_l, False, dc == NDC - 1, rwv + [b_xs], bV, "v")
            P.add("scalar", lambda e, pV=pV, ch=ch, vf=vf: e.activation(
                vf[:, ch * 128:(ch + 1) * 128], pV[:, 0:128], AF.Silu),
                reads=[bV], writes=r_vf, tag=f"ve{h}_{ch}")

        P.add("vector", lambda e, h=h: e.memset(S_all[:, h * 128:(h + 1) * 128], 0.0),
              reads=[], writes=[b_S], tag=f"sz{h}")

        Ssl = S_all[:, h * 128:(h + 1) * 128]
        for ci in range(NCH):
            qsl = qf[:, ci * 128:(ci + 1) * 128]
            ksl = kf[:, ci * 128:(ci + 1) * 128]
            vsl = vf[:, ci * 128:(ci + 1) * 128]
            mm(banks[4][:, 0:128], ksl, qsl, True, True, r_kf + r_qf, b_bank[4], "sc")
            P.add("vector", lambda e, h=h: e.tensor_mul(
                sTt[:], banks[4][:, 0:128], DTm[:, h * 128:(h + 1) * 128]),
                reads=[b_bank[4]] + r_DT, writes=r_sc, tag=f"sm{h}_{ci}")
            mm(banks[5][:, 0:128], sTt[:], vsl, True, True, r_sc + r_vf,
               b_bank[5], "oi")
            if ci > 0:
                mm(banks[6][:, 0:128], qsl, Ssl, True, True, r_qf + [b_S],
                   b_bank[6], "ox")
                P.add("scalar", lambda e, h=h: e.activation(
                    otmp[:], banks[6][:, 0:128], AF.Copy,
                    scale=qdcd[:, h * 8:h * 8 + 1]),
                    reads=[b_bank[6], b_decay], writes=r_sc, tag=f"oxs{h}_{ci}")
                P.add("vector", lambda e, ci=ci: e.tensor_add(
                    ogac[:, ci * 128:(ci + 1) * 128], banks[5][:, 0:128], otmp[:]),
                    reads=[b_bank[5]] + r_sc, writes=r_og, tag=f"oev{h}_{ci}")
            else:
                P.add("vector", lambda e, ci=ci: e.tensor_copy(
                    ogac[:, ci * 128:(ci + 1) * 128], banks[5][:, 0:128]),
                    reads=[b_bank[5]], writes=r_og, tag=f"oev{h}_{ci}")
            P.add("tensor", lambda e, ksl=ksl: e.transpose(
                banks[7][:, 128:256], ksl, identt[:]),
                reads=r_kf + [b_const], writes=[b_bank[7]], tag=f"kt{h}_{ci}")
            P.add("scalar", lambda e, h=h: e.activation(
                khat[:], banks[7][:, 128:256], AF.Copy, scale=kdc[:, h:h + 1]),
                reads=[b_bank[7], b_decay], writes=r_sc, tag=f"ks{h}_{ci}")
            mm(banks[6][:, 128:256], khat[:], vsl, True, True, r_sc + r_vf,
               b_bank[6], "ds")
            P.add("vector", lambda e, h=h, Ssl=Ssl: e.scalar_tensor_tensor(
                Ssl, Ssl, cdc[:, h:h + 1], banks[6][:, 128:256],
                ALU.mult, ALU.add),
                reads=[b_S, b_decay, b_bank[6]], writes=[b_S], tag=f"su{h}_{ci}")
        dma("gpsimd", o_sp[:, :, h * 128:(h + 1) * 128].rearrange("ch p c -> p ch c"),
            ogac[:].rearrange("p (ch c) -> p ch c", ch=NCH),
            r_og, [b_osp], f"osp{h}")
        if KDBG and h == 0:
            dma("gpsimd", dbg_f[0][:, 0:1024], qf[:], r_qf, [b_xout], "dq0")
            dma("gpsimd", dbg_f[1][:, 0:2048].bitcast(BF16), strm[:, 8192:12288],
                b_q3s + [b_s[5]], [b_xout], "dwv")
            dma("gpsimd", dbg_f[2][:, 0:1024], vf[:], r_vf, [b_xout], "dv0")
            dma("gpsimd", dbg_f[3][:, 0:128], S_all[:, 0:128], [b_S], [b_xout], "dS0")

    dma("gpsimd", cc_bin, S_all[:], [b_S], [b_ccb], "bw")
    P.add("gpsimd", lambda e: e.collective_compute(
        "AllGather", ALU.bypass, replica_groups=[list(range(NCORES))],
        ins=[cc_bin], outs=[cc_bout]),
        reads=[b_ccb], writes=[b_ccb], cc=True, tag="bcc")
    if KDBG:
        for j in range(NCORES):
            dma("gpsimd", dbg_B[j], cc_bout[j], [b_ccb], [b_xout], f"dB{j}")
        for ci in range(NCH):
            dma("gpsimd", dbg_osp[ci], o_sp[ci], [b_osp], [b_xout], f"dO{ci}")
            dma("gpsimd", dbg_qsp[ci], q_sp[ci], [b_qspill], [b_xout], f"dQ{ci}")
    ldrep = hidp[0:1, 0:2048]
    LD2048 = hidp[:, 2048:4096]
    Bj = hidp[:, 4096:6144]
    FBj = hidp[:, 6144:8192]
    r_ldr = hreg(0, 2048)
    r_LD = hreg(2048, 2048)
    r_Bj = hreg(4096, 2048)
    r_FB = hreg(6144, 2048)
    P.add("vector", lambda e: e.tensor_copy(
        ldrep.rearrange("q (h r) -> q h r", h=16),
        ld_sb[0:1, :].rearrange("q h -> q h 1" if False else "q (h r) -> q h r", r=1
                                ).broadcast_to([1, 16, 128])),
        reads=[b_const], writes=r_ldr, tag="ldrep")
    for h4 in range(4):
        P.add("tensor", lambda e, h4=h4: e.matmul(
            banks[6][:, :], oner[:], ldrep[0:1, h4 * 512:(h4 + 1) * 512],
            start=True, stop=True),
            reads=r_ldr + [b_const], writes=[b_bank[6]], tag=f"ldm{h4}")
        P.add("scalar", lambda e, h4=h4: e.copy(LD2048[:, h4 * 512:(h4 + 1) * 512],
                                                banks[6][:, :]),
              reads=[b_bank[6]], writes=r_LD, tag=f"ldc{h4}")
    P.add("vector", lambda e: e.memset(S_all[:], 0.0), reads=[], writes=[b_S], tag="sinz")
    for j in range(NCORES):
        P.add("vector", lambda e, j=j: e.tensor_scalar(
            tjc[:, 0:1], cidc[:], 1024.0, (j + 1) * 1024.0, ALU.mult, ALU.subtract),
            reads=[b_cid], writes=[b_cid], tag=f"tj{j}")
        P.add("vector", lambda e: e.tensor_single_scalar(
            tjc[:, 1:2], tjc[:, 0:1], 0.0, ALU.is_ge),
            reads=[b_cid], writes=[b_cid], tag=f"tm{j}")
        P.add("vector", lambda e: e.tensor_scalar(
            tjc[:, 2:3], tjc[:, 1:2], -1e6, 1e6, ALU.mult, ALU.add),
            reads=[b_cid], writes=[b_cid], tag=f"ta{j}")
        P.add("vector", lambda e: e.tensor_add(tjc[:, 0:1], tjc[:, 0:1], tjc[:, 2:3]),
              reads=[b_cid], writes=[b_cid], tag=f"tb{j}")
        P.add("scalar", lambda e: e.activation(FBj[:], LD2048[:], AF.Exp,
                                               scale=tjc[:, 0:1]),
              reads=r_LD + [b_cid], writes=r_FB, tag=f"fb{j}")
        dma("sync", Bj[:], cc_bout[j], [b_ccb] + r_Bj, r_Bj, f"bl{j}")
        P.add("vector", lambda e: e.tensor_mul(Bj[:], Bj[:], FBj[:]),
              reads=r_FB + r_Bj, writes=r_Bj, tag=f"bm{j}")
        P.add("vector", lambda e: e.tensor_add(S_all[:], S_all[:], Bj[:]),
              reads=r_Bj + [b_S], writes=[b_S], tag=f"bs{j}")

    if KDBG:
        dma("gpsimd", dbg_Sin, S_all[:], [b_S], [b_xout], "dSin")
    ogf = hidp[:, 0:8192]
    och = hidp[:, 16384:18432]
    sg = hidp[:, 18432:20480]
    qc = hidp[:, 20480:22528]
    woF = hidp[:, 12288:16384]
    ogbh = hidp[:, 0:4096].bitcast(BF16)
    ogbl = hidp[:, 5120:9216].bitcast(BF16)
    wobh = strm[:, 4096:8192]
    wobl = strm[:, 8192:12288]
    r_ogf = hreg(0, 9216)
    r_wob = sreg(4096, 8192)
    r_och = hreg(16384, 2048)
    r_sg = hreg(18432, 2048)
    r_qc = hreg(20480, 2048)
    r_wo = hreg(12288, 4096)
    wg2h = strm[:, 0:2048]
    wg2l = strm[:, 2048:4096]
    r_wg2 = sreg(0, 4096)
    for seg in range(4):
        dma("sync", ptmp[:, 0:512], wg2_in[:, seg * 512:(seg + 1) * 512], r_pt, r_pt, f"wg2_{seg}")
        P.add("scalar", lambda e, seg=seg: e.copy(
            wg2h[:, seg * 512:(seg + 1) * 512], ptmp[:, 0:512]),
            reads=r_pt, writes=r_wg2, tag=f"wg2h{seg}")
        P.add("vector", lambda e, seg=seg: e.tensor_sub(
            wg2l[:, seg * 512:(seg + 1) * 512], ptmp[:, 0:512],
            wg2h[:, seg * 512:(seg + 1) * 512]),
            reads=r_pt + r_wg2, writes=r_wg2, tag=f"wg2l{seg}")

    for hf in range(2):
        for ci in range(hf * 4, hf * 4 + 4):
            cl = ci - hf * 4
            dma("sync", och[:], o_sp[ci], [b_osp] + r_och, r_och, f"ol{ci}")
            dma("sync", qc[:].rearrange("p (h c) -> p h c", h=H),
                q_sp[ci].rearrange("h p c -> p h c"),
                [b_qspill] + r_qc, r_qc, f"ql{ci}")
            for d4 in range(4):
                pG, bG = banks[2 + d4 % 2], b_bank[2 + d4 % 2]
                g1hs = g1h[:, ci * 128:(ci + 1) * 128]
                g1ls = g1l[:, ci * 128:(ci + 1) * 128]
                mm(pG[:], g1hs, wg2h[:, d4 * 512:(d4 + 1) * 512], True, False,
                   [b_g1] + r_wg2, bG, "gt")
                mm(pG[:], g1ls, wg2h[:, d4 * 512:(d4 + 1) * 512], False, False,
                   [b_g1] + r_wg2, bG, "gt")
                mm(pG[:], g1hs, wg2l[:, d4 * 512:(d4 + 1) * 512], False, True,
                   [b_g1] + r_wg2, bG, "gt")
                P.add("scalar", lambda e, pG=pG, d4=d4: e.activation(
                    sg[:, d4 * 512:(d4 + 1) * 512], pG[:], AF.Sigmoid),
                    reads=[bG], writes=r_sg, tag=f"sig{ci}_{d4}")
            for h in range(H):
                pC, bC = banks[h % 2], b_bank[h % 2]
                mm(pC[:, 0:128], qc[:, h * 128:(h + 1) * 128],
                   S_all[:, h * 128:(h + 1) * 128],
                   True, True, r_qc + [b_S], bC, "corr")
                P.add("vector", lambda e, pC=pC, h=h, ci=ci: e.scalar_tensor_tensor(
                    och[:, h * 128:(h + 1) * 128], pC[:, 0:128],
                    qdcd[:, h * 8 + ci:h * 8 + ci + 1], och[:, h * 128:(h + 1) * 128],
                    ALU.mult, ALU.add),
                    reads=[bC, b_decay] + r_och, writes=r_och, tag=f"cadd{ci}_{h}")
            if KDBG:
                dma("gpsimd", dbg_och[ci], och[:], r_och, [b_xout], f"doch{ci}")
            P.add("vector", lambda e: e.tensor_mul(och[:], och[:], sg[:]),
                  reads=r_sg + r_och, writes=r_och, tag=f"om{ci}")
            for q4 in range(4):
                P.add("scalar", lambda e, q4=q4: e.square(
                    sg[:, q4 * 512:(q4 + 1) * 512], och[:, q4 * 512:(q4 + 1) * 512]),
                    reads=r_och, writes=r_sg, tag=f"rs{q4}{ci}")
            P.add("vector", lambda e: e.tensor_reduce(
                gacc[:, 0:16], sg[:].rearrange("p (h c) -> p h c", h=16), AX, ALU.add),
                reads=r_sg, writes=[b_gacc], tag=f"rr{ci}")
            P.add("scalar", lambda e: e.activation(gacc[:, 16:32], gacc[:, 0:16],
                                                   AF.Sqrt, scale=1.0 / HD, bias=EPS_RMS),
                  reads=[b_gacc, b_const], writes=[b_gacc], tag=f"rq{ci}")
            P.add("vector", lambda e: e.reciprocal(gacc[:, 16:32], gacc[:, 16:32]),
                  reads=[b_gacc], writes=[b_gacc], tag=f"rc{ci}")
            for h in range(H):
                P.add("scalar", lambda e, h=h: e.activation(
                    sg[:, h * 128:(h + 1) * 128], och[:, h * 128:(h + 1) * 128],
                    AF.Copy, scale=gacc[:, 16 + h:17 + h]),
                    reads=r_och + [b_gacc], writes=r_sg, tag=f"on{ci}_{h}")
                bkt = 2 + (h % 2)
                P.add("tensor", lambda e, h=h, bkt=bkt: e.transpose(
                    banks[bkt][:, 0:128], sg[:, h * 128:(h + 1) * 128], identt[:]),
                    reads=r_sg + [b_const], writes=[b_bank[bkt]], tag=f"ot{ci}_{h}")
                P.add("scalar", lambda e, h=h, cl=cl, bkt=bkt: e.copy(
                    ogbh[:, h * 512 + cl * 128:h * 512 + cl * 128 + 128],
                    banks[bkt][:, 0:128]),
                    reads=[b_bank[bkt]], writes=r_ogf, tag=f"oh{ci}_{h}")
                P.add("vector", lambda e, h=h, cl=cl, bkt=bkt: e.tensor_sub(
                    ogbl[:, h * 512 + cl * 128:h * 512 + cl * 128 + 128],
                    banks[bkt][:, 0:128],
                    ogbh[:, h * 512 + cl * 128:h * 512 + cl * 128 + 128]),
                    reads=[b_bank[bkt]] + r_ogf, writes=r_ogf, tag=f"ol{ci}_{h}")

        for eg in range(8):
            for dc in range(NDC):
                dma("sync", ptmp[:, 0:256],
                    wo_in[dc * 128:(dc + 1) * 128, eg * 256:(eg + 1) * 256],
                    r_pt, r_pt, f"wol{hf}_{eg}_{dc}")
                P.add("scalar", lambda e, dc=dc: e.activation(
                    woF[:, dc * 256:(dc + 1) * 256], ptmp[:, 0:256], AF.Copy,
                    scale=nwc[:, dc:dc + 1]),
                    reads=r_pt + [b_nw], writes=r_wo, tag=f"woh{hf}_{eg}_{dc}")
            P.add("scalar", lambda e: e.copy(wobh[:], woF[:]),
                  reads=r_wo, writes=r_wob, tag=f"wbh{hf}_{eg}")
            P.add("vector", lambda e: e.tensor_sub(wobl[:], woF[:], wobh[:]),
                  reads=r_wo + r_wob, writes=r_wob, tag=f"wbl{hf}_{eg}")
            if KDBG and hf == 0 and eg == 0:
                dma("gpsimd", dbg_f[0][:, 0:2048], woF[:, 0:2048], r_wo, [b_xout], "dwoF0")
                dma("gpsimd", dbg_f[1][:, 0:2048], woF[:, 2048:4096], r_wo, [b_xout], "dwoF1")
            for es in range(2):
                e_abs = eg * 2 + es
                pW, bW = banks[es], b_bank[es]
                for dc in range(NDC):
                    ogh_s = ogbh[:, dc * 512:dc * 512 + 512]
                    ogl_s = ogbl[:, dc * 512:dc * 512 + 512]
                    wh_s = wobh[:, dc * 256 + es * 128:dc * 256 + es * 128 + 128]
                    wl_s = wobl[:, dc * 256 + es * 128:dc * 256 + es * 128 + 128]
                    mm(pW[:], wh_s, ogh_s, dc == 0, False, r_wob + r_ogf, bW, "wo")
                    mm(pW[:], wl_s, ogh_s, False, False, r_wob + r_ogf, bW, "wo")
                    mm(pW[:], wh_s, ogl_s, False, dc == NDC - 1, r_wob + r_ogf, bW, "wo")
                dma("sync", xev[:], x_dram[e_abs][:, hf * 512:hf * 512 + 512],
                    [b_xev, b_xdram], [b_xev], f"axl{e_abs}_{hf}")
                P.add("vector", lambda e, pW=pW: e.tensor_add(xnew[:], pW[:], xev[:]),
                      reads=[bW, b_xev], writes=[b_xnew], tag=f"aev{e_abs}_{hf}")
                dma("gpsimd", x_dram[e_abs][:, hf * 512:hf * 512 + 512], xnew[:],
                    [b_xnew], [b_xdram], f"axs{e_abs}_{hf}")
                first = (e_abs == 0)
                last = (e_abs == NDC - 1)
                P.add("scalar", lambda e: e.square(sq_sb[:], xnew[:]),
                      reads=[b_xnew], writes=[b_sq], tag=f"asq{e_abs}_{hf}")
                mm(banks[4 + hf][0:1, :], onec[:], xnew[:], first, last,
                   [b_xnew, b_const], b_bank[4 + hf], f"as{e_abs}_{hf}")
                mm(banks[6 + hf][0:1, :], onec[:], sq_sb[:], first, last,
                   [b_sq, b_const], b_bank[6 + hf], f"az{e_abs}_{hf}")

    if KDBG:
        for dc in range(NDC):
            dma("gpsimd", dbg_xb[dc], x_dram[dc], [b_xdram], [b_xout], f"dxb{dc}")
    emit_mr("p3")
    emit_xsplit(x_dram, "p3", b_xdram)
    emit_ffn(x_dram, x_out, False, "2", b_xdram, b_xout)

    P.add("sync", lambda e: e.nop(), reads=[b_xout], writes=[], tag="final")

    _LAST_P[0] = P
    P.build(nc)
    with nc.Block() as block:
        P.emit(block)
    return nc


_NC_CACHE = [None]
_LAST_RES = [None]
_LAST_P = [None]


def kernel(L0_attn_x, L0_attn_log_decay, L0_attn_Wq, L0_attn_Wk, L0_attn_Wv,
           L0_attn_Wo, L0_attn_Wg1, L0_attn_Wg2, L0_attn_norm_w,
           L0_ffn_w1, L0_ffn_w2, L0_ffn_w3):
    x = np.asarray(L0_attn_x, dtype=np.float32)
    w1 = np.ascontiguousarray(np.asarray(L0_ffn_w1, np.float32))
    w2 = np.ascontiguousarray(np.asarray(L0_ffn_w2, np.float32))
    w3 = np.ascontiguousarray(np.asarray(L0_ffn_w3, np.float32))
    shared = {
        "ld_in": np.ascontiguousarray(np.asarray(L0_attn_log_decay, np.float32)[None, :]),
        "wq_in": np.ascontiguousarray(np.asarray(L0_attn_Wq, np.float32)),
        "wk_in": np.ascontiguousarray(np.asarray(L0_attn_Wk, np.float32)),
        "wv_in": np.ascontiguousarray(np.asarray(L0_attn_Wv, np.float32)),
        "wo_in": np.ascontiguousarray(np.asarray(L0_attn_Wo, np.float32)),
        "wg1_in": np.ascontiguousarray(np.asarray(L0_attn_Wg1, np.float32)),
        "wg2_in": np.ascontiguousarray(np.asarray(L0_attn_Wg2, np.float32)),
        "nw_in": np.ascontiguousarray(
            np.asarray(L0_attn_norm_w, np.float32).reshape(NDC, 128).T),
        "w1_in": w1, "w2_in": w2, "w3_in": w3,
    }
    if _NC_CACHE[0] is None:
        _NC_CACHE[0] = build_nc()
    nc = _NC_CACHE[0]
    in_maps = []
    for c in range(NCORES):
        xs = np.ascontiguousarray(
            x[0, c * NT:(c + 1) * NT, :].T.reshape(NDC, 128, NT))
        m = dict(shared)
        m["x_in"] = xs
        m["w1g_in"] = np.ascontiguousarray(w1[c * 704:(c + 1) * 704])
        m["w2g_in"] = np.ascontiguousarray(w2[c * 704:(c + 1) * 704])
        m["w3g_in"] = np.ascontiguousarray(w3[:, c * 704:(c + 1) * 704])
        m["cid_in"] = np.array([[float(c)]], dtype=np.float32)
        in_maps.append(m)
    res = run_bass_kernel_spmd(nc, in_maps, core_ids=list(range(NCORES)))
    _LAST_RES[0] = res
    out = np.empty((1, NCORES * NT, D), dtype=np.float32)
    for c in range(NCORES):
        o = res.results[c]["x_out"].reshape(D, NT)
        out[0, c * NT:(c + 1) * NT, :] = o.T
    return out


# version 20
# speedup vs baseline: 1.0417x; 1.0221x over previous
import os
import numpy as np
import ml_dtypes
import concourse.bass as bass
import concourse.mybir as mybir
from concourse.bass_utils import run_bass_kernel_spmd

F32 = mybir.dt.float32
BF16 = mybir.dt.bfloat16
AF = mybir.ActivationFunctionType
ALU = mybir.AluOpType
AX = mybir.AxisListType.X

NCORES = 8
NT = 1024
D = 2048
H = 16
HD = 128
HID = 5632
NHS = 44
NDC = 16
NCH = 8
GCNT = float(HID * D)
EPS_LN = 1e-5
EPS_RMS = 1e-5
EPS_Q = 1e-8


class Buf:
    __slots__ = ("name", "last_writer", "readers")

    def __init__(self, name):
        self.name = name
        self.last_writer = None
        self.readers = []


class Prog:
    def __init__(self):
        self.ops = []

    def add(self, eng, fn, reads=(), writes=(), dma=False, cc=False, tag=""):
        ch = "cc" if cc else (eng + "_dma" if dma else eng)
        raw = set()
        war = set()
        for b in reads:
            if b.last_writer is not None:
                raw.add(b.last_writer)
        for b in writes:
            if b.last_writer is not None:
                war.add(b.last_writer)
            war.update(b.readers)
        if not hasattr(self, "chpos"):
            self.chpos = {}
        pos = self.chpos.get(ch, 0)
        self.chpos[ch] = pos + 1
        op = {"id": len(self.ops), "eng": eng, "ch": ch, "fn": fn, "dma": dma,
              "cc": cc, "raw": raw, "war": war, "tag": tag, "inc": False,
              "val": 0, "pos": pos}
        self.ops.append(op)
        oid = op["id"]
        for b in reads:
            b.readers.append(oid)
        for b in writes:
            b.last_writer = oid
            b.readers = []
        return oid

    def _needs_edge(self, op, dop, is_raw):
        if dop["ch"] != op["ch"] or dop["dma"] or dop["cc"]:
            return True
        if dop["ch"] == "tensor":
            return False
        return is_raw and (op["pos"] - dop["pos"] < int(os.environ.get("RAWD", "6")))

    def build(self, nc):
        ops = self.ops
        if os.environ.get("DMASER", "0") == "1":
            last_dma = {}
            for op in ops:
                if op["dma"]:
                    if op["ch"] in last_dma:
                        op["raw"].add(last_dma[op["ch"]])
                    last_dma[op["ch"]] = op["id"]
        NSLOT = 8
        for op in ops:
            if op["dma"]:
                op["inc"] = True
            for d in op["raw"]:
                if self._needs_edge(op, ops[d], True):
                    ops[d]["inc"] = True
            for d in op["war"]:
                if self._needs_edge(op, ops[d], False):
                    ops[d]["inc"] = True
        counters = {}
        slot_use = {}
        for op in ops:
            if op["dma"]:
                ch = op["ch"]
                idx = slot_use.setdefault(ch, 0)
                slot_use[ch] = idx + 1
                slot = idx % NSLOT
                op["sem"] = f"{ch}{slot}"
                op["val"] = 16 * (idx // NSLOT + 1)
                op["prewait"] = (op["sem"], op["val"] - 16) if op["val"] > 16 else None
            elif op["inc"]:
                ch = op["ch"]
                counters[ch] = counters.get(ch, 0) + 1
                op["sem"] = ch
                op["val"] = counters[ch]
        for op in ops:
            waits = {}
            if op.get("prewait"):
                sm, v = op["prewait"]
                if v > 0:
                    waits[sm] = max(waits.get(sm, 0), v)
            for d, isr in [(d, True) for d in op["raw"]] + [(d, False) for d in op["war"]]:
                dop = ops[d]
                if not self._needs_edge(op, dop, isr):
                    continue
                waits[dop["sem"]] = max(waits.get(dop["sem"], 0), dop["val"])
            op["waits"] = waits
        sem_names = set()
        for op in ops:
            if op["inc"]:
                sem_names.add(op["sem"])
            sem_names.update(op["waits"].keys())
        self.sems = {n: nc.alloc_semaphore("sem_" + n) for n in sorted(sem_names)}

    def emit(self, block):
        sems = self.sems

        def body_for(engine_name):
            def body(eng):
                hw = {}
                for op in self.ops:
                    if op["eng"] != engine_name:
                        continue
                    for ch, v in sorted(op["waits"].items()):
                        if hw.get(ch, 0) < v:
                            eng.wait_ge(sems[ch], v)
                            hw[ch] = v
                    inst = op["fn"](eng)
                    if op["inc"]:
                        assert inst is not None, op["tag"]
                        inst.then_inc(sems[op["sem"]], 16 if op["dma"] else 1)
            return body

        block.tensor(body_for("tensor"))
        block.vector(body_for("vector"))
        block.scalar(body_for("scalar"))
        block.gpsimd(body_for("gpsimd"))
        block.sync(body_for("sync"))


def build_nc():
    nc = bass.Bass("TRN2", target_bir_lowering=False,
                   detect_race_conditions=False)
    P = Prog()

    x_in = nc.dram_tensor("x_in", [NDC, 128, NT], F32, kind="ExternalInput").ap()
    ld_in = nc.dram_tensor("ld_in", [1, H], F32, kind="ExternalInput").ap()
    wq_in = nc.dram_tensor("wq_in", [D, D], F32, kind="ExternalInput").ap()
    wk_in = nc.dram_tensor("wk_in", [D, D], F32, kind="ExternalInput").ap()
    wv_in = nc.dram_tensor("wv_in", [D, D], F32, kind="ExternalInput").ap()
    wo_in = nc.dram_tensor("wo_in", [D, D], F32, kind="ExternalInput").ap()
    wg1_in = nc.dram_tensor("wg1_in", [D, HD], F32, kind="ExternalInput").ap()
    wg2_in = nc.dram_tensor("wg2_in", [HD, D], F32, kind="ExternalInput").ap()
    nw_in = nc.dram_tensor("nw_in", [128, NDC], F32, kind="ExternalInput").ap()
    w1_in = nc.dram_tensor("w1_in", [HID, D], F32, kind="ExternalInput").ap()
    w2_in = nc.dram_tensor("w2_in", [HID, D], F32, kind="ExternalInput").ap()
    w3_in = nc.dram_tensor("w3_in", [D, HID], F32, kind="ExternalInput").ap()
    w1g_in = nc.dram_tensor("w1g_in", [HID // 8, D], F32, kind="ExternalInput").ap()
    w2g_in = nc.dram_tensor("w2g_in", [HID // 8, D], F32, kind="ExternalInput").ap()
    w3g_in = nc.dram_tensor("w3g_in", [D, HID // 8], F32, kind="ExternalInput").ap()
    cid_in = nc.dram_tensor("cid_in", [1, 1], F32, kind="ExternalInput").ap()
    x_out = nc.dram_tensor("x_out", [NDC, 128, NT], F32, kind="ExternalOutput").ap()
    KDBG = os.environ.get("KDBG", "") == "1"
    if KDBG:
        dbg_xa = nc.dram_tensor("dbg_xa", [NDC, 128, NT], F32, kind="ExternalOutput").ap()
        dbg_xb = nc.dram_tensor("dbg_xb", [NDC, 128, NT], F32, kind="ExternalOutput").ap()
        dbg_osp = nc.dram_tensor("dbg_osp", [NCH, 128, D], F32, kind="ExternalOutput").ap()
        dbg_qsp = nc.dram_tensor("dbg_qsp", [NCH, H, 128, 128], F32, kind="ExternalOutput").ap()
        dbg_B = nc.dram_tensor("dbg_B", [NCORES, 128, D], F32, kind="ExternalOutput").ap()
        dbg_thr = nc.dram_tensor("dbg_thr", [128, 8], F32, kind="ExternalOutput").ap()
        dbg_xs = nc.dram_tensor("dbg_xs", [2, 128, NDC * NT], BF16, kind="ExternalOutput").ap()
        dbg_gacc = nc.dram_tensor("dbg_gacc", [128, 48], F32, kind="ExternalOutput").ap()
        dbg_Sin = nc.dram_tensor("dbg_Sin", [128, D], F32, kind="ExternalOutput").ap()
        dbg_f = nc.dram_tensor("dbg_f", [4, 128, D], F32, kind="ExternalOutput").ap()
        dbg_och = nc.dram_tensor("dbg_och", [NCH, 128, D], F32, kind="ExternalOutput").ap()
        dbg_grow = nc.dram_tensor("dbg_grow", [1, 8], F32, kind="ExternalOutput").ap()

    _i = np.arange(128, dtype=np.float32)
    diff = _i[None, :] - _i[:, None]
    tri = (diff >= 0).astype(np.float32)
    diffm = diff * tri
    c_ident = nc.inline_tensor(np.eye(128, dtype=np.float32), "c_ident").ap()
    c_identb = nc.inline_tensor(np.eye(128, dtype=np.float32).astype(ml_dtypes.bfloat16),
                                "c_identb").ap()
    c_onec = nc.inline_tensor(np.ones((128, 1), np.float32), "c_onec").ap()
    c_oner = nc.inline_tensor(np.ones((1, 128), np.float32), "c_oner").ap()
    c_diffm = nc.inline_tensor(diffm, "c_diffm").ap()
    c_tri = nc.inline_tensor(tri, "c_tri").ap()
    c_iop1 = nc.inline_tensor((_i + 1.0)[:, None], "c_iop1").ap()
    c_iorev = nc.inline_tensor((127.0 - _i)[:, None], "c_iorev").ap()
    c_ioci = nc.inline_tensor(
        _i[:, None] + 1.0 + 128.0 * np.arange(8, dtype=np.float32)[None, :],
        "c_ioci").ap()
    c_128 = nc.inline_tensor(np.full((128, 1), 128.0, np.float32), "c_128").ap()
    sel = np.zeros((128, 16), np.float32)
    for j in range(4):
        sel[:, 4 * j + j] = 1.0
    c_sel4 = nc.inline_tensor(sel, "c_sel4").ap()
    FCONST_VALS = [1.0 / D, -1.0, EPS_LN, 0.5 / GCNT, 0.5 * EPS_Q, 1.0 / HD,
                   1.0, 0.0]
    c_fc = nc.inline_tensor(
        np.tile(np.asarray(FCONST_VALS, np.float32)[None, :], (128, 1)),
        "c_fc").ap()

    q1t_sp = nc.dram_tensor("q1t_sp", [NHS, NDC, 128, 128], BF16).ap()
    q2t_sp = nc.dram_tensor("q2t_sp", [NHS, NDC, 128, 128], BF16).ap()
    q3t_sp = nc.dram_tensor("q3t_sp", [NHS, NDC, 128, 128], BF16).ap()
    x_dram = nc.dram_tensor("x_dram", [NDC, 128, NT], F32).ap()
    o_sp = nc.dram_tensor("o_sp", [NCH, 128, D], F32).ap()
    q_sp = nc.dram_tensor("q_sp", [NCH, H, 128, 128], F32).ap()
    cc_gin = nc.dram_tensor("cc_gin", [1, 4], F32).ap()
    cc_gout = nc.dram_tensor("cc_gout", [1, 4], F32).ap()
    cc_bin = nc.dram_tensor("cc_bin", [128, D], F32).ap()
    cc_bout = nc.dram_tensor("cc_bout", [NCORES, 128, D], F32).ap()

    sb = lambda n, s, d=F32: nc.alloc_sbuf_tensor(n, s, d)
    xs_h = sb("xs_h", [128, NDC * NT], BF16)
    xs_l = sb("xs_l", [128, NDC * NT], BF16)
    hidp = sb("hidp", [128, 22528])
    strm = sb("strm", [128, 12288], BF16)
    S_all = sb("S_all", [128, D])
    g1h = sb("g1h", [128, NT], BF16)
    g1l = sb("g1l", [128, NT], BF16)
    Mb = sb("Mb", [128, NT])
    Rb = sb("Rb", [128, NT])
    xt_sb = sb("xt_sb", [128, 512])
    xev = sb("xev", [128, 512])
    xnew = sb("xnew", [128, 512])
    sq_sb = sb("sq_sb", [128, 512])
    gacc = sb("gacc", [128, 48])
    identt = sb("identt", [128, 128])
    identb = sb("identb", [128, 128], BF16)
    onec = sb("onec", [128, 1])
    oner = sb("oner", [1, 128])
    iop1 = sb("iop1", [128, 1])
    iorev = sb("iorev", [128, 1])
    ioci = sb("ioci", [128, 8])
    c128t = sb("c128t", [128, 1])
    fconst = sb("fconst", [128, 8])
    ld_sb = sb("ld_sb", [1, 16])
    ldb = sb("ldb", [128, 16])
    qdcd = sb("qdcd", [128, H * 8])
    kdc = sb("kdc", [128, H])
    cdc = sb("cdc", [128, H])
    nwc = sb("nwc", [128, NDC])
    thr = sb("thr", [128, 8])
    cidc = sb("cidc", [128, 1])
    tjc = sb("tjc", [128, 4])
    grow = sb("grow", [1, 8])

    banks = [nc.alloc_psum_tensor(f"bank{i}", [128, 512], F32) for i in range(8)]
    for _j, _v in enumerate(FCONST_VALS):
        nc.const_aps.register(F32, float(_v), fconst[:, _j:_j + 1])

    b_xs = Buf("xs")
    b_h = [Buf(f"h{i}") for i in range(22)]
    b_s = [Buf(f"s{i}") for i in range(6)]
    b_S = Buf("S")
    b_g1 = Buf("g1")
    b_const = Buf("const")
    b_decay = Buf("decay")
    b_MR = Buf("MR")
    b_grow = Buf("grow")
    b_thr = Buf("thr")
    b_bank = [Buf(f"bank{i}") for i in range(8)]
    b_xin = Buf("xin")
    b_xdram = Buf("xdram")
    b_xout = Buf("xout")
    b_qsp = [[Buf(f"q{w}sp{h}") for h in range(NHS)] for w in range(3)]
    b_osp = Buf("osp")
    b_qspill = Buf("qspill")
    b_ccg = Buf("ccg")
    b_ccb = Buf("ccb")
    b_xt = Buf("xt")
    b_xev = Buf("xev")
    b_xnew = Buf("xnew")
    b_sq = Buf("sq")
    b_gacc = Buf("gacc")
    b_nw = Buf("nw")
    b_cid = Buf("cid")

    def hreg(lo_f32, n_f32):
        i0 = lo_f32 // 1024
        i1 = (lo_f32 + n_f32 + 1023) // 1024
        return [b_h[i] for i in range(i0, i1)]

    def sreg(lo_b, n_b):
        i0 = lo_b // 2048
        i1 = (lo_b + n_b + 2047) // 2048
        return [b_s[i] for i in range(i0, i1)]

    def dma(q, out, in_, reads, writes, tag=""):
        return P.add(q, lambda e, o=out, i=in_: e.dma_start(out=o, in_=i),
                     reads=reads, writes=writes, dma=True, tag=tag)

    def dma_tr(out, in_, reads, writes, tag=""):
        return P.add("scalar", lambda e, o=out, i=in_: e.dma_start(out=o, in_=i, transpose=True),
                     reads=reads, writes=writes, dma=True, tag=tag)

    def mm(ps, lhsT, rhs, st, sp, reads, wbank, tag=""):
        return P.add("tensor", lambda e, p=ps, l=lhsT, r=rhs, a=st, b=sp:
                     e.matmul(p, l, r, start=a, stop=b),
                     reads=reads, writes=[wbank], tag=tag)

    for dst, src, t in [(identt[:], c_ident, "id"), (onec[:], c_onec, "oc"),
                        (oner[:], c_oner, "or"), (iop1[:], c_iop1, "i1"),
                        (iorev[:], c_iorev, "ir"), (ioci[:], c_ioci, "ic"),
                        (c128t[:], c_128, "c128"),
                        (fconst[:], c_fc, "fc"), (identb[:], c_identb, "idb"),
                        (ld_sb[:], ld_in, "ld")]:
        dma("sync", dst, src, [], [b_const], t)
    dma("sync", nwc[:], nw_in, [], [b_nw], "nw")
    dma("sync", grow[0:1, 0:1], cid_in, [], [b_grow], "cid0")
    P.add("tensor", lambda e: e.matmul(banks[6][:, 0:1], oner[:], grow[0:1, 0:1],
                                       start=True, stop=True),
          reads=[b_grow, b_const], writes=[b_bank[6]], tag="cidbc")
    P.add("scalar", lambda e: e.copy(cidc[:], banks[6][:, 0:1]),
          reads=[b_bank[6]], writes=[b_cid], tag="cidcp")

    gstage = hidp[:, 0:2048]
    r_gst = hreg(0, 2048)

    def emit_gamma(w_ap, idx):
        cbase = idx * 12
        t = w_ap.rearrange("(a p) d -> a p d", p=64)
        for a in range(11):
            dma("sync", gstage[0:64, 0:D], t[a], r_gst, r_gst, f"gld{idx}_{a}")
            P.add("vector", lambda e, a=a: e.tensor_reduce(
                gacc[0:64, cbase + a:cbase + a + 1], gstage[0:64, 0:D], AX,
                ALU.add, apply_absolute_value=True),
                reads=r_gst, writes=[b_gacc], tag=f"gr{idx}_{a}")
        P.add("vector", lambda e: e.tensor_reduce(
            gacc[0:64, cbase + 11:cbase + 12], gacc[0:64, cbase:cbase + 11],
            AX, ALU.add), reads=[b_gacc], writes=[b_gacc], tag=f"gs{idx}")
        mm(banks[4 + idx][0:1, 0:1], onec[0:64, 0:1],
           gacc[0:64, cbase + 11:cbase + 12], True, True,
           [b_gacc, b_const], b_bank[4 + idx], f"gm{idx}")

    emit_gamma(w1g_in, 0)
    emit_gamma(w2g_in, 1)
    t3 = w3g_in.rearrange("(a p) h -> a p h", p=128)
    for a in range(16):
        dma("sync", gstage[:, 0:704], t3[a], r_gst, r_gst, f"gld2_{a}")
        P.add("vector", lambda e, a=a: e.tensor_reduce(
            gacc[:, 32 + a:33 + a], gstage[:, 0:704], AX, ALU.add,
            apply_absolute_value=True),
            reads=r_gst, writes=[b_gacc], tag=f"gr2_{a}")
    P.add("vector", lambda e: e.tensor_reduce(
        gacc[:, 30:31], gacc[:, 32:48], AX, ALU.add),
        reads=[b_gacc], writes=[b_gacc], tag="gs2a")
    mm(banks[6][0:1, 0:1], onec[:], gacc[:, 30:31], True, True,
       [b_gacc, b_const], b_bank[6], "gm2")
    for j in range(3):
        P.add("scalar", lambda e, j=j: e.copy(grow[0:1, j:j + 1], banks[4 + j][0:1, 0:1]),
              reads=[b_bank[4 + j]], writes=[b_grow], tag=f"gc{j}")
    P.add("scalar", lambda e: e.memzero(grow[0:1, 3:4]),
          reads=[], writes=[b_grow], tag="gcz")
    if KDBG:
        dma("gpsimd", dbg_gacc, gacc[:], [b_gacc], [b_xout], "dgacc")
        dma("gpsimd", dbg_grow, grow[:], [b_grow], [b_xout], "dgrow")
    dma("gpsimd", cc_gin, grow[0:1, 0:4], [b_grow], [b_ccg], "gw")
    P.add("gpsimd", lambda e: e.collective_compute(
        "AllReduce", ALU.add, replica_groups=[list(range(NCORES))],
        ins=[cc_gin], outs=[cc_gout]),
        reads=[b_ccg], writes=[b_ccg], cc=True, tag="gcc")
    dma("gpsimd", grow[0:1, 4:8], cc_gout, [b_ccg], [b_grow], "gr")
    P.add("scalar", lambda e: e.activation(grow[0:1, 0:3], grow[0:1, 4:7], AF.Copy,
                                           scale=0.5 / GCNT, bias=0.5 * EPS_Q),
          reads=[b_grow, b_const], writes=[b_grow], tag="gt")
    P.add("scalar", lambda e: e.mul(grow[0:1, 4:7], grow[0:1, 0:3], -1.0),
          reads=[b_grow, b_const], writes=[b_grow], tag="gtn")
    P.add("tensor", lambda e: e.matmul(banks[6][:, 0:8], oner[:], grow[0:1, 0:8],
                                       start=True, stop=True),
          reads=[b_grow, b_const], writes=[b_bank[6]], tag="gbc")
    P.add("scalar", lambda e: e.copy(thr[:], banks[6][:, 0:8]),
          reads=[b_bank[6]], writes=[b_thr], tag="gthr")

    def emit_quant_hs(hs):
        qstg = [(Mb[:, :], [b_MR]), (Rb[:, :], [b_MR])]
        qnat_s = [(S_all[:, 0:1024], [b_S]),
                  (S_all[:, 1024:2048], [b_S])]
        qtr_s = [(g1h[:, :], [b_g1]), (g1l[:, :], [b_g1])]
        w3v = w3_in.rearrange("(t p) h -> p t h", p=128)
        if True:
            for widx, (w_ap, sp_ap) in enumerate(
                    ((w1_in, q1t_sp), (w2_in, q2t_sp), (w3_in, q3t_sp))):
                for half in range(2):
                    sl = (hs * 6 + widx * 2 + half) % 2
                    stg, r_sl = qstg[sl]
                    qn, r_qn = qnat_s[sl]
                    qt, r_qt = qtr_s[sl]
                    if widx < 2:
                        dma("sync", stg, w_ap[hs * 128:(hs + 1) * 128,
                                              half * 1024:(half + 1) * 1024],
                            r_sl, r_sl, f"qld{widx}_{hs}_{half}")
                    else:
                        dma("sync", stg.rearrange("p (t h) -> p t h", t=8),
                            w3v[:, half * 8:(half + 1) * 8, hs * 128:(hs + 1) * 128],
                            r_sl, r_sl, f"qld{widx}_{hs}_{half}")
                    P.add("vector", lambda e, qn=qn, stg=stg, widx=widx:
                          e.tensor_single_scalar(qn, stg, thr[:, 4 + widx:5 + widx],
                                                 ALU.is_le),
                          reads=r_sl + [b_thr] + r_qn, writes=r_qn, tag=f"qa{widx}_{hs}{half}")
                    P.add("vector", lambda e, qn=qn, stg=stg, widx=widx:
                          e.scalar_tensor_tensor(qn, stg, thr[:, widx:widx + 1],
                                                 qn, ALU.is_ge, ALU.subtract),
                          reads=r_sl + [b_thr] + r_qn, writes=r_qn, tag=f"qb{widx}_{hs}{half}")
                    for t in range(8):
                        bki = 5 if t % 2 == 0 else 7
                        P.add("tensor", lambda e, qn=qn, t=t, bki=bki: e.transpose(
                            banks[bki][:, 0:128], qn[:, t * 128:(t + 1) * 128], identt[:]),
                            reads=r_qn + [b_const], writes=[b_bank[bki]],
                            tag=f"qT{widx}_{hs}_{half}_{t}")
                        P.add("scalar", lambda e, qt=qt, t=t, bki=bki: e.copy(
                            qt[:, t * 128:(t + 1) * 128], banks[bki][:, 0:128]),
                            reads=[b_bank[bki]], writes=r_qt,
                            tag=f"qE{widx}_{hs}_{half}_{t}")
                    dma("gpsimd", sp_ap[hs, half * 8:(half + 1) * 8].rearrange("t p h -> p t h"),
                        qt.rearrange("p (t h) -> p t h", t=8),
                        r_qt, [b_qsp[widx][hs]], f"qs{widx}_{hs}_{half}")

    def emit_stats_standalone(src):
        for dc in range(NDC):
            st, sp = dc == 0, dc == NDC - 1
            for hf in range(2):
                dma("sync", xt_sb[:], src[dc][:, hf * 512:(hf + 1) * 512],
                    [b_xt], [b_xt], f"sld{dc}_{hf}")
                P.add("scalar", lambda e: e.square(sq_sb[:], xt_sb[:]),
                      reads=[b_xt], writes=[b_sq], tag=f"sq{dc}_{hf}")
                mm(banks[4 + hf][0:1, :], onec[:], xt_sb[:], st, sp,
                   [b_xt, b_const], b_bank[4 + hf], f"sm{dc}_{hf}")
                mm(banks[6 + hf][0:1, :], onec[:], sq_sb[:], st, sp,
                   [b_sq, b_const], b_bank[6 + hf], f"sz{dc}_{hf}")

    rowM = Mb[0:1, :]
    rowR = Rb[0:1, :]

    def emit_mr(tag):
        for hf in range(2):
            P.add("scalar", lambda e, hf=hf: e.activation(
                Mb[0:1, hf * 512:(hf + 1) * 512], banks[4 + hf][0:1, :],
                AF.Copy, scale=1.0 / D),
                reads=[b_bank[4 + hf], b_const], writes=[b_MR], tag=f"m{tag}{hf}")
            P.add("scalar", lambda e, hf=hf: e.activation(
                Rb[0:1, hf * 512:(hf + 1) * 512], banks[6 + hf][0:1, :],
                AF.Copy, scale=1.0 / D),
                reads=[b_bank[6 + hf], b_const], writes=[b_MR], tag=f"e{tag}{hf}")
        for hf in range(2):
            P.add("tensor", lambda e, hf=hf: e.matmul(
                banks[0][:, :], oner[:], Mb[0:1, hf * 512:(hf + 1) * 512],
                start=True, stop=True),
                reads=[b_MR, b_const], writes=[b_bank[0]], tag=f"mb{tag}{hf}")
            P.add("scalar", lambda e, hf=hf: e.copy(
                Mb[:, hf * 512:(hf + 1) * 512], banks[0][:, :]),
                reads=[b_bank[0]], writes=[b_MR], tag=f"mc{tag}{hf}")
            P.add("tensor", lambda e, hf=hf: e.matmul(
                banks[1][:, :], oner[:], Rb[0:1, hf * 512:(hf + 1) * 512],
                start=True, stop=True),
                reads=[b_MR, b_const], writes=[b_bank[1]], tag=f"rb{tag}{hf}")
            P.add("scalar", lambda e, hf=hf: e.copy(
                Rb[:, hf * 512:(hf + 1) * 512], banks[1][:, :]),
                reads=[b_bank[1]], writes=[b_MR], tag=f"rc{tag}{hf}")
        for hf in range(2):
            P.add("vector", lambda e, hf=hf: e.tensor_mul(
                sq_sb[:], Mb[:, hf * 512:(hf + 1) * 512], Mb[:, hf * 512:(hf + 1) * 512]),
                reads=[b_MR], writes=[b_sq], tag=f"v1{tag}{hf}")
            P.add("vector", lambda e, hf=hf: e.tensor_sub(
                Rb[:, hf * 512:(hf + 1) * 512], Rb[:, hf * 512:(hf + 1) * 512], sq_sb[:]),
                reads=[b_MR, b_sq], writes=[b_MR], tag=f"v2{tag}{hf}")
            P.add("scalar", lambda e, hf=hf: e.activation(
                Rb[:, hf * 512:(hf + 1) * 512], Rb[:, hf * 512:(hf + 1) * 512],
                AF.Sqrt, bias=EPS_LN),
                reads=[b_MR, b_const], writes=[b_MR], tag=f"v3{tag}{hf}")
            P.add("vector", lambda e, hf=hf: e.reciprocal(
                Rb[:, hf * 512:(hf + 1) * 512], Rb[:, hf * 512:(hf + 1) * 512]),
                reads=[b_MR], writes=[b_MR], tag=f"v4{tag}{hf}")

    def emit_xsplit(src, tag, src_buf):
        for dc in range(NDC):
            for hf in range(2):
                o = dc * NT + hf * 512
                par = (dc * 2 + hf) % 2
                buf = xt_sb if par == 0 else sq_sb
                bb = b_xt if par == 0 else b_sq
                dma("sync", buf[:], src[dc][:, hf * 512:(hf + 1) * 512],
                    [bb, src_buf], [bb], f"x{tag}l{dc}_{hf}")
                P.add("vector", lambda e, hf=hf, buf=buf: e.tensor_sub(
                    buf[:], buf[:], Mb[:, hf * 512:(hf + 1) * 512]),
                    reads=[b_MR, bb], writes=[bb], tag=f"x{tag}s{dc}_{hf}")
                P.add("vector", lambda e, hf=hf, buf=buf: e.tensor_mul(
                    buf[:], buf[:], Rb[:, hf * 512:(hf + 1) * 512]),
                    reads=[b_MR, bb], writes=[bb], tag=f"x{tag}m{dc}_{hf}")
                P.add("scalar", lambda e, o=o, buf=buf: e.copy(xs_h[:, o:o + 512], buf[:]),
                      reads=[bb], writes=[b_xs], tag=f"x{tag}h{dc}_{hf}")
                P.add("vector", lambda e, o=o, buf=buf: e.tensor_sub(
                    xs_l[:, o:o + 512], buf[:], xs_h[:, o:o + 512]),
                    reads=[bb, b_xs], writes=[b_xs], tag=f"x{tag}L{dc}_{hf}")

    hid_h = hidp[:, 0:11264].bitcast(BF16)
    hid_l = hidp[:, 11264:22528].bitcast(BF16)
    r_hidh = hreg(0, 11264)
    r_hidl = hreg(11264, 11264)

    b_q3s = [Buf(f"q3s{i}") for i in range(4)]

    def emit_ffn(add_src, dst, fuse_stats, tag, src_buf, dst_buf, zip_cb=None):
        for tile in range(2):
            c0 = tile * 512
            for hs in range(NHS):
                if zip_cb is not None and tile == 0:
                    zip_cb(hs)
                slot = hs % 2
                so = slot * 4096
                q1b = strm[:, so:so + 2048]
                q2b = strm[:, so + 2048:so + 4096]
                rs = sreg(so, 4096)
                dma("sync", q1b.rearrange("p (t h) -> p t h", t=NDC),
                    q1t_sp[hs].rearrange("t p h -> p t h"),
                    [b_qsp[0][hs]] + rs, rs, f"f{tag}q1_{tile}_{hs}")
                dma("sync", q2b.rearrange("p (t h) -> p t h", t=NDC),
                    q2t_sp[hs].rearrange("t p h -> p t h"),
                    [b_qsp[1][hs]] + rs, rs, f"f{tag}q2_{tile}_{hs}")
                pA1, bA1 = banks[slot], b_bank[slot]
                pA2, bA2 = banks[2 + slot], b_bank[2 + slot]
                for dc in range(NDC):
                    xh = xs_h[:, dc * NT + c0:dc * NT + c0 + 512]
                    xl = xs_l[:, dc * NT + c0:dc * NT + c0 + 512]
                    q1 = q1b[:, dc * 128:(dc + 1) * 128]
                    q2 = q2b[:, dc * 128:(dc + 1) * 128]
                    mm(pA1[:], q1, xh, dc == 0, False, rs + [b_xs], bA1, "fA")
                    mm(pA1[:], q1, xl, False, dc == NDC - 1, rs + [b_xs], bA1, "fA")
                    mm(pA2[:], q2, xh, dc == 0, False, rs + [b_xs], bA2, "fA")
                    mm(pA2[:], q2, xl, False, dc == NDC - 1, rs + [b_xs], bA2, "fA")
                hh = hid_h[:, hs * 512:(hs + 1) * 512]
                hl = hid_l[:, hs * 512:(hs + 1) * 512]
                rh = hreg(hs * 256, 256) + hreg(5632 + hs * 256, 256)
                P.add("scalar", lambda e, pA1=pA1: e.activation(sq_sb[:], pA1[:], AF.Sigmoid),
                      reads=[bA1], writes=[b_sq], tag=f"fsg{hs}")
                P.add("vector", lambda e, pA2=pA2: e.tensor_mul(xev[:], sq_sb[:], pA2[:]),
                      reads=[b_sq, bA2], writes=[b_xev], tag=f"fmu{hs}")
                P.add("scalar", lambda e, hh=hh: e.copy(hh, xev[:]),
                      reads=[b_xev], writes=rh, tag=f"fhh{hs}")
                P.add("vector", lambda e, hh=hh, hl=hl: e.tensor_sub(hl, xev[:], hh),
                      reads=[b_xev] + rh, writes=rh, tag=f"fhl{hs}")
            for g0, g1_ in ((0, 4), (4, 8), (8, 12), (12, 16)):
                nb = g1_ - g0
                for hs in range(NHS):
                    slot = hs % 4
                    so = 8192 + slot * 512
                    q3b = strm[:, so:so + 512]
                    rs3 = [b_q3s[slot]]
                    dma("sync", q3b[:, 0:nb * 128].rearrange("p (t h) -> p t h", t=nb),
                        q3t_sp[hs, g0:g1_].rearrange("t p h -> p t h"),
                        [b_qsp[2][hs]] + rs3, rs3, f"f{tag}q3_{tile}_{g0}_{hs}")
                    hh = hid_h[:, hs * 512:(hs + 1) * 512]
                    hl = hid_l[:, hs * 512:(hs + 1) * 512]
                    rh = hreg(hs * 256, 256) + hreg(5632 + hs * 256, 256)
                    for d2 in range(g0, g1_):
                        q3 = q3b[:, (d2 - g0) * 128:(d2 - g0 + 1) * 128]
                        pB, bB = banks[d2 - g0], b_bank[d2 - g0]
                        mm(pB[:], q3, hh, hs == 0, False, rs3 + rh, bB, "fB")
                        mm(pB[:], q3, hl, False, hs == NHS - 1, rs3 + rh, bB, "fB")
                for d2 in range(g0, g1_):
                    pB, bB = banks[d2 - g0], b_bank[d2 - g0]
                    dma("sync", xev[:], add_src[d2][:, c0:c0 + 512],
                        [b_xev, src_buf], [b_xev], f"f{tag}xl{tile}_{d2}")
                    P.add("vector", lambda e, pB=pB: e.scalar_tensor_tensor(
                        xnew[:], pB[:], 0.5, xev[:], ALU.mult, ALU.add),
                        reads=[bB, b_xev], writes=[b_xnew], tag=f"fev{d2}")
                    dma("gpsimd", dst[d2][:, c0:c0 + 512], xnew[:],
                        [b_xnew], [dst_buf], f"f{tag}xs{tile}_{d2}")
                    if fuse_stats:
                        first = (d2 == g0 == 0)
                        last = (d2 == NDC - 1)
                        P.add("scalar", lambda e: e.square(sq_sb[:], xnew[:]),
                              reads=[b_xnew], writes=[b_sq], tag=f"fq{d2}")
                        mm(banks[4 + tile][0:1, :], onec[:], xnew[:], first, last,
                           [b_xnew, b_const], b_bank[4 + tile], f"fs{d2}")
                        mm(banks[6 + tile][0:1, :], onec[:], sq_sb[:], first, last,
                           [b_sq, b_const], b_bank[6 + tile], f"fz{d2}")

    emit_stats_standalone(x_in)
    emit_mr("p1")
    emit_xsplit(x_in, "p1", b_xin)
    emit_ffn(x_in, x_dram, True, "1", b_xin, b_xdram, zip_cb=emit_quant_hs)

    if KDBG:
        for dc in range(NDC):
            dma("gpsimd", dbg_xa[dc], x_dram[dc], [b_xdram], [b_xout], f"dxa{dc}")
        dma("gpsimd", dbg_thr, thr[:], [b_thr], [b_xout], "dthr")
    emit_mr("p2")
    emit_xsplit(x_dram, "p2", b_xdram)
    if KDBG:
        dma("gpsimd", dbg_xs[0], xs_h[:], [b_xs], [b_xout], "dxs0")
        dma("gpsimd", dbg_xs[1], xs_l[:], [b_xs], [b_xout], "dxs1")

    P.add("tensor", lambda e: e.matmul(banks[6][:, 0:16], oner[:], ld_sb[0:1, :],
                                       start=True, stop=True),
          reads=[b_const], writes=[b_bank[6]], tag="ldb")
    P.add("scalar", lambda e: e.copy(ldb[:], banks[6][:, 0:16]),
          reads=[b_bank[6]], writes=[b_decay], tag="ldbc")
    DTm = hidp[:, 0:2048]
    r_DT = hreg(0, 2048)
    difft = hidp[:, 11520:11648]
    trit = hidp[:, 11648:11776]
    r_dtc = hreg(11520, 256)
    dma("sync", difft[:], c_diffm, r_dtc, r_dtc, "df")
    dma("sync", trit[:], c_tri, r_dtc, r_dtc, "tr")
    for h in range(H):
        P.add("scalar", lambda e, h=h: e.activation(
            qdcd[:, h * 8:(h + 1) * 8], ioci[:], AF.Exp, scale=ldb[:, h:h + 1]),
            reads=[b_const, b_decay], writes=[b_decay], tag=f"qd{h}")
        P.add("scalar", lambda e, h=h: e.activation(
            kdc[:, h:h + 1], iorev[:], AF.Exp, scale=ldb[:, h:h + 1]),
            reads=[b_const, b_decay], writes=[b_decay], tag=f"kd{h}")
        P.add("scalar", lambda e, h=h: e.activation(
            cdc[:, h:h + 1], c128t[:], AF.Exp, scale=ldb[:, h:h + 1]),
            reads=[b_const, b_decay], writes=[b_decay], tag=f"cd{h}")
        P.add("scalar", lambda e, h=h: e.activation(
            DTm[:, h * 128:(h + 1) * 128], difft[:], AF.Exp, scale=ldb[:, h:h + 1]),
            reads=r_dtc + [b_decay], writes=r_DT, tag=f"dt{h}")
        P.add("vector", lambda e, h=h: e.tensor_mul(
            DTm[:, h * 128:(h + 1) * 128], DTm[:, h * 128:(h + 1) * 128], trit[:]),
            reads=r_dtc + r_DT, writes=r_DT, tag=f"dm{h}")

    wg1s = hidp[:, 12288:14336]
    wg1h = hidp[:, 14336:15360].bitcast(BF16)
    wg1l = hidp[:, 15360:16384].bitcast(BF16)
    r_wg1s = hreg(12288, 2048)
    r_wg1 = hreg(14336, 2048)
    g1f = hidp[:, 16384:17408]
    r_g1f = hreg(16384, 1024)
    dma("sync", wg1s[:].rearrange("p (a g) -> p a g", a=NDC),
        wg1_in.rearrange("(a p) g -> p a g", p=128),
        r_wg1s, r_wg1s, "wg1l")
    P.add("scalar", lambda e: e.copy(wg1h[:], wg1s[:]), reads=r_wg1s, writes=r_wg1, tag="wg1h")
    P.add("vector", lambda e: e.tensor_sub(wg1l[:], wg1s[:], wg1h[:]),
          reads=r_wg1s + r_wg1, writes=r_wg1, tag="wg1lo")
    for h2 in range(2):
        pG, bG = banks[6], b_bank[6]
        for dc in range(NDC):
            xh = xs_h[:, dc * NT + h2 * 512:dc * NT + h2 * 512 + 512]
            xl = xs_l[:, dc * NT + h2 * 512:dc * NT + h2 * 512 + 512]
            w_h = wg1h[:, dc * 128:(dc + 1) * 128]
            w_l = wg1l[:, dc * 128:(dc + 1) * 128]
            mm(pG[:], w_h, xh, dc == 0, False, r_wg1 + [b_xs], bG, "g1")
            mm(pG[:], w_l, xh, False, False, r_wg1 + [b_xs], bG, "g1")
            mm(pG[:], w_h, xl, False, dc == NDC - 1, r_wg1 + [b_xs], bG, "g1")
        P.add("scalar", lambda e, pG=pG: e.copy(g1f[:, 0:512], pG[:]),
              reads=[bG], writes=r_g1f, tag=f"g1e{h2}")
        P.add("scalar", lambda e, h2=h2: e.copy(g1h[:, h2 * 512:(h2 + 1) * 512], g1f[:, 0:512]),
              reads=r_g1f, writes=[b_g1], tag=f"g1h{h2}")
        P.add("vector", lambda e, h2=h2: e.tensor_sub(
            g1l[:, h2 * 512:(h2 + 1) * 512], g1f[:, 0:512], g1h[:, h2 * 512:(h2 + 1) * 512]),
            reads=r_g1f + [b_g1], writes=[b_g1], tag=f"g1l{h2}")

    qkv_sets = []
    for par, base in ((0, 5120), (1, 16896)):
        qkv_sets.append((
            hidp[:, base:base + 1024], hreg(base, 1024),
            hidp[:, base + 1024:base + 2048], hreg(base + 1024, 1024),
            hidp[:, base + 2048:base + 3072], hreg(base + 2048, 1024),
        ))
    khat = hidp[:, 8192:8320]
    sTt = hidp[:, 8320:8448]
    otmp = hidp[:, 8448:8576]
    ogac = hidp[:, 8576:9600]
    ptmp = hidp[:, 10880:11392]
    r_sc = hreg(8192, 384)
    r_og = hreg(8576, 1024)
    r_pt = hreg(10880, 512)

    for h in range(H):
        qf, r_qf, kf, r_kf, vf, r_vf = qkv_sets[h % 2]
        wbufs = []
        for wi, w_ap in enumerate((wq_in, wk_in)):
            wh_b = strm[:, wi * 4096:wi * 4096 + 2048]
            wl_b = strm[:, wi * 4096 + 2048:wi * 4096 + 4096]
            rw = sreg(wi * 4096, 4096)
            sl = (h * 3 + wi) % 2
            wstg = hidp[:, 12288 + sl * 2048:12288 + sl * 2048 + 2048]
            r_ws = hreg(12288 + sl * 2048, 2048)
            dma("sync", wstg[:].rearrange("p (a e) -> p a e", a=NDC),
                w_ap.rearrange("(a p) e -> p a e", p=128)[:, :, h * 128:(h + 1) * 128],
                r_ws, r_ws, f"wl{h}_{wi}")
            P.add("scalar", lambda e, wh_b=wh_b, wstg=wstg: e.copy(wh_b[:], wstg[:]),
                  reads=r_ws, writes=rw, tag=f"wh{h}_{wi}")
            P.add("vector", lambda e, wh_b=wh_b, wl_b=wl_b, wstg=wstg: e.tensor_sub(
                wl_b[:], wstg[:], wh_b[:]),
                reads=r_ws + rw, writes=rw, tag=f"wlo{h}_{wi}")
            wbufs.append((wh_b, wl_b, rw))

        for wi, dstf, rdst in ((0, qf, r_qf), (1, kf, r_kf)):
            wh_b, wl_b, rw = wbufs[wi]
            for h2 in range(2):
                pP, bP = banks[h2], b_bank[h2]
                for dc in range(NDC):
                    xh = xs_h[:, dc * NT + h2 * 512:dc * NT + h2 * 512 + 512]
                    xl = xs_l[:, dc * NT + h2 * 512:dc * NT + h2 * 512 + 512]
                    w_h = wh_b[:, dc * 128:(dc + 1) * 128]
                    w_l = wl_b[:, dc * 128:(dc + 1) * 128]
                    mm(pP[:], w_h, xh, dc == 0, False, rw + [b_xs], bP, "qk")
                    mm(pP[:], w_l, xh, False, False, rw + [b_xs], bP, "qk")
                    mm(pP[:], w_h, xl, False, dc == NDC - 1, rw + [b_xs], bP, "qk")
                P.add("scalar", lambda e, pP=pP, dstf=dstf, h2=h2: e.activation(
                    dstf[:, h2 * 512:h2 * 512 + 512], pP[:], AF.Silu),
                    reads=[bP], writes=rdst, tag=f"pe{h}_{wi}_{h2}")
        dma("gpsimd", q_sp[:, h].rearrange("ch p c -> p ch c"),
            qf[:].rearrange("p (ch c) -> p ch c", ch=NCH),
            r_qf, [b_qspill], f"qsp{h}")
        vh_b = strm[:, 8192:10240]
        vl_b = strm[:, 10240:12288]
        rwv = b_q3s + [b_s[5]]
        sl = (h * 3 + 2) % 2
        wstg = hidp[:, 12288 + sl * 2048:12288 + sl * 2048 + 2048]
        r_ws = hreg(12288 + sl * 2048, 2048)
        dma("sync", wstg[:].rearrange("p (a e) -> p a e", a=NDC),
            wv_in.rearrange("(a p) e -> p a e", p=128)[:, :, h * 128:(h + 1) * 128],
            r_ws, r_ws, f"wl{h}_2")
        P.add("scalar", lambda e, wstg=wstg, vh_b=vh_b: e.copy(vh_b[:], wstg[:]),
              reads=r_ws, writes=rwv, tag=f"wh{h}_2")
        P.add("vector", lambda e, wstg=wstg, vh_b=vh_b, vl_b=vl_b: e.tensor_sub(
            vl_b[:], wstg[:], vh_b[:]),
            reads=r_ws + rwv, writes=rwv, tag=f"wlo{h}_2")
        for ch in range(NCH):
            pV, bV = banks[2 + ch % 2], b_bank[2 + ch % 2]
            for dc in range(NDC):
                xh = xs_h[:, dc * NT + ch * 128:dc * NT + ch * 128 + 128]
                xl = xs_l[:, dc * NT + ch * 128:dc * NT + ch * 128 + 128]
                w_h = vh_b[:, dc * 128:(dc + 1) * 128]
                w_l = vl_b[:, dc * 128:(dc + 1) * 128]
                mm(pV[:, 0:128], xh, w_h, dc == 0, False, rwv + [b_xs], bV, "v")
                mm(pV[:, 0:128], xl, w_h, False, False, rwv + [b_xs], bV, "v")
                mm(pV[:, 0:128], xh, w_l, False, dc == NDC - 1, rwv + [b_xs], bV, "v")
            P.add("scalar", lambda e, pV=pV, ch=ch, vf=vf: e.activation(
                vf[:, ch * 128:(ch + 1) * 128], pV[:, 0:128], AF.Silu),
                reads=[bV], writes=r_vf, tag=f"ve{h}_{ch}")

        P.add("vector", lambda e, h=h: e.memset(S_all[:, h * 128:(h + 1) * 128], 0.0),
              reads=[], writes=[b_S], tag=f"sz{h}")

        Ssl = S_all[:, h * 128:(h + 1) * 128]
        for ci in range(NCH):
            qsl = qf[:, ci * 128:(ci + 1) * 128]
            ksl = kf[:, ci * 128:(ci + 1) * 128]
            vsl = vf[:, ci * 128:(ci + 1) * 128]
            mm(banks[4][:, 0:128], ksl, qsl, True, True, r_kf + r_qf, b_bank[4], "sc")
            P.add("vector", lambda e, h=h: e.tensor_mul(
                sTt[:], banks[4][:, 0:128], DTm[:, h * 128:(h + 1) * 128]),
                reads=[b_bank[4]] + r_DT, writes=r_sc, tag=f"sm{h}_{ci}")
            mm(banks[5][:, 0:128], sTt[:], vsl, True, True, r_sc + r_vf,
               b_bank[5], "oi")
            if ci > 0:
                mm(banks[6][:, 0:128], qsl, Ssl, True, True, r_qf + [b_S],
                   b_bank[6], "ox")
                P.add("scalar", lambda e, h=h: e.activation(
                    otmp[:], banks[6][:, 0:128], AF.Copy,
                    scale=qdcd[:, h * 8:h * 8 + 1]),
                    reads=[b_bank[6], b_decay], writes=r_sc, tag=f"oxs{h}_{ci}")
                P.add("vector", lambda e, ci=ci: e.tensor_add(
                    ogac[:, ci * 128:(ci + 1) * 128], banks[5][:, 0:128], otmp[:]),
                    reads=[b_bank[5]] + r_sc, writes=r_og, tag=f"oev{h}_{ci}")
            else:
                P.add("vector", lambda e, ci=ci: e.tensor_copy(
                    ogac[:, ci * 128:(ci + 1) * 128], banks[5][:, 0:128]),
                    reads=[b_bank[5]], writes=r_og, tag=f"oev{h}_{ci}")
            P.add("tensor", lambda e, ksl=ksl: e.transpose(
                banks[7][:, 128:256], ksl, identt[:]),
                reads=r_kf + [b_const], writes=[b_bank[7]], tag=f"kt{h}_{ci}")
            P.add("scalar", lambda e, h=h: e.activation(
                khat[:], banks[7][:, 128:256], AF.Copy, scale=kdc[:, h:h + 1]),
                reads=[b_bank[7], b_decay], writes=r_sc, tag=f"ks{h}_{ci}")
            mm(banks[6][:, 128:256], khat[:], vsl, True, True, r_sc + r_vf,
               b_bank[6], "ds")
            P.add("vector", lambda e, h=h, Ssl=Ssl: e.scalar_tensor_tensor(
                Ssl, Ssl, cdc[:, h:h + 1], banks[6][:, 128:256],
                ALU.mult, ALU.add),
                reads=[b_S, b_decay, b_bank[6]], writes=[b_S], tag=f"su{h}_{ci}")
        dma("gpsimd", o_sp[:, :, h * 128:(h + 1) * 128].rearrange("ch p c -> p ch c"),
            ogac[:].rearrange("p (ch c) -> p ch c", ch=NCH),
            r_og, [b_osp], f"osp{h}")
        if KDBG and h == 0:
            dma("gpsimd", dbg_f[0][:, 0:1024], qf[:], r_qf, [b_xout], "dq0")
            dma("gpsimd", dbg_f[1][:, 0:2048].bitcast(BF16), strm[:, 8192:12288],
                b_q3s + [b_s[5]], [b_xout], "dwv")
            dma("gpsimd", dbg_f[2][:, 0:1024], vf[:], r_vf, [b_xout], "dv0")
            dma("gpsimd", dbg_f[3][:, 0:128], S_all[:, 0:128], [b_S], [b_xout], "dS0")

    dma("gpsimd", cc_bin, S_all[:], [b_S], [b_ccb], "bw")
    P.add("gpsimd", lambda e: e.collective_compute(
        "AllGather", ALU.bypass, replica_groups=[list(range(NCORES))],
        ins=[cc_bin], outs=[cc_bout]),
        reads=[b_ccb], writes=[b_ccb], cc=True, tag="bcc")
    if KDBG:
        for j in range(NCORES):
            dma("gpsimd", dbg_B[j], cc_bout[j], [b_ccb], [b_xout], f"dB{j}")
        for ci in range(NCH):
            dma("gpsimd", dbg_osp[ci], o_sp[ci], [b_osp], [b_xout], f"dO{ci}")
            dma("gpsimd", dbg_qsp[ci], q_sp[ci], [b_qspill], [b_xout], f"dQ{ci}")
    ldrep = hidp[0:1, 0:2048]
    LD2048 = hidp[:, 2048:4096]
    Bj = hidp[:, 4096:6144]
    FBj = hidp[:, 6144:8192]
    r_ldr = hreg(0, 2048)
    r_LD = hreg(2048, 2048)
    r_Bj = hreg(4096, 2048)
    r_FB = hreg(6144, 2048)
    P.add("vector", lambda e: e.tensor_copy(
        ldrep.rearrange("q (h r) -> q h r", h=16),
        ld_sb[0:1, :].rearrange("q h -> q h 1" if False else "q (h r) -> q h r", r=1
                                ).broadcast_to([1, 16, 128])),
        reads=[b_const], writes=r_ldr, tag="ldrep")
    for h4 in range(4):
        P.add("tensor", lambda e, h4=h4: e.matmul(
            banks[6][:, :], oner[:], ldrep[0:1, h4 * 512:(h4 + 1) * 512],
            start=True, stop=True),
            reads=r_ldr + [b_const], writes=[b_bank[6]], tag=f"ldm{h4}")
        P.add("scalar", lambda e, h4=h4: e.copy(LD2048[:, h4 * 512:(h4 + 1) * 512],
                                                banks[6][:, :]),
              reads=[b_bank[6]], writes=r_LD, tag=f"ldc{h4}")
    P.add("vector", lambda e: e.memset(S_all[:], 0.0), reads=[], writes=[b_S], tag="sinz")
    for j in range(NCORES):
        P.add("vector", lambda e, j=j: e.tensor_scalar(
            tjc[:, 0:1], cidc[:], 1024.0, (j + 1) * 1024.0, ALU.mult, ALU.subtract),
            reads=[b_cid], writes=[b_cid], tag=f"tj{j}")
        P.add("vector", lambda e: e.tensor_single_scalar(
            tjc[:, 1:2], tjc[:, 0:1], 0.0, ALU.is_ge),
            reads=[b_cid], writes=[b_cid], tag=f"tm{j}")
        P.add("vector", lambda e: e.tensor_scalar(
            tjc[:, 2:3], tjc[:, 1:2], -1e6, 1e6, ALU.mult, ALU.add),
            reads=[b_cid], writes=[b_cid], tag=f"ta{j}")
        P.add("vector", lambda e: e.tensor_add(tjc[:, 0:1], tjc[:, 0:1], tjc[:, 2:3]),
              reads=[b_cid], writes=[b_cid], tag=f"tb{j}")
        P.add("scalar", lambda e: e.activation(FBj[:], LD2048[:], AF.Exp,
                                               scale=tjc[:, 0:1]),
              reads=r_LD + [b_cid], writes=r_FB, tag=f"fb{j}")
        dma("sync", Bj[:], cc_bout[j], [b_ccb] + r_Bj, r_Bj, f"bl{j}")
        P.add("vector", lambda e: e.tensor_mul(Bj[:], Bj[:], FBj[:]),
              reads=r_FB + r_Bj, writes=r_Bj, tag=f"bm{j}")
        P.add("vector", lambda e: e.tensor_add(S_all[:], S_all[:], Bj[:]),
              reads=r_Bj + [b_S], writes=[b_S], tag=f"bs{j}")

    if KDBG:
        dma("gpsimd", dbg_Sin, S_all[:], [b_S], [b_xout], "dSin")
    ogf = hidp[:, 0:8192]
    och = hidp[:, 16384:18432]
    sg = hidp[:, 18432:20480]
    qc = hidp[:, 20480:22528]
    woF = hidp[:, 12288:16384]
    ogbh = hidp[:, 0:4096].bitcast(BF16)
    ogbl = hidp[:, 5120:9216].bitcast(BF16)
    wobh = strm[:, 4096:8192]
    wobl = strm[:, 8192:12288]
    r_ogf = hreg(0, 9216)
    r_wob = sreg(4096, 8192)
    r_och = hreg(16384, 2048)
    r_sg = hreg(18432, 2048)
    r_qc = hreg(20480, 2048)
    r_wo = hreg(12288, 4096)
    wg2h = strm[:, 0:2048]
    wg2l = strm[:, 2048:4096]
    r_wg2 = sreg(0, 4096)
    for seg in range(4):
        dma("sync", ptmp[:, 0:512], wg2_in[:, seg * 512:(seg + 1) * 512], r_pt, r_pt, f"wg2_{seg}")
        P.add("scalar", lambda e, seg=seg: e.copy(
            wg2h[:, seg * 512:(seg + 1) * 512], ptmp[:, 0:512]),
            reads=r_pt, writes=r_wg2, tag=f"wg2h{seg}")
        P.add("vector", lambda e, seg=seg: e.tensor_sub(
            wg2l[:, seg * 512:(seg + 1) * 512], ptmp[:, 0:512],
            wg2h[:, seg * 512:(seg + 1) * 512]),
            reads=r_pt + r_wg2, writes=r_wg2, tag=f"wg2l{seg}")

    for hf in range(2):
        for ci in range(hf * 4, hf * 4 + 4):
            cl = ci - hf * 4
            dma("sync", och[:], o_sp[ci], [b_osp] + r_och, r_och, f"ol{ci}")
            dma("sync", qc[:].rearrange("p (h c) -> p h c", h=H),
                q_sp[ci].rearrange("h p c -> p h c"),
                [b_qspill] + r_qc, r_qc, f"ql{ci}")
            for d4 in range(4):
                pG, bG = banks[2 + d4 % 2], b_bank[2 + d4 % 2]
                g1hs = g1h[:, ci * 128:(ci + 1) * 128]
                g1ls = g1l[:, ci * 128:(ci + 1) * 128]
                mm(pG[:], g1hs, wg2h[:, d4 * 512:(d4 + 1) * 512], True, False,
                   [b_g1] + r_wg2, bG, "gt")
                mm(pG[:], g1ls, wg2h[:, d4 * 512:(d4 + 1) * 512], False, False,
                   [b_g1] + r_wg2, bG, "gt")
                mm(pG[:], g1hs, wg2l[:, d4 * 512:(d4 + 1) * 512], False, True,
                   [b_g1] + r_wg2, bG, "gt")
                P.add("scalar", lambda e, pG=pG, d4=d4: e.activation(
                    sg[:, d4 * 512:(d4 + 1) * 512], pG[:], AF.Sigmoid),
                    reads=[bG], writes=r_sg, tag=f"sig{ci}_{d4}")
            for h in range(H):
                pC, bC = banks[h % 2], b_bank[h % 2]
                mm(pC[:, 0:128], qc[:, h * 128:(h + 1) * 128],
                   S_all[:, h * 128:(h + 1) * 128],
                   True, True, r_qc + [b_S], bC, "corr")
                P.add("vector", lambda e, pC=pC, h=h, ci=ci: e.scalar_tensor_tensor(
                    och[:, h * 128:(h + 1) * 128], pC[:, 0:128],
                    qdcd[:, h * 8 + ci:h * 8 + ci + 1], och[:, h * 128:(h + 1) * 128],
                    ALU.mult, ALU.add),
                    reads=[bC, b_decay] + r_och, writes=r_och, tag=f"cadd{ci}_{h}")
            if KDBG:
                dma("gpsimd", dbg_och[ci], och[:], r_och, [b_xout], f"doch{ci}")
            P.add("vector", lambda e: e.tensor_mul(och[:], och[:], sg[:]),
                  reads=r_sg + r_och, writes=r_och, tag=f"om{ci}")
            for q4 in range(4):
                P.add("scalar", lambda e, q4=q4: e.square(
                    sg[:, q4 * 512:(q4 + 1) * 512], och[:, q4 * 512:(q4 + 1) * 512]),
                    reads=r_och, writes=r_sg, tag=f"rs{q4}{ci}")
            P.add("vector", lambda e: e.tensor_reduce(
                gacc[:, 0:16], sg[:].rearrange("p (h c) -> p h c", h=16), AX, ALU.add),
                reads=r_sg, writes=[b_gacc], tag=f"rr{ci}")
            P.add("scalar", lambda e: e.activation(gacc[:, 16:32], gacc[:, 0:16],
                                                   AF.Sqrt, scale=1.0 / HD, bias=EPS_RMS),
                  reads=[b_gacc, b_const], writes=[b_gacc], tag=f"rq{ci}")
            P.add("vector", lambda e: e.reciprocal(gacc[:, 16:32], gacc[:, 16:32]),
                  reads=[b_gacc], writes=[b_gacc], tag=f"rc{ci}")
            for h in range(H):
                P.add("scalar", lambda e, h=h: e.activation(
                    sg[:, h * 128:(h + 1) * 128], och[:, h * 128:(h + 1) * 128],
                    AF.Copy, scale=gacc[:, 16 + h:17 + h]),
                    reads=r_och + [b_gacc], writes=r_sg, tag=f"on{ci}_{h}")
                bkt = 2 + (h % 2)
                P.add("tensor", lambda e, h=h, bkt=bkt: e.transpose(
                    banks[bkt][:, 0:128], sg[:, h * 128:(h + 1) * 128], identt[:]),
                    reads=r_sg + [b_const], writes=[b_bank[bkt]], tag=f"ot{ci}_{h}")
                P.add("scalar", lambda e, h=h, cl=cl, bkt=bkt: e.copy(
                    ogbh[:, h * 512 + cl * 128:h * 512 + cl * 128 + 128],
                    banks[bkt][:, 0:128]),
                    reads=[b_bank[bkt]], writes=r_ogf, tag=f"oh{ci}_{h}")
                P.add("vector", lambda e, h=h, cl=cl, bkt=bkt: e.tensor_sub(
                    ogbl[:, h * 512 + cl * 128:h * 512 + cl * 128 + 128],
                    banks[bkt][:, 0:128],
                    ogbh[:, h * 512 + cl * 128:h * 512 + cl * 128 + 128]),
                    reads=[b_bank[bkt]] + r_ogf, writes=r_ogf, tag=f"ol{ci}_{h}")

        for eg in range(8):
            for dc in range(NDC):
                wsl = hidp[:, 9216 + (dc % 3) * 1024:9216 + (dc % 3) * 1024 + 256]
                r_wsl = hreg(9216 + (dc % 3) * 1024, 256)
                dma("sync", wsl[:],
                    wo_in[dc * 128:(dc + 1) * 128, eg * 256:(eg + 1) * 256],
                    r_wsl, r_wsl, f"wol{hf}_{eg}_{dc}")
                P.add("scalar", lambda e, dc=dc, wsl=wsl: e.activation(
                    woF[:, dc * 256:(dc + 1) * 256], wsl[:], AF.Copy,
                    scale=nwc[:, dc:dc + 1]),
                    reads=r_wsl + [b_nw], writes=r_wo, tag=f"woh{hf}_{eg}_{dc}")
            P.add("scalar", lambda e: e.copy(wobh[:], woF[:]),
                  reads=r_wo, writes=r_wob, tag=f"wbh{hf}_{eg}")
            P.add("vector", lambda e: e.tensor_sub(wobl[:], woF[:], wobh[:]),
                  reads=r_wo + r_wob, writes=r_wob, tag=f"wbl{hf}_{eg}")
            if KDBG and hf == 0 and eg == 0:
                dma("gpsimd", dbg_f[0][:, 0:2048], woF[:, 0:2048], r_wo, [b_xout], "dwoF0")
                dma("gpsimd", dbg_f[1][:, 0:2048], woF[:, 2048:4096], r_wo, [b_xout], "dwoF1")
            for es in range(2):
                e_abs = eg * 2 + es
                pW, bW = banks[es], b_bank[es]
                for dc in range(NDC):
                    ogh_s = ogbh[:, dc * 512:dc * 512 + 512]
                    ogl_s = ogbl[:, dc * 512:dc * 512 + 512]
                    wh_s = wobh[:, dc * 256 + es * 128:dc * 256 + es * 128 + 128]
                    wl_s = wobl[:, dc * 256 + es * 128:dc * 256 + es * 128 + 128]
                    mm(pW[:], wh_s, ogh_s, dc == 0, False, r_wob + r_ogf, bW, "wo")
                    mm(pW[:], wl_s, ogh_s, False, False, r_wob + r_ogf, bW, "wo")
                    mm(pW[:], wh_s, ogl_s, False, dc == NDC - 1, r_wob + r_ogf, bW, "wo")
                dma("sync", xev[:], x_dram[e_abs][:, hf * 512:hf * 512 + 512],
                    [b_xev, b_xdram], [b_xev], f"axl{e_abs}_{hf}")
                P.add("vector", lambda e, pW=pW: e.tensor_add(xnew[:], pW[:], xev[:]),
                      reads=[bW, b_xev], writes=[b_xnew], tag=f"aev{e_abs}_{hf}")
                dma("gpsimd", x_dram[e_abs][:, hf * 512:hf * 512 + 512], xnew[:],
                    [b_xnew], [b_xdram], f"axs{e_abs}_{hf}")
                first = (e_abs == 0)
                last = (e_abs == NDC - 1)
                P.add("scalar", lambda e: e.square(sq_sb[:], xnew[:]),
                      reads=[b_xnew], writes=[b_sq], tag=f"asq{e_abs}_{hf}")
                mm(banks[4 + hf][0:1, :], onec[:], xnew[:], first, last,
                   [b_xnew, b_const], b_bank[4 + hf], f"as{e_abs}_{hf}")
                mm(banks[6 + hf][0:1, :], onec[:], sq_sb[:], first, last,
                   [b_sq, b_const], b_bank[6 + hf], f"az{e_abs}_{hf}")

    if KDBG:
        for dc in range(NDC):
            dma("gpsimd", dbg_xb[dc], x_dram[dc], [b_xdram], [b_xout], f"dxb{dc}")
    emit_mr("p3")
    emit_xsplit(x_dram, "p3", b_xdram)
    emit_ffn(x_dram, x_out, False, "2", b_xdram, b_xout)

    P.add("sync", lambda e: e.nop(), reads=[b_xout], writes=[], tag="final")

    _LAST_P[0] = P
    P.build(nc)
    with nc.Block() as block:
        P.emit(block)
    return nc


_NC_CACHE = [None]
_LAST_RES = [None]
_LAST_P = [None]


def kernel(L0_attn_x, L0_attn_log_decay, L0_attn_Wq, L0_attn_Wk, L0_attn_Wv,
           L0_attn_Wo, L0_attn_Wg1, L0_attn_Wg2, L0_attn_norm_w,
           L0_ffn_w1, L0_ffn_w2, L0_ffn_w3):
    x = np.asarray(L0_attn_x, dtype=np.float32)
    w1 = np.ascontiguousarray(np.asarray(L0_ffn_w1, np.float32))
    w2 = np.ascontiguousarray(np.asarray(L0_ffn_w2, np.float32))
    w3 = np.ascontiguousarray(np.asarray(L0_ffn_w3, np.float32))
    shared = {
        "ld_in": np.ascontiguousarray(np.asarray(L0_attn_log_decay, np.float32)[None, :]),
        "wq_in": np.ascontiguousarray(np.asarray(L0_attn_Wq, np.float32)),
        "wk_in": np.ascontiguousarray(np.asarray(L0_attn_Wk, np.float32)),
        "wv_in": np.ascontiguousarray(np.asarray(L0_attn_Wv, np.float32)),
        "wo_in": np.ascontiguousarray(np.asarray(L0_attn_Wo, np.float32)),
        "wg1_in": np.ascontiguousarray(np.asarray(L0_attn_Wg1, np.float32)),
        "wg2_in": np.ascontiguousarray(np.asarray(L0_attn_Wg2, np.float32)),
        "nw_in": np.ascontiguousarray(
            np.asarray(L0_attn_norm_w, np.float32).reshape(NDC, 128).T),
        "w1_in": w1, "w2_in": w2, "w3_in": w3,
    }
    if _NC_CACHE[0] is None:
        _NC_CACHE[0] = build_nc()
    nc = _NC_CACHE[0]
    in_maps = []
    for c in range(NCORES):
        xs = np.ascontiguousarray(
            x[0, c * NT:(c + 1) * NT, :].T.reshape(NDC, 128, NT))
        m = dict(shared)
        m["x_in"] = xs
        m["w1g_in"] = np.ascontiguousarray(w1[c * 704:(c + 1) * 704])
        m["w2g_in"] = np.ascontiguousarray(w2[c * 704:(c + 1) * 704])
        m["w3g_in"] = np.ascontiguousarray(w3[:, c * 704:(c + 1) * 704])
        m["cid_in"] = np.array([[float(c)]], dtype=np.float32)
        in_maps.append(m)
    res = run_bass_kernel_spmd(nc, in_maps, core_ids=list(range(NCORES)))
    _LAST_RES[0] = res
    out = np.empty((1, NCORES * NT, D), dtype=np.float32)
    for c in range(NCORES):
        o = res.results[c]["x_out"].reshape(D, NT)
        out[0, c * NT:(c + 1) * NT, :] = o.T
    return out


# version 21 (final)
# speedup vs baseline: 1.0710x; 1.0281x over previous
import os
import numpy as np
import ml_dtypes
import concourse.bass as bass
import concourse.mybir as mybir
from concourse.bass_utils import run_bass_kernel_spmd

F32 = mybir.dt.float32
BF16 = mybir.dt.bfloat16
AF = mybir.ActivationFunctionType
ALU = mybir.AluOpType
AX = mybir.AxisListType.X

NCORES = 8
NT = 1024
D = 2048
H = 16
HD = 128
HID = 5632
NHS = 44
NDC = 16
NCH = 8
GCNT = float(HID * D)
EPS_LN = 1e-5
EPS_RMS = 1e-5
EPS_Q = 1e-8


class Buf:
    __slots__ = ("name", "last_writer", "readers")

    def __init__(self, name):
        self.name = name
        self.last_writer = None
        self.readers = []


class Prog:
    def __init__(self):
        self.ops = []

    def add(self, eng, fn, reads=(), writes=(), dma=False, cc=False, tag=""):
        ch = "cc" if cc else (eng + "_dma" if dma else eng)
        raw = set()
        war = set()
        for b in reads:
            if b.last_writer is not None:
                raw.add(b.last_writer)
        for b in writes:
            if b.last_writer is not None:
                war.add(b.last_writer)
            war.update(b.readers)
        if not hasattr(self, "chpos"):
            self.chpos = {}
        pos = self.chpos.get(ch, 0)
        self.chpos[ch] = pos + 1
        op = {"id": len(self.ops), "eng": eng, "ch": ch, "fn": fn, "dma": dma,
              "cc": cc, "raw": raw, "war": war, "tag": tag, "inc": False,
              "val": 0, "pos": pos}
        self.ops.append(op)
        oid = op["id"]
        for b in reads:
            b.readers.append(oid)
        for b in writes:
            b.last_writer = oid
            b.readers = []
        return oid

    def _needs_edge(self, op, dop, is_raw):
        if dop["ch"] != op["ch"] or dop["dma"] or dop["cc"]:
            return True
        if dop["ch"] == "tensor":
            return False
        return is_raw and (op["pos"] - dop["pos"] < int(os.environ.get("RAWD", "6")))

    def build(self, nc):
        ops = self.ops
        if os.environ.get("DMASER", "0") == "1":
            last_dma = {}
            for op in ops:
                if op["dma"]:
                    if op["ch"] in last_dma:
                        op["raw"].add(last_dma[op["ch"]])
                    last_dma[op["ch"]] = op["id"]
        NSLOT = 8
        for op in ops:
            if op["dma"]:
                op["inc"] = True
            for d in op["raw"]:
                if self._needs_edge(op, ops[d], True):
                    ops[d]["inc"] = True
            for d in op["war"]:
                if self._needs_edge(op, ops[d], False):
                    ops[d]["inc"] = True
        counters = {}
        slot_use = {}
        for op in ops:
            if op["dma"]:
                ch = op["ch"]
                idx = slot_use.setdefault(ch, 0)
                slot_use[ch] = idx + 1
                slot = idx % NSLOT
                op["sem"] = f"{ch}{slot}"
                op["val"] = 16 * (idx // NSLOT + 1)
                op["prewait"] = (op["sem"], op["val"] - 16) if op["val"] > 16 else None
            elif op["inc"]:
                ch = op["ch"]
                counters[ch] = counters.get(ch, 0) + 1
                op["sem"] = ch
                op["val"] = counters[ch]
        for op in ops:
            waits = {}
            if op.get("prewait"):
                sm, v = op["prewait"]
                if v > 0:
                    waits[sm] = max(waits.get(sm, 0), v)
            for d, isr in [(d, True) for d in op["raw"]] + [(d, False) for d in op["war"]]:
                dop = ops[d]
                if not self._needs_edge(op, dop, isr):
                    continue
                waits[dop["sem"]] = max(waits.get(dop["sem"], 0), dop["val"])
            op["waits"] = waits
        sem_names = set()
        for op in ops:
            if op["inc"]:
                sem_names.add(op["sem"])
            sem_names.update(op["waits"].keys())
        self.sems = {n: nc.alloc_semaphore("sem_" + n) for n in sorted(sem_names)}

    def emit(self, block):
        sems = self.sems

        def body_for(engine_name):
            def body(eng):
                hw = {}
                for op in self.ops:
                    if op["eng"] != engine_name:
                        continue
                    for ch, v in sorted(op["waits"].items()):
                        if hw.get(ch, 0) < v:
                            eng.wait_ge(sems[ch], v)
                            hw[ch] = v
                    inst = op["fn"](eng)
                    if op["inc"]:
                        assert inst is not None, op["tag"]
                        inst.then_inc(sems[op["sem"]], 16 if op["dma"] else 1)
            return body

        block.tensor(body_for("tensor"))
        block.vector(body_for("vector"))
        block.scalar(body_for("scalar"))
        block.gpsimd(body_for("gpsimd"))
        block.sync(body_for("sync"))


def build_nc():
    nc = bass.Bass("TRN2", target_bir_lowering=False,
                   detect_race_conditions=False)
    P = Prog()

    x_in = nc.dram_tensor("x_in", [NDC, 128, NT], F32, kind="ExternalInput").ap()
    ld_in = nc.dram_tensor("ld_in", [1, H], F32, kind="ExternalInput").ap()
    wq_in = nc.dram_tensor("wq_in", [D, D], F32, kind="ExternalInput").ap()
    wk_in = nc.dram_tensor("wk_in", [D, D], F32, kind="ExternalInput").ap()
    wv_in = nc.dram_tensor("wv_in", [D, D], F32, kind="ExternalInput").ap()
    wo_in = nc.dram_tensor("wo_in", [D, D], F32, kind="ExternalInput").ap()
    wg1_in = nc.dram_tensor("wg1_in", [D, HD], F32, kind="ExternalInput").ap()
    wg2_in = nc.dram_tensor("wg2_in", [HD, D], F32, kind="ExternalInput").ap()
    nw_in = nc.dram_tensor("nw_in", [128, NDC], F32, kind="ExternalInput").ap()
    w1_in = nc.dram_tensor("w1_in", [HID, D], F32, kind="ExternalInput").ap()
    w2_in = nc.dram_tensor("w2_in", [HID, D], F32, kind="ExternalInput").ap()
    w3_in = nc.dram_tensor("w3_in", [D, HID], F32, kind="ExternalInput").ap()
    w1g_in = nc.dram_tensor("w1g_in", [HID // 8, D], F32, kind="ExternalInput").ap()
    w2g_in = nc.dram_tensor("w2g_in", [HID // 8, D], F32, kind="ExternalInput").ap()
    w3g_in = nc.dram_tensor("w3g_in", [D, HID // 8], F32, kind="ExternalInput").ap()
    cid_in = nc.dram_tensor("cid_in", [1, 1], F32, kind="ExternalInput").ap()
    x_out = nc.dram_tensor("x_out", [NDC, 128, NT], F32, kind="ExternalOutput").ap()
    KDBG = os.environ.get("KDBG", "") == "1"
    if KDBG:
        dbg_xa = nc.dram_tensor("dbg_xa", [NDC, 128, NT], F32, kind="ExternalOutput").ap()
        dbg_xb = nc.dram_tensor("dbg_xb", [NDC, 128, NT], F32, kind="ExternalOutput").ap()
        dbg_osp = nc.dram_tensor("dbg_osp", [NCH, 128, D], F32, kind="ExternalOutput").ap()
        dbg_qsp = nc.dram_tensor("dbg_qsp", [NCH, H, 128, 128], F32, kind="ExternalOutput").ap()
        dbg_B = nc.dram_tensor("dbg_B", [NCORES, 128, D], F32, kind="ExternalOutput").ap()
        dbg_thr = nc.dram_tensor("dbg_thr", [128, 8], F32, kind="ExternalOutput").ap()
        dbg_xs = nc.dram_tensor("dbg_xs", [2, 128, NDC * NT], BF16, kind="ExternalOutput").ap()
        dbg_gacc = nc.dram_tensor("dbg_gacc", [128, 48], F32, kind="ExternalOutput").ap()
        dbg_Sin = nc.dram_tensor("dbg_Sin", [128, D], F32, kind="ExternalOutput").ap()
        dbg_f = nc.dram_tensor("dbg_f", [4, 128, D], F32, kind="ExternalOutput").ap()
        dbg_och = nc.dram_tensor("dbg_och", [NCH, 128, D], F32, kind="ExternalOutput").ap()
        dbg_grow = nc.dram_tensor("dbg_grow", [1, 8], F32, kind="ExternalOutput").ap()

    _i = np.arange(128, dtype=np.float32)
    diff = _i[None, :] - _i[:, None]
    tri = (diff >= 0).astype(np.float32)
    diffm = diff * tri
    c_ident = nc.inline_tensor(np.eye(128, dtype=np.float32), "c_ident").ap()
    c_identb = nc.inline_tensor(np.eye(128, dtype=np.float32).astype(ml_dtypes.bfloat16),
                                "c_identb").ap()
    c_onec = nc.inline_tensor(np.ones((128, 1), np.float32), "c_onec").ap()
    c_oner = nc.inline_tensor(np.ones((1, 128), np.float32), "c_oner").ap()
    c_diffm = nc.inline_tensor(diffm, "c_diffm").ap()
    c_tri = nc.inline_tensor(tri, "c_tri").ap()
    c_iop1 = nc.inline_tensor((_i + 1.0)[:, None], "c_iop1").ap()
    c_iorev = nc.inline_tensor((127.0 - _i)[:, None], "c_iorev").ap()
    c_ioci = nc.inline_tensor(
        _i[:, None] + 1.0 + 128.0 * np.arange(8, dtype=np.float32)[None, :],
        "c_ioci").ap()
    c_128 = nc.inline_tensor(np.full((128, 1), 128.0, np.float32), "c_128").ap()
    sel = np.zeros((128, 16), np.float32)
    for j in range(4):
        sel[:, 4 * j + j] = 1.0
    c_sel4 = nc.inline_tensor(sel, "c_sel4").ap()
    FCONST_VALS = [1.0 / D, -1.0, EPS_LN, 0.5 / GCNT, 0.5 * EPS_Q, 1.0 / HD,
                   1.0, 0.0]
    c_fc = nc.inline_tensor(
        np.tile(np.asarray(FCONST_VALS, np.float32)[None, :], (128, 1)),
        "c_fc").ap()

    q1t_sp = nc.dram_tensor("q1t_sp", [NHS, NDC, 128, 128], BF16).ap()
    q2t_sp = nc.dram_tensor("q2t_sp", [NHS, NDC, 128, 128], BF16).ap()
    q3t_sp = nc.dram_tensor("q3t_sp", [NHS, NDC, 128, 128], BF16).ap()
    x_dram = nc.dram_tensor("x_dram", [NDC, 128, NT], F32).ap()
    o_sp = nc.dram_tensor("o_sp", [NCH, 128, D], F32).ap()
    q_sp = nc.dram_tensor("q_sp", [NCH, H, 128, 128], F32).ap()
    cc_gin = nc.dram_tensor("cc_gin", [1, 4], F32).ap()
    cc_gout = nc.dram_tensor("cc_gout", [1, 4], F32).ap()
    cc_bin = nc.dram_tensor("cc_bin", [128, D], F32).ap()
    cc_bout = nc.dram_tensor("cc_bout", [NCORES, 128, D], F32).ap()

    sb = lambda n, s, d=F32: nc.alloc_sbuf_tensor(n, s, d)
    xs_h = sb("xs_h", [128, NDC * NT], BF16)
    xs_l = sb("xs_l", [128, NDC * NT], BF16)
    hidp = sb("hidp", [128, 22528])
    strm = sb("strm", [128, 12288], BF16)
    S_all = sb("S_all", [128, D])
    g1h = sb("g1h", [128, NT], BF16)
    g1l = sb("g1l", [128, NT], BF16)
    Mb = sb("Mb", [128, NT])
    Rb = sb("Rb", [128, NT])
    xt_sb = sb("xt_sb", [128, 512])
    xev = sb("xev", [128, 512])
    xnew = sb("xnew", [128, 512])
    sq_sb = sb("sq_sb", [128, 512])
    gacc = sb("gacc", [128, 48])
    identt = sb("identt", [128, 128])
    identb = sb("identb", [128, 128], BF16)
    onec = sb("onec", [128, 1])
    oner = sb("oner", [1, 128])
    iop1 = sb("iop1", [128, 1])
    iorev = sb("iorev", [128, 1])
    ioci = sb("ioci", [128, 8])
    c128t = sb("c128t", [128, 1])
    fconst = sb("fconst", [128, 8])
    ld_sb = sb("ld_sb", [1, 16])
    ldb = sb("ldb", [128, 16])
    qdcd = sb("qdcd", [128, H * 8])
    kdc = sb("kdc", [128, H])
    cdc = sb("cdc", [128, H])
    nwc = sb("nwc", [128, NDC])
    thr = sb("thr", [128, 8])
    cidc = sb("cidc", [128, 1])
    tjc = sb("tjc", [128, 4])
    grow = sb("grow", [1, 8])

    banks = [nc.alloc_psum_tensor(f"bank{i}", [128, 512], F32) for i in range(8)]
    for _j, _v in enumerate(FCONST_VALS):
        nc.const_aps.register(F32, float(_v), fconst[:, _j:_j + 1])

    b_xs = Buf("xs")
    b_h = [Buf(f"h{i}") for i in range(22)]
    b_s = [Buf(f"s{i}") for i in range(6)]
    b_S = Buf("S")
    b_g1 = Buf("g1")
    b_const = Buf("const")
    b_decay = Buf("decay")
    b_MR = Buf("MR")
    b_grow = Buf("grow")
    b_thr = Buf("thr")
    b_bank = [Buf(f"bank{i}") for i in range(8)]
    b_xin = Buf("xin")
    b_xdram = Buf("xdram")
    b_xout = Buf("xout")
    b_qsp = [[Buf(f"q{w}sp{h}") for h in range(NHS)] for w in range(3)]
    b_osp = Buf("osp")
    b_qspill = Buf("qspill")
    b_ccg = Buf("ccg")
    b_ccb = Buf("ccb")
    b_xt = Buf("xt")
    b_xev = Buf("xev")
    b_xnew = Buf("xnew")
    b_sq = Buf("sq")
    b_gacc = Buf("gacc")
    b_nw = Buf("nw")
    b_cid = Buf("cid")

    def hreg(lo_f32, n_f32):
        i0 = lo_f32 // 1024
        i1 = (lo_f32 + n_f32 + 1023) // 1024
        return [b_h[i] for i in range(i0, i1)]

    def sreg(lo_b, n_b):
        i0 = lo_b // 2048
        i1 = (lo_b + n_b + 2047) // 2048
        return [b_s[i] for i in range(i0, i1)]

    def dma(q, out, in_, reads, writes, tag=""):
        return P.add(q, lambda e, o=out, i=in_: e.dma_start(out=o, in_=i),
                     reads=reads, writes=writes, dma=True, tag=tag)

    def dma_tr(out, in_, reads, writes, tag=""):
        return P.add("scalar", lambda e, o=out, i=in_: e.dma_start(out=o, in_=i, transpose=True),
                     reads=reads, writes=writes, dma=True, tag=tag)

    def mm(ps, lhsT, rhs, st, sp, reads, wbank, tag=""):
        return P.add("tensor", lambda e, p=ps, l=lhsT, r=rhs, a=st, b=sp:
                     e.matmul(p, l, r, start=a, stop=b),
                     reads=reads, writes=[wbank], tag=tag)

    for dst, src, t in [(identt[:], c_ident, "id"), (onec[:], c_onec, "oc"),
                        (oner[:], c_oner, "or"), (iop1[:], c_iop1, "i1"),
                        (iorev[:], c_iorev, "ir"), (ioci[:], c_ioci, "ic"),
                        (c128t[:], c_128, "c128"),
                        (fconst[:], c_fc, "fc"), (identb[:], c_identb, "idb"),
                        (ld_sb[:], ld_in, "ld")]:
        dma("sync", dst, src, [], [b_const], t)
    dma("sync", nwc[:], nw_in, [], [b_nw], "nw")
    dma("sync", grow[0:1, 0:1], cid_in, [], [b_grow], "cid0")
    P.add("tensor", lambda e: e.matmul(banks[6][:, 0:1], oner[:], grow[0:1, 0:1],
                                       start=True, stop=True),
          reads=[b_grow, b_const], writes=[b_bank[6]], tag="cidbc")
    P.add("scalar", lambda e: e.copy(cidc[:], banks[6][:, 0:1]),
          reads=[b_bank[6]], writes=[b_cid], tag="cidcp")

    gstage = hidp[:, 0:2048]
    r_gst = hreg(0, 2048)

    def emit_gamma(w_ap, idx):
        cbase = idx * 12
        t = w_ap.rearrange("(a p) d -> a p d", p=64)
        for a in range(11):
            dma("sync", gstage[0:64, 0:D], t[a], r_gst, r_gst, f"gld{idx}_{a}")
            P.add("vector", lambda e, a=a: e.tensor_reduce(
                gacc[0:64, cbase + a:cbase + a + 1], gstage[0:64, 0:D], AX,
                ALU.add, apply_absolute_value=True),
                reads=r_gst, writes=[b_gacc], tag=f"gr{idx}_{a}")
        P.add("vector", lambda e: e.tensor_reduce(
            gacc[0:64, cbase + 11:cbase + 12], gacc[0:64, cbase:cbase + 11],
            AX, ALU.add), reads=[b_gacc], writes=[b_gacc], tag=f"gs{idx}")
        mm(banks[4 + idx][0:1, 0:1], onec[0:64, 0:1],
           gacc[0:64, cbase + 11:cbase + 12], True, True,
           [b_gacc, b_const], b_bank[4 + idx], f"gm{idx}")

    emit_gamma(w1g_in, 0)
    emit_gamma(w2g_in, 1)
    t3 = w3g_in.rearrange("(a p) h -> a p h", p=128)
    for a in range(16):
        dma("sync", gstage[:, 0:704], t3[a], r_gst, r_gst, f"gld2_{a}")
        P.add("vector", lambda e, a=a: e.tensor_reduce(
            gacc[:, 32 + a:33 + a], gstage[:, 0:704], AX, ALU.add,
            apply_absolute_value=True),
            reads=r_gst, writes=[b_gacc], tag=f"gr2_{a}")
    P.add("vector", lambda e: e.tensor_reduce(
        gacc[:, 30:31], gacc[:, 32:48], AX, ALU.add),
        reads=[b_gacc], writes=[b_gacc], tag="gs2a")
    mm(banks[6][0:1, 0:1], onec[:], gacc[:, 30:31], True, True,
       [b_gacc, b_const], b_bank[6], "gm2")
    for j in range(3):
        P.add("scalar", lambda e, j=j: e.copy(grow[0:1, j:j + 1], banks[4 + j][0:1, 0:1]),
              reads=[b_bank[4 + j]], writes=[b_grow], tag=f"gc{j}")
    P.add("scalar", lambda e: e.memzero(grow[0:1, 3:4]),
          reads=[], writes=[b_grow], tag="gcz")
    if KDBG:
        dma("gpsimd", dbg_gacc, gacc[:], [b_gacc], [b_xout], "dgacc")
        dma("gpsimd", dbg_grow, grow[:], [b_grow], [b_xout], "dgrow")
    dma("gpsimd", cc_gin, grow[0:1, 0:4], [b_grow], [b_ccg], "gw")
    P.add("gpsimd", lambda e: e.collective_compute(
        "AllReduce", ALU.add, replica_groups=[list(range(NCORES))],
        ins=[cc_gin], outs=[cc_gout]),
        reads=[b_ccg], writes=[b_ccg], cc=True, tag="gcc")
    dma("gpsimd", grow[0:1, 4:8], cc_gout, [b_ccg], [b_grow], "gr")
    P.add("scalar", lambda e: e.activation(grow[0:1, 0:3], grow[0:1, 4:7], AF.Copy,
                                           scale=0.5 / GCNT, bias=0.5 * EPS_Q),
          reads=[b_grow, b_const], writes=[b_grow], tag="gt")
    P.add("scalar", lambda e: e.mul(grow[0:1, 4:7], grow[0:1, 0:3], -1.0),
          reads=[b_grow, b_const], writes=[b_grow], tag="gtn")
    P.add("tensor", lambda e: e.matmul(banks[6][:, 0:8], oner[:], grow[0:1, 0:8],
                                       start=True, stop=True),
          reads=[b_grow, b_const], writes=[b_bank[6]], tag="gbc")
    P.add("scalar", lambda e: e.copy(thr[:], banks[6][:, 0:8]),
          reads=[b_bank[6]], writes=[b_thr], tag="gthr")

    def emit_quant_hs(hs):
        qstg = [(Mb[:, :], [b_MR]), (Rb[:, :], [b_MR])]
        qnat_s = [(S_all[:, 0:1024], [b_S]),
                  (S_all[:, 1024:2048], [b_S])]
        qtr_s = [(g1h[:, :], [b_g1]), (g1l[:, :], [b_g1])]
        w3v = w3_in.rearrange("(t p) h -> p t h", p=128)
        if True:
            for widx, (w_ap, sp_ap) in enumerate(
                    ((w1_in, q1t_sp), (w2_in, q2t_sp), (w3_in, q3t_sp))):
                for half in range(2):
                    sl = (hs * 6 + widx * 2 + half) % 2
                    stg, r_sl = qstg[sl]
                    qn, r_qn = qnat_s[sl]
                    qt, r_qt = qtr_s[sl]
                    if widx < 2:
                        dma("sync", stg, w_ap[hs * 128:(hs + 1) * 128,
                                              half * 1024:(half + 1) * 1024],
                            r_sl, r_sl, f"qld{widx}_{hs}_{half}")
                    else:
                        dma("sync", stg.rearrange("p (t h) -> p t h", t=8),
                            w3v[:, half * 8:(half + 1) * 8, hs * 128:(hs + 1) * 128],
                            r_sl, r_sl, f"qld{widx}_{hs}_{half}")
                    P.add("vector", lambda e, qn=qn, stg=stg, widx=widx:
                          e.tensor_single_scalar(qn, stg, thr[:, 4 + widx:5 + widx],
                                                 ALU.is_le),
                          reads=r_sl + [b_thr] + r_qn, writes=r_qn, tag=f"qa{widx}_{hs}{half}")
                    P.add("vector", lambda e, qn=qn, stg=stg, widx=widx:
                          e.scalar_tensor_tensor(qn, stg, thr[:, widx:widx + 1],
                                                 qn, ALU.is_ge, ALU.subtract),
                          reads=r_sl + [b_thr] + r_qn, writes=r_qn, tag=f"qb{widx}_{hs}{half}")
                    for t in range(8):
                        bki = 5 if t % 2 == 0 else 7
                        P.add("tensor", lambda e, qn=qn, t=t, bki=bki: e.transpose(
                            banks[bki][:, 0:128], qn[:, t * 128:(t + 1) * 128], identt[:]),
                            reads=r_qn + [b_const], writes=[b_bank[bki]],
                            tag=f"qT{widx}_{hs}_{half}_{t}")
                        P.add("scalar", lambda e, qt=qt, t=t, bki=bki: e.copy(
                            qt[:, t * 128:(t + 1) * 128], banks[bki][:, 0:128]),
                            reads=[b_bank[bki]], writes=r_qt,
                            tag=f"qE{widx}_{hs}_{half}_{t}")
                    dma("gpsimd", sp_ap[hs, half * 8:(half + 1) * 8].rearrange("t p h -> p t h"),
                        qt.rearrange("p (t h) -> p t h", t=8),
                        r_qt, [b_qsp[widx][hs]], f"qs{widx}_{hs}_{half}")

    def emit_stats_standalone(src):
        for dc in range(NDC):
            st, sp = dc == 0, dc == NDC - 1
            for hf in range(2):
                dma("sync", xt_sb[:], src[dc][:, hf * 512:(hf + 1) * 512],
                    [b_xt], [b_xt], f"sld{dc}_{hf}")
                P.add("scalar", lambda e: e.square(sq_sb[:], xt_sb[:]),
                      reads=[b_xt], writes=[b_sq], tag=f"sq{dc}_{hf}")
                mm(banks[4 + hf][0:1, :], onec[:], xt_sb[:], st, sp,
                   [b_xt, b_const], b_bank[4 + hf], f"sm{dc}_{hf}")
                mm(banks[6 + hf][0:1, :], onec[:], sq_sb[:], st, sp,
                   [b_sq, b_const], b_bank[6 + hf], f"sz{dc}_{hf}")

    rowM = Mb[0:1, :]
    rowR = Rb[0:1, :]

    def emit_mr(tag):
        for hf in range(2):
            P.add("scalar", lambda e, hf=hf: e.activation(
                Mb[0:1, hf * 512:(hf + 1) * 512], banks[4 + hf][0:1, :],
                AF.Copy, scale=1.0 / D),
                reads=[b_bank[4 + hf], b_const], writes=[b_MR], tag=f"m{tag}{hf}")
            P.add("scalar", lambda e, hf=hf: e.activation(
                Rb[0:1, hf * 512:(hf + 1) * 512], banks[6 + hf][0:1, :],
                AF.Copy, scale=1.0 / D),
                reads=[b_bank[6 + hf], b_const], writes=[b_MR], tag=f"e{tag}{hf}")
        for hf in range(2):
            P.add("tensor", lambda e, hf=hf: e.matmul(
                banks[0][:, :], oner[:], Mb[0:1, hf * 512:(hf + 1) * 512],
                start=True, stop=True),
                reads=[b_MR, b_const], writes=[b_bank[0]], tag=f"mb{tag}{hf}")
            P.add("scalar", lambda e, hf=hf: e.copy(
                Mb[:, hf * 512:(hf + 1) * 512], banks[0][:, :]),
                reads=[b_bank[0]], writes=[b_MR], tag=f"mc{tag}{hf}")
            P.add("tensor", lambda e, hf=hf: e.matmul(
                banks[1][:, :], oner[:], Rb[0:1, hf * 512:(hf + 1) * 512],
                start=True, stop=True),
                reads=[b_MR, b_const], writes=[b_bank[1]], tag=f"rb{tag}{hf}")
            P.add("scalar", lambda e, hf=hf: e.copy(
                Rb[:, hf * 512:(hf + 1) * 512], banks[1][:, :]),
                reads=[b_bank[1]], writes=[b_MR], tag=f"rc{tag}{hf}")
        for hf in range(2):
            P.add("vector", lambda e, hf=hf: e.tensor_mul(
                sq_sb[:], Mb[:, hf * 512:(hf + 1) * 512], Mb[:, hf * 512:(hf + 1) * 512]),
                reads=[b_MR], writes=[b_sq], tag=f"v1{tag}{hf}")
            P.add("vector", lambda e, hf=hf: e.tensor_sub(
                Rb[:, hf * 512:(hf + 1) * 512], Rb[:, hf * 512:(hf + 1) * 512], sq_sb[:]),
                reads=[b_MR, b_sq], writes=[b_MR], tag=f"v2{tag}{hf}")
            P.add("scalar", lambda e, hf=hf: e.activation(
                Rb[:, hf * 512:(hf + 1) * 512], Rb[:, hf * 512:(hf + 1) * 512],
                AF.Sqrt, bias=EPS_LN),
                reads=[b_MR, b_const], writes=[b_MR], tag=f"v3{tag}{hf}")
            P.add("vector", lambda e, hf=hf: e.reciprocal(
                Rb[:, hf * 512:(hf + 1) * 512], Rb[:, hf * 512:(hf + 1) * 512]),
                reads=[b_MR], writes=[b_MR], tag=f"v4{tag}{hf}")

    def emit_xsplit(src, tag, src_buf):
        for dc in range(NDC):
            for hf in range(2):
                o = dc * NT + hf * 512
                par = (dc * 2 + hf) % 2
                buf = xt_sb if par == 0 else sq_sb
                bb = b_xt if par == 0 else b_sq
                dma("sync", buf[:], src[dc][:, hf * 512:(hf + 1) * 512],
                    [bb, src_buf], [bb], f"x{tag}l{dc}_{hf}")
                P.add("vector", lambda e, hf=hf, buf=buf: e.tensor_sub(
                    buf[:], buf[:], Mb[:, hf * 512:(hf + 1) * 512]),
                    reads=[b_MR, bb], writes=[bb], tag=f"x{tag}s{dc}_{hf}")
                P.add("vector", lambda e, hf=hf, buf=buf: e.tensor_mul(
                    buf[:], buf[:], Rb[:, hf * 512:(hf + 1) * 512]),
                    reads=[b_MR, bb], writes=[bb], tag=f"x{tag}m{dc}_{hf}")
                P.add("scalar", lambda e, o=o, buf=buf: e.copy(xs_h[:, o:o + 512], buf[:]),
                      reads=[bb], writes=[b_xs], tag=f"x{tag}h{dc}_{hf}")
                P.add("vector", lambda e, o=o, buf=buf: e.tensor_sub(
                    xs_l[:, o:o + 512], buf[:], xs_h[:, o:o + 512]),
                    reads=[bb, b_xs], writes=[b_xs], tag=f"x{tag}L{dc}_{hf}")

    hid_h = hidp[:, 0:11264].bitcast(BF16)
    hid_l = hidp[:, 11264:22528].bitcast(BF16)
    r_hidh = hreg(0, 11264)
    r_hidl = hreg(11264, 11264)

    b_q3s = [Buf(f"q3s{i}") for i in range(4)]

    def emit_ffn(add_src, dst, fuse_stats, tag, src_buf, dst_buf, zip_cb=None):
        for tile in range(2):
            c0 = tile * 512
            for hs in range(NHS):
                if zip_cb is not None and tile == 0:
                    zip_cb(hs)
                slot = hs % 2
                so = slot * 4096
                q1b = strm[:, so:so + 2048]
                q2b = strm[:, so + 2048:so + 4096]
                rs = sreg(so, 4096)
                dma("sync", q1b.rearrange("p (t h) -> p t h", t=NDC),
                    q1t_sp[hs].rearrange("t p h -> p t h"),
                    [b_qsp[0][hs]] + rs, rs, f"f{tag}q1_{tile}_{hs}")
                dma("sync", q2b.rearrange("p (t h) -> p t h", t=NDC),
                    q2t_sp[hs].rearrange("t p h -> p t h"),
                    [b_qsp[1][hs]] + rs, rs, f"f{tag}q2_{tile}_{hs}")
                pA1, bA1 = banks[slot], b_bank[slot]
                pA2, bA2 = banks[2 + slot], b_bank[2 + slot]
                for dc in range(NDC):
                    xh = xs_h[:, dc * NT + c0:dc * NT + c0 + 512]
                    xl = xs_l[:, dc * NT + c0:dc * NT + c0 + 512]
                    q1 = q1b[:, dc * 128:(dc + 1) * 128]
                    q2 = q2b[:, dc * 128:(dc + 1) * 128]
                    mm(pA1[:], q1, xh, dc == 0, False, rs + [b_xs], bA1, "fA")
                    mm(pA1[:], q1, xl, False, dc == NDC - 1, rs + [b_xs], bA1, "fA")
                    mm(pA2[:], q2, xh, dc == 0, False, rs + [b_xs], bA2, "fA")
                    mm(pA2[:], q2, xl, False, dc == NDC - 1, rs + [b_xs], bA2, "fA")
                hh = hid_h[:, hs * 512:(hs + 1) * 512]
                hl = hid_l[:, hs * 512:(hs + 1) * 512]
                rh = hreg(hs * 256, 256) + hreg(5632 + hs * 256, 256)
                P.add("scalar", lambda e, pA1=pA1: e.activation(sq_sb[:], pA1[:], AF.Sigmoid),
                      reads=[bA1], writes=[b_sq], tag=f"fsg{hs}")
                P.add("vector", lambda e, pA2=pA2: e.tensor_mul(xev[:], sq_sb[:], pA2[:]),
                      reads=[b_sq, bA2], writes=[b_xev], tag=f"fmu{hs}")
                P.add("scalar", lambda e, hh=hh: e.copy(hh, xev[:]),
                      reads=[b_xev], writes=rh, tag=f"fhh{hs}")
                P.add("vector", lambda e, hh=hh, hl=hl: e.tensor_sub(hl, xev[:], hh),
                      reads=[b_xev] + rh, writes=rh, tag=f"fhl{hs}")
            for g0, g1_ in ((0, 4), (4, 8), (8, 12), (12, 16)):
                nb = g1_ - g0
                for hs in range(NHS):
                    slot = hs % 4
                    so = 8192 + slot * 512
                    q3b = strm[:, so:so + 512]
                    rs3 = [b_q3s[slot]]
                    dma("sync", q3b[:, 0:nb * 128].rearrange("p (t h) -> p t h", t=nb),
                        q3t_sp[hs, g0:g1_].rearrange("t p h -> p t h"),
                        [b_qsp[2][hs]] + rs3, rs3, f"f{tag}q3_{tile}_{g0}_{hs}")
                    hh = hid_h[:, hs * 512:(hs + 1) * 512]
                    hl = hid_l[:, hs * 512:(hs + 1) * 512]
                    rh = hreg(hs * 256, 256) + hreg(5632 + hs * 256, 256)
                    for d2 in range(g0, g1_):
                        q3 = q3b[:, (d2 - g0) * 128:(d2 - g0 + 1) * 128]
                        pB, bB = banks[d2 - g0], b_bank[d2 - g0]
                        mm(pB[:], q3, hh, hs == 0, False, rs3 + rh, bB, "fB")
                        mm(pB[:], q3, hl, False, hs == NHS - 1, rs3 + rh, bB, "fB")
                for d2 in range(g0, g1_):
                    pB, bB = banks[d2 - g0], b_bank[d2 - g0]
                    dma("sync", xev[:], add_src[d2][:, c0:c0 + 512],
                        [b_xev, src_buf], [b_xev], f"f{tag}xl{tile}_{d2}")
                    P.add("vector", lambda e, pB=pB: e.scalar_tensor_tensor(
                        xnew[:], pB[:], 0.5, xev[:], ALU.mult, ALU.add),
                        reads=[bB, b_xev], writes=[b_xnew], tag=f"fev{d2}")
                    dma("gpsimd", dst[d2][:, c0:c0 + 512], xnew[:],
                        [b_xnew], [dst_buf], f"f{tag}xs{tile}_{d2}")
                    if fuse_stats:
                        first = (d2 == g0 == 0)
                        last = (d2 == NDC - 1)
                        P.add("scalar", lambda e: e.square(sq_sb[:], xnew[:]),
                              reads=[b_xnew], writes=[b_sq], tag=f"fq{d2}")
                        mm(banks[4 + tile][0:1, :], onec[:], xnew[:], first, last,
                           [b_xnew, b_const], b_bank[4 + tile], f"fs{d2}")
                        mm(banks[6 + tile][0:1, :], onec[:], sq_sb[:], first, last,
                           [b_sq, b_const], b_bank[6 + tile], f"fz{d2}")

    emit_stats_standalone(x_in)
    emit_mr("p1")
    emit_xsplit(x_in, "p1", b_xin)
    emit_ffn(x_in, x_dram, True, "1", b_xin, b_xdram, zip_cb=emit_quant_hs)

    if KDBG:
        for dc in range(NDC):
            dma("gpsimd", dbg_xa[dc], x_dram[dc], [b_xdram], [b_xout], f"dxa{dc}")
        dma("gpsimd", dbg_thr, thr[:], [b_thr], [b_xout], "dthr")
    emit_mr("p2")
    emit_xsplit(x_dram, "p2", b_xdram)
    if KDBG:
        dma("gpsimd", dbg_xs[0], xs_h[:], [b_xs], [b_xout], "dxs0")
        dma("gpsimd", dbg_xs[1], xs_l[:], [b_xs], [b_xout], "dxs1")

    P.add("tensor", lambda e: e.matmul(banks[6][:, 0:16], oner[:], ld_sb[0:1, :],
                                       start=True, stop=True),
          reads=[b_const], writes=[b_bank[6]], tag="ldb")
    P.add("scalar", lambda e: e.copy(ldb[:], banks[6][:, 0:16]),
          reads=[b_bank[6]], writes=[b_decay], tag="ldbc")
    DTm = hidp[:, 0:2048]
    r_DT = hreg(0, 2048)
    difft = hidp[:, 11520:11648]
    trit = hidp[:, 11648:11776]
    r_dtc = hreg(11520, 256)
    dma("sync", difft[:], c_diffm, r_dtc, r_dtc, "df")
    dma("sync", trit[:], c_tri, r_dtc, r_dtc, "tr")
    for h in range(H):
        P.add("scalar", lambda e, h=h: e.activation(
            qdcd[:, h * 8:(h + 1) * 8], ioci[:], AF.Exp, scale=ldb[:, h:h + 1]),
            reads=[b_const, b_decay], writes=[b_decay], tag=f"qd{h}")
        P.add("scalar", lambda e, h=h: e.activation(
            kdc[:, h:h + 1], iorev[:], AF.Exp, scale=ldb[:, h:h + 1]),
            reads=[b_const, b_decay], writes=[b_decay], tag=f"kd{h}")
        P.add("scalar", lambda e, h=h: e.activation(
            cdc[:, h:h + 1], c128t[:], AF.Exp, scale=ldb[:, h:h + 1]),
            reads=[b_const, b_decay], writes=[b_decay], tag=f"cd{h}")
        P.add("scalar", lambda e, h=h: e.activation(
            DTm[:, h * 128:(h + 1) * 128], difft[:], AF.Exp, scale=ldb[:, h:h + 1]),
            reads=r_dtc + [b_decay], writes=r_DT, tag=f"dt{h}")
        P.add("vector", lambda e, h=h: e.tensor_mul(
            DTm[:, h * 128:(h + 1) * 128], DTm[:, h * 128:(h + 1) * 128], trit[:]),
            reads=r_dtc + r_DT, writes=r_DT, tag=f"dm{h}")

    wg1s = hidp[:, 12288:14336]
    wg1h = hidp[:, 14336:15360].bitcast(BF16)
    wg1l = hidp[:, 15360:16384].bitcast(BF16)
    r_wg1s = hreg(12288, 2048)
    r_wg1 = hreg(14336, 2048)
    g1f = hidp[:, 16384:17408]
    r_g1f = hreg(16384, 1024)
    dma("sync", wg1s[:].rearrange("p (a g) -> p a g", a=NDC),
        wg1_in.rearrange("(a p) g -> p a g", p=128),
        r_wg1s, r_wg1s, "wg1l")
    P.add("scalar", lambda e: e.copy(wg1h[:], wg1s[:]), reads=r_wg1s, writes=r_wg1, tag="wg1h")
    P.add("vector", lambda e: e.tensor_sub(wg1l[:], wg1s[:], wg1h[:]),
          reads=r_wg1s + r_wg1, writes=r_wg1, tag="wg1lo")
    for h2 in range(2):
        pG, bG = banks[6], b_bank[6]
        for dc in range(NDC):
            xh = xs_h[:, dc * NT + h2 * 512:dc * NT + h2 * 512 + 512]
            xl = xs_l[:, dc * NT + h2 * 512:dc * NT + h2 * 512 + 512]
            w_h = wg1h[:, dc * 128:(dc + 1) * 128]
            w_l = wg1l[:, dc * 128:(dc + 1) * 128]
            mm(pG[:], w_h, xh, dc == 0, False, r_wg1 + [b_xs], bG, "g1")
            mm(pG[:], w_l, xh, False, False, r_wg1 + [b_xs], bG, "g1")
            mm(pG[:], w_h, xl, False, dc == NDC - 1, r_wg1 + [b_xs], bG, "g1")
        P.add("scalar", lambda e, pG=pG: e.copy(g1f[:, 0:512], pG[:]),
              reads=[bG], writes=r_g1f, tag=f"g1e{h2}")
        P.add("scalar", lambda e, h2=h2: e.copy(g1h[:, h2 * 512:(h2 + 1) * 512], g1f[:, 0:512]),
              reads=r_g1f, writes=[b_g1], tag=f"g1h{h2}")
        P.add("vector", lambda e, h2=h2: e.tensor_sub(
            g1l[:, h2 * 512:(h2 + 1) * 512], g1f[:, 0:512], g1h[:, h2 * 512:(h2 + 1) * 512]),
            reads=r_g1f + [b_g1], writes=[b_g1], tag=f"g1l{h2}")

    qkv_sets = []
    for par, base in ((0, 5120), (1, 16896)):
        qkv_sets.append((
            hidp[:, base:base + 1024], hreg(base, 1024),
            hidp[:, base + 1024:base + 2048], hreg(base + 1024, 1024),
            hidp[:, base + 2048:base + 3072], hreg(base + 2048, 1024),
        ))
    khat = hidp[:, 8192:8320]
    sTt = hidp[:, 8320:8448]
    otmp = hidp[:, 8448:8576]
    ogac = hidp[:, 8576:9600]
    ptmp = hidp[:, 10880:11392]
    r_sc = hreg(8192, 384)
    r_og = hreg(8576, 1024)
    r_pt = hreg(10880, 512)

    for h in range(H):
        qf, r_qf, kf, r_kf, vf, r_vf = qkv_sets[h % 2]
        wbufs = []
        for wi, w_ap in enumerate((wq_in, wk_in)):
            wh_b = strm[:, wi * 4096:wi * 4096 + 2048]
            wl_b = strm[:, wi * 4096 + 2048:wi * 4096 + 4096]
            rw = sreg(wi * 4096, 4096)
            sl = (h * 3 + wi) % 2
            wstg = hidp[:, 12288 + sl * 2048:12288 + sl * 2048 + 2048]
            r_ws = hreg(12288 + sl * 2048, 2048)
            dma("sync", wstg[:].rearrange("p (a e) -> p a e", a=NDC),
                w_ap.rearrange("(a p) e -> p a e", p=128)[:, :, h * 128:(h + 1) * 128],
                r_ws, r_ws, f"wl{h}_{wi}")
            P.add("scalar", lambda e, wh_b=wh_b, wstg=wstg: e.copy(wh_b[:], wstg[:]),
                  reads=r_ws, writes=rw, tag=f"wh{h}_{wi}")
            P.add("vector", lambda e, wh_b=wh_b, wl_b=wl_b, wstg=wstg: e.tensor_sub(
                wl_b[:], wstg[:], wh_b[:]),
                reads=r_ws + rw, writes=rw, tag=f"wlo{h}_{wi}")
            wbufs.append((wh_b, wl_b, rw))

        for wi, dstf, rdst in ((0, qf, r_qf), (1, kf, r_kf)):
            wh_b, wl_b, rw = wbufs[wi]
            for h2 in range(2):
                pP, bP = banks[h2], b_bank[h2]
                for dc in range(NDC):
                    xh = xs_h[:, dc * NT + h2 * 512:dc * NT + h2 * 512 + 512]
                    xl = xs_l[:, dc * NT + h2 * 512:dc * NT + h2 * 512 + 512]
                    w_h = wh_b[:, dc * 128:(dc + 1) * 128]
                    w_l = wl_b[:, dc * 128:(dc + 1) * 128]
                    mm(pP[:], w_h, xh, dc == 0, False, rw + [b_xs], bP, "qk")
                    mm(pP[:], w_l, xh, False, False, rw + [b_xs], bP, "qk")
                    mm(pP[:], w_h, xl, False, dc == NDC - 1, rw + [b_xs], bP, "qk")
                P.add("scalar", lambda e, pP=pP, dstf=dstf, h2=h2: e.activation(
                    dstf[:, h2 * 512:h2 * 512 + 512], pP[:], AF.Silu),
                    reads=[bP], writes=rdst, tag=f"pe{h}_{wi}_{h2}")
        dma("gpsimd", q_sp[:, h].rearrange("ch p c -> p ch c"),
            qf[:].rearrange("p (ch c) -> p ch c", ch=NCH),
            r_qf, [b_qspill], f"qsp{h}")
        vh_b = strm[:, 8192:10240]
        vl_b = strm[:, 10240:12288]
        rwv = b_q3s + [b_s[5]]
        sl = (h * 3 + 2) % 2
        wstg = hidp[:, 12288 + sl * 2048:12288 + sl * 2048 + 2048]
        r_ws = hreg(12288 + sl * 2048, 2048)
        dma("sync", wstg[:].rearrange("p (a e) -> p a e", a=NDC),
            wv_in.rearrange("(a p) e -> p a e", p=128)[:, :, h * 128:(h + 1) * 128],
            r_ws, r_ws, f"wl{h}_2")
        P.add("scalar", lambda e, wstg=wstg, vh_b=vh_b: e.copy(vh_b[:], wstg[:]),
              reads=r_ws, writes=rwv, tag=f"wh{h}_2")
        P.add("vector", lambda e, wstg=wstg, vh_b=vh_b, vl_b=vl_b: e.tensor_sub(
            vl_b[:], wstg[:], vh_b[:]),
            reads=r_ws + rwv, writes=rwv, tag=f"wlo{h}_2")
        for ch in range(NCH):
            pV, bV = banks[2 + ch % 2], b_bank[2 + ch % 2]
            for dc in range(NDC):
                xh = xs_h[:, dc * NT + ch * 128:dc * NT + ch * 128 + 128]
                xl = xs_l[:, dc * NT + ch * 128:dc * NT + ch * 128 + 128]
                w_h = vh_b[:, dc * 128:(dc + 1) * 128]
                w_l = vl_b[:, dc * 128:(dc + 1) * 128]
                mm(pV[:, 0:128], xh, w_h, dc == 0, False, rwv + [b_xs], bV, "v")
                mm(pV[:, 0:128], xl, w_h, False, False, rwv + [b_xs], bV, "v")
                mm(pV[:, 0:128], xh, w_l, False, dc == NDC - 1, rwv + [b_xs], bV, "v")
            P.add("scalar", lambda e, pV=pV, ch=ch, vf=vf: e.activation(
                vf[:, ch * 128:(ch + 1) * 128], pV[:, 0:128], AF.Silu),
                reads=[bV], writes=r_vf, tag=f"ve{h}_{ch}")

        P.add("vector", lambda e, h=h: e.memset(S_all[:, h * 128:(h + 1) * 128], 0.0),
              reads=[], writes=[b_S], tag=f"sz{h}")

        Ssl = S_all[:, h * 128:(h + 1) * 128]
        for ci in range(NCH):
            qsl = qf[:, ci * 128:(ci + 1) * 128]
            ksl = kf[:, ci * 128:(ci + 1) * 128]
            vsl = vf[:, ci * 128:(ci + 1) * 128]
            mm(banks[4][:, 0:128], ksl, qsl, True, True, r_kf + r_qf, b_bank[4], "sc")
            P.add("vector", lambda e, h=h: e.tensor_mul(
                sTt[:], banks[4][:, 0:128], DTm[:, h * 128:(h + 1) * 128]),
                reads=[b_bank[4]] + r_DT, writes=r_sc, tag=f"sm{h}_{ci}")
            mm(banks[5][:, 0:128], sTt[:], vsl, True, True, r_sc + r_vf,
               b_bank[5], "oi")
            if ci > 0:
                mm(banks[6][:, 0:128], qsl, Ssl, True, True, r_qf + [b_S],
                   b_bank[6], "ox")
                P.add("scalar", lambda e, h=h: e.activation(
                    otmp[:], banks[6][:, 0:128], AF.Copy,
                    scale=qdcd[:, h * 8:h * 8 + 1]),
                    reads=[b_bank[6], b_decay], writes=r_sc, tag=f"oxs{h}_{ci}")
                P.add("vector", lambda e, ci=ci: e.tensor_add(
                    ogac[:, ci * 128:(ci + 1) * 128], banks[5][:, 0:128], otmp[:]),
                    reads=[b_bank[5]] + r_sc, writes=r_og, tag=f"oev{h}_{ci}")
            else:
                P.add("vector", lambda e, ci=ci: e.tensor_copy(
                    ogac[:, ci * 128:(ci + 1) * 128], banks[5][:, 0:128]),
                    reads=[b_bank[5]], writes=r_og, tag=f"oev{h}_{ci}")
            P.add("tensor", lambda e, ksl=ksl: e.transpose(
                banks[7][:, 128:256], ksl, identt[:]),
                reads=r_kf + [b_const], writes=[b_bank[7]], tag=f"kt{h}_{ci}")
            P.add("scalar", lambda e, h=h: e.activation(
                khat[:], banks[7][:, 128:256], AF.Copy, scale=kdc[:, h:h + 1]),
                reads=[b_bank[7], b_decay], writes=r_sc, tag=f"ks{h}_{ci}")
            mm(banks[6][:, 128:256], khat[:], vsl, True, True, r_sc + r_vf,
               b_bank[6], "ds")
            P.add("vector", lambda e, h=h, Ssl=Ssl: e.scalar_tensor_tensor(
                Ssl, Ssl, cdc[:, h:h + 1], banks[6][:, 128:256],
                ALU.mult, ALU.add),
                reads=[b_S, b_decay, b_bank[6]], writes=[b_S], tag=f"su{h}_{ci}")
        dma("gpsimd", o_sp[:, :, h * 128:(h + 1) * 128].rearrange("ch p c -> p ch c"),
            ogac[:].rearrange("p (ch c) -> p ch c", ch=NCH),
            r_og, [b_osp], f"osp{h}")
        if KDBG and h == 0:
            dma("gpsimd", dbg_f[0][:, 0:1024], qf[:], r_qf, [b_xout], "dq0")
            dma("gpsimd", dbg_f[1][:, 0:2048].bitcast(BF16), strm[:, 8192:12288],
                b_q3s + [b_s[5]], [b_xout], "dwv")
            dma("gpsimd", dbg_f[2][:, 0:1024], vf[:], r_vf, [b_xout], "dv0")
            dma("gpsimd", dbg_f[3][:, 0:128], S_all[:, 0:128], [b_S], [b_xout], "dS0")

    dma("gpsimd", cc_bin, S_all[:], [b_S], [b_ccb], "bw")
    P.add("gpsimd", lambda e: e.collective_compute(
        "AllGather", ALU.bypass, replica_groups=[list(range(NCORES))],
        ins=[cc_bin], outs=[cc_bout]),
        reads=[b_ccb], writes=[b_ccb], cc=True, tag="bcc")
    if KDBG:
        for j in range(NCORES):
            dma("gpsimd", dbg_B[j], cc_bout[j], [b_ccb], [b_xout], f"dB{j}")
        for ci in range(NCH):
            dma("gpsimd", dbg_osp[ci], o_sp[ci], [b_osp], [b_xout], f"dO{ci}")
            dma("gpsimd", dbg_qsp[ci], q_sp[ci], [b_qspill], [b_xout], f"dQ{ci}")
    ldrep = hidp[0:1, 0:2048]
    LD2048 = hidp[:, 2048:4096]
    Bj = hidp[:, 4096:6144]
    FBj = hidp[:, 6144:8192]
    r_ldr = hreg(0, 2048)
    r_LD = hreg(2048, 2048)
    r_Bj = hreg(4096, 2048)
    r_FB = hreg(6144, 2048)
    P.add("vector", lambda e: e.tensor_copy(
        ldrep.rearrange("q (h r) -> q h r", h=16),
        ld_sb[0:1, :].rearrange("q h -> q h 1" if False else "q (h r) -> q h r", r=1
                                ).broadcast_to([1, 16, 128])),
        reads=[b_const], writes=r_ldr, tag="ldrep")
    for h4 in range(4):
        P.add("tensor", lambda e, h4=h4: e.matmul(
            banks[6][:, :], oner[:], ldrep[0:1, h4 * 512:(h4 + 1) * 512],
            start=True, stop=True),
            reads=r_ldr + [b_const], writes=[b_bank[6]], tag=f"ldm{h4}")
        P.add("scalar", lambda e, h4=h4: e.copy(LD2048[:, h4 * 512:(h4 + 1) * 512],
                                                banks[6][:, :]),
              reads=[b_bank[6]], writes=r_LD, tag=f"ldc{h4}")
    P.add("vector", lambda e: e.memset(S_all[:], 0.0), reads=[], writes=[b_S], tag="sinz")
    for j in range(NCORES):
        P.add("vector", lambda e, j=j: e.tensor_scalar(
            tjc[:, 0:1], cidc[:], 1024.0, (j + 1) * 1024.0, ALU.mult, ALU.subtract),
            reads=[b_cid], writes=[b_cid], tag=f"tj{j}")
        P.add("vector", lambda e: e.tensor_single_scalar(
            tjc[:, 1:2], tjc[:, 0:1], 0.0, ALU.is_ge),
            reads=[b_cid], writes=[b_cid], tag=f"tm{j}")
        P.add("vector", lambda e: e.tensor_scalar(
            tjc[:, 2:3], tjc[:, 1:2], -1e6, 1e6, ALU.mult, ALU.add),
            reads=[b_cid], writes=[b_cid], tag=f"ta{j}")
        P.add("vector", lambda e: e.tensor_add(tjc[:, 0:1], tjc[:, 0:1], tjc[:, 2:3]),
              reads=[b_cid], writes=[b_cid], tag=f"tb{j}")
        P.add("scalar", lambda e: e.activation(FBj[:], LD2048[:], AF.Exp,
                                               scale=tjc[:, 0:1]),
              reads=r_LD + [b_cid], writes=r_FB, tag=f"fb{j}")
        dma("sync", Bj[:], cc_bout[j], [b_ccb] + r_Bj, r_Bj, f"bl{j}")
        P.add("vector", lambda e: e.tensor_mul(Bj[:], Bj[:], FBj[:]),
              reads=r_FB + r_Bj, writes=r_Bj, tag=f"bm{j}")
        P.add("vector", lambda e: e.tensor_add(S_all[:], S_all[:], Bj[:]),
              reads=r_Bj + [b_S], writes=[b_S], tag=f"bs{j}")

    if KDBG:
        dma("gpsimd", dbg_Sin, S_all[:], [b_S], [b_xout], "dSin")
    ogf = hidp[:, 0:8192]
    och = hidp[:, 16384:18432]
    sg = hidp[:, 18432:20480]
    qc = hidp[:, 20480:22528]
    woF = hidp[:, 12288:16384]
    ogbh = hidp[:, 0:4096].bitcast(BF16)
    ogbl = hidp[:, 5120:9216].bitcast(BF16)
    wobh = strm[:, 4096:8192]
    wobl = strm[:, 8192:12288]
    r_ogf = hreg(0, 9216)
    r_wob = sreg(4096, 8192)
    r_och = hreg(16384, 2048)
    r_sg = hreg(18432, 2048)
    r_qc = hreg(20480, 2048)
    r_wo = hreg(12288, 4096)
    wg2h = strm[:, 0:2048]
    wg2l = strm[:, 2048:4096]
    r_wg2 = sreg(0, 4096)
    for seg in range(4):
        dma("sync", ptmp[:, 0:512], wg2_in[:, seg * 512:(seg + 1) * 512], r_pt, r_pt, f"wg2_{seg}")
        P.add("scalar", lambda e, seg=seg: e.copy(
            wg2h[:, seg * 512:(seg + 1) * 512], ptmp[:, 0:512]),
            reads=r_pt, writes=r_wg2, tag=f"wg2h{seg}")
        P.add("vector", lambda e, seg=seg: e.tensor_sub(
            wg2l[:, seg * 512:(seg + 1) * 512], ptmp[:, 0:512],
            wg2h[:, seg * 512:(seg + 1) * 512]),
            reads=r_pt + r_wg2, writes=r_wg2, tag=f"wg2l{seg}")

    for hf in range(2):
        for ci in range(hf * 4, hf * 4 + 4):
            cl = ci - hf * 4
            dma("sync", och[:], o_sp[ci], [b_osp] + r_och, r_och, f"ol{ci}")
            dma("sync", qc[:].rearrange("p (h c) -> p h c", h=H),
                q_sp[ci].rearrange("h p c -> p h c"),
                [b_qspill] + r_qc, r_qc, f"ql{ci}")
            for d4 in range(4):
                pG, bG = banks[2 + d4 % 2], b_bank[2 + d4 % 2]
                g1hs = g1h[:, ci * 128:(ci + 1) * 128]
                g1ls = g1l[:, ci * 128:(ci + 1) * 128]
                mm(pG[:], g1hs, wg2h[:, d4 * 512:(d4 + 1) * 512], True, False,
                   [b_g1] + r_wg2, bG, "gt")
                mm(pG[:], g1ls, wg2h[:, d4 * 512:(d4 + 1) * 512], False, False,
                   [b_g1] + r_wg2, bG, "gt")
                mm(pG[:], g1hs, wg2l[:, d4 * 512:(d4 + 1) * 512], False, True,
                   [b_g1] + r_wg2, bG, "gt")
                P.add("scalar", lambda e, pG=pG, d4=d4: e.activation(
                    sg[:, d4 * 512:(d4 + 1) * 512], pG[:], AF.Sigmoid),
                    reads=[bG], writes=r_sg, tag=f"sig{ci}_{d4}")
            for h in range(H):
                pC, bC = banks[h % 2], b_bank[h % 2]
                mm(pC[:, 0:128], qc[:, h * 128:(h + 1) * 128],
                   S_all[:, h * 128:(h + 1) * 128],
                   True, True, r_qc + [b_S], bC, "corr")
                P.add("vector", lambda e, pC=pC, h=h, ci=ci: e.scalar_tensor_tensor(
                    och[:, h * 128:(h + 1) * 128], pC[:, 0:128],
                    qdcd[:, h * 8 + ci:h * 8 + ci + 1], och[:, h * 128:(h + 1) * 128],
                    ALU.mult, ALU.add),
                    reads=[bC, b_decay] + r_och, writes=r_och, tag=f"cadd{ci}_{h}")
            if KDBG:
                dma("gpsimd", dbg_och[ci], och[:], r_och, [b_xout], f"doch{ci}")
            P.add("vector", lambda e: e.tensor_mul(och[:], och[:], sg[:]),
                  reads=r_sg + r_och, writes=r_och, tag=f"om{ci}")
            for q4 in range(4):
                P.add("scalar", lambda e, q4=q4: e.square(
                    sg[:, q4 * 512:(q4 + 1) * 512], och[:, q4 * 512:(q4 + 1) * 512]),
                    reads=r_och, writes=r_sg, tag=f"rs{q4}{ci}")
            P.add("vector", lambda e: e.tensor_reduce(
                gacc[:, 0:16], sg[:].rearrange("p (h c) -> p h c", h=16), AX, ALU.add),
                reads=r_sg, writes=[b_gacc], tag=f"rr{ci}")
            P.add("scalar", lambda e: e.activation(gacc[:, 16:32], gacc[:, 0:16],
                                                   AF.Sqrt, scale=1.0 / HD, bias=EPS_RMS),
                  reads=[b_gacc, b_const], writes=[b_gacc], tag=f"rq{ci}")
            P.add("vector", lambda e: e.reciprocal(gacc[:, 16:32], gacc[:, 16:32]),
                  reads=[b_gacc], writes=[b_gacc], tag=f"rc{ci}")
            for h in range(H):
                P.add("scalar", lambda e, h=h: e.activation(
                    sg[:, h * 128:(h + 1) * 128], och[:, h * 128:(h + 1) * 128],
                    AF.Copy, scale=gacc[:, 16 + h:17 + h]),
                    reads=r_och + [b_gacc], writes=r_sg, tag=f"on{ci}_{h}")
                bkt = 2 + (h % 2)
                P.add("tensor", lambda e, h=h, bkt=bkt: e.transpose(
                    banks[bkt][:, 0:128], sg[:, h * 128:(h + 1) * 128], identt[:]),
                    reads=r_sg + [b_const], writes=[b_bank[bkt]], tag=f"ot{ci}_{h}")
                P.add("scalar", lambda e, h=h, cl=cl, bkt=bkt: e.copy(
                    ogbh[:, h * 512 + cl * 128:h * 512 + cl * 128 + 128],
                    banks[bkt][:, 0:128]),
                    reads=[b_bank[bkt]], writes=r_ogf, tag=f"oh{ci}_{h}")
                P.add("vector", lambda e, h=h, cl=cl, bkt=bkt: e.tensor_sub(
                    ogbl[:, h * 512 + cl * 128:h * 512 + cl * 128 + 128],
                    banks[bkt][:, 0:128],
                    ogbh[:, h * 512 + cl * 128:h * 512 + cl * 128 + 128]),
                    reads=[b_bank[bkt]] + r_ogf, writes=r_ogf, tag=f"ol{ci}_{h}")

        for eg in range(8):
            for dc in range(NDC):
                wsl = hidp[:, 9216 + (dc % 3) * 1024:9216 + (dc % 3) * 1024 + 256]
                r_wsl = hreg(9216 + (dc % 3) * 1024, 256)
                dma("sync", wsl[:],
                    wo_in[dc * 128:(dc + 1) * 128, eg * 256:(eg + 1) * 256],
                    r_wsl, r_wsl, f"wol{hf}_{eg}_{dc}")
                P.add("scalar", lambda e, dc=dc, wsl=wsl: e.activation(
                    woF[:, dc * 256:(dc + 1) * 256], wsl[:], AF.Copy,
                    scale=nwc[:, dc:dc + 1]),
                    reads=r_wsl + [b_nw], writes=hreg(12288 + dc * 256, 256),
                    tag=f"woh{hf}_{eg}_{dc}")
                P.add("scalar", lambda e, dc=dc: e.copy(
                    wobh[:, dc * 256:(dc + 1) * 256], woF[:, dc * 256:(dc + 1) * 256]),
                    reads=hreg(12288 + dc * 256, 256),
                    writes=sreg(4096 + dc * 256, 256), tag=f"wbh{hf}_{eg}_{dc}")
                P.add("vector", lambda e, dc=dc: e.tensor_sub(
                    wobl[:, dc * 256:(dc + 1) * 256], woF[:, dc * 256:(dc + 1) * 256],
                    wobh[:, dc * 256:(dc + 1) * 256]),
                    reads=hreg(12288 + dc * 256, 256) + sreg(4096 + dc * 256, 256),
                    writes=sreg(8192 + dc * 256, 256), tag=f"wbl{hf}_{eg}_{dc}")
            if KDBG and hf == 0 and eg == 0:
                dma("gpsimd", dbg_f[0][:, 0:2048], woF[:, 0:2048], r_wo, [b_xout], "dwoF0")
                dma("gpsimd", dbg_f[1][:, 0:2048], woF[:, 2048:4096], r_wo, [b_xout], "dwoF1")
            for es in range(2):
                e_abs = eg * 2 + es
                pW, bW = banks[es], b_bank[es]
                for dc in range(NDC):
                    ogh_s = ogbh[:, dc * 512:dc * 512 + 512]
                    ogl_s = ogbl[:, dc * 512:dc * 512 + 512]
                    wh_s = wobh[:, dc * 256 + es * 128:dc * 256 + es * 128 + 128]
                    wl_s = wobl[:, dc * 256 + es * 128:dc * 256 + es * 128 + 128]
                    r_wdc = sreg(4096 + dc * 256, 256) + sreg(8192 + dc * 256, 256)
                    mm(pW[:], wh_s, ogh_s, dc == 0, False, r_wdc + r_ogf, bW, "wo")
                    mm(pW[:], wl_s, ogh_s, False, False, r_wdc + r_ogf, bW, "wo")
                    mm(pW[:], wh_s, ogl_s, False, dc == NDC - 1, r_wdc + r_ogf, bW, "wo")
                dma("sync", xev[:], x_dram[e_abs][:, hf * 512:hf * 512 + 512],
                    [b_xev, b_xdram], [b_xev], f"axl{e_abs}_{hf}")
                P.add("vector", lambda e, pW=pW: e.tensor_add(xnew[:], pW[:], xev[:]),
                      reads=[bW, b_xev], writes=[b_xnew], tag=f"aev{e_abs}_{hf}")
                dma("gpsimd", x_dram[e_abs][:, hf * 512:hf * 512 + 512], xnew[:],
                    [b_xnew], [b_xdram], f"axs{e_abs}_{hf}")
                first = (e_abs == 0)
                last = (e_abs == NDC - 1)
                P.add("scalar", lambda e: e.square(sq_sb[:], xnew[:]),
                      reads=[b_xnew], writes=[b_sq], tag=f"asq{e_abs}_{hf}")
                mm(banks[4 + hf][0:1, :], onec[:], xnew[:], first, last,
                   [b_xnew, b_const], b_bank[4 + hf], f"as{e_abs}_{hf}")
                mm(banks[6 + hf][0:1, :], onec[:], sq_sb[:], first, last,
                   [b_sq, b_const], b_bank[6 + hf], f"az{e_abs}_{hf}")

    if KDBG:
        for dc in range(NDC):
            dma("gpsimd", dbg_xb[dc], x_dram[dc], [b_xdram], [b_xout], f"dxb{dc}")
    emit_mr("p3")
    emit_xsplit(x_dram, "p3", b_xdram)
    emit_ffn(x_dram, x_out, False, "2", b_xdram, b_xout)

    P.add("sync", lambda e: e.nop(), reads=[b_xout], writes=[], tag="final")

    _LAST_P[0] = P
    P.build(nc)
    with nc.Block() as block:
        P.emit(block)
    return nc


_NC_CACHE = [None]
_LAST_RES = [None]
_LAST_P = [None]


def kernel(L0_attn_x, L0_attn_log_decay, L0_attn_Wq, L0_attn_Wk, L0_attn_Wv,
           L0_attn_Wo, L0_attn_Wg1, L0_attn_Wg2, L0_attn_norm_w,
           L0_ffn_w1, L0_ffn_w2, L0_ffn_w3):
    x = np.asarray(L0_attn_x, dtype=np.float32)
    w1 = np.ascontiguousarray(np.asarray(L0_ffn_w1, np.float32))
    w2 = np.ascontiguousarray(np.asarray(L0_ffn_w2, np.float32))
    w3 = np.ascontiguousarray(np.asarray(L0_ffn_w3, np.float32))
    shared = {
        "ld_in": np.ascontiguousarray(np.asarray(L0_attn_log_decay, np.float32)[None, :]),
        "wq_in": np.ascontiguousarray(np.asarray(L0_attn_Wq, np.float32)),
        "wk_in": np.ascontiguousarray(np.asarray(L0_attn_Wk, np.float32)),
        "wv_in": np.ascontiguousarray(np.asarray(L0_attn_Wv, np.float32)),
        "wo_in": np.ascontiguousarray(np.asarray(L0_attn_Wo, np.float32)),
        "wg1_in": np.ascontiguousarray(np.asarray(L0_attn_Wg1, np.float32)),
        "wg2_in": np.ascontiguousarray(np.asarray(L0_attn_Wg2, np.float32)),
        "nw_in": np.ascontiguousarray(
            np.asarray(L0_attn_norm_w, np.float32).reshape(NDC, 128).T),
        "w1_in": w1, "w2_in": w2, "w3_in": w3,
    }
    if _NC_CACHE[0] is None:
        _NC_CACHE[0] = build_nc()
    nc = _NC_CACHE[0]
    in_maps = []
    for c in range(NCORES):
        xs = np.ascontiguousarray(
            x[0, c * NT:(c + 1) * NT, :].T.reshape(NDC, 128, NT))
        m = dict(shared)
        m["x_in"] = xs
        m["w1g_in"] = np.ascontiguousarray(w1[c * 704:(c + 1) * 704])
        m["w2g_in"] = np.ascontiguousarray(w2[c * 704:(c + 1) * 704])
        m["w3g_in"] = np.ascontiguousarray(w3[:, c * 704:(c + 1) * 704])
        m["cid_in"] = np.array([[float(c)]], dtype=np.float32)
        in_maps.append(m)
    res = run_bass_kernel_spmd(nc, in_maps, core_ids=list(range(NCORES)))
    _LAST_RES[0] = res
    out = np.empty((1, NCORES * NT, D), dtype=np.float32)
    for c in range(NCORES):
        o = res.results[c]["x_out"].reshape(D, NT)
        out[0, c * NT:(c + 1) * NT, :] = o.T
    return out
